# version 02
# baseline (speedup 1.0000x reference)
import numpy as np
import ml_dtypes
import concourse.bass as bass
import concourse.mybir as mybir
from concourse.bass_utils import run_bass_kernel_spmd

F32 = mybir.dt.float32
BF16 = mybir.dt.bfloat16
FP8 = mybir.dt.float8e4
ALU = mybir.AluOpType
ACT = mybir.ActivationFunctionType
AXX = mybir.AxisListType.X

NCORE = 8
T, D = 8192, 2048
TC = T // NCORE
DK = D // 128
HM, HMC, NE = 1408, 11, 8
FFH = 8192
RS128 = float(1.0 / np.sqrt(128.0))
EPS = 1e-5

DEBUG = False
PHASES = 99
SUBPH = 99
SCSTEP = 99


def ts(i, n=128):
    return slice(i * n, (i + 1) * n)


class Rec:
    def __init__(self, nc):
        self.nc = nc
        self.items = []
        self.sem = {}
        self.semctx = []

    def S(self, name):
        cm = self.nc.semaphore(name)
        h = cm.__enter__()
        self.semctx.append(cm)
        self.sem[name] = [h, 0]
        return name

    def fresh(self, prefix="f", kind="hdma"):
        if not hasattr(self, "pools"):
            self.pools = {"gdma": [], "hdma": [], "evt": []}
            self.kinds = {}
        if kind == "evt" and self.pools[kind]:
            return self.pools[kind].pop()
        name = f"sm{len(self.sem)}"
        self.S(name)
        self.kinds[name] = kind
        return name

    def release(self, name):
        self.pools[self.kinds[name]].append(name)

    def inc(self, name, k=1):
        e = self.sem[name]
        e[1] += k
        return e[0], e[1]

    def val(self, name):
        return self.sem[name][1]

    def emit(self, eng, fn, weak=False):
        self.items.append((eng, fn, weak))

    def wait(self, eng, name, v=None):
        hh = self.sem[name][0]
        vv = self.sem[name][1] if v is None else v
        if vv > 0:
            self.emit(eng, lambda e, hh=hh, vv=vv: e.wait_ge(hh, vv), weak=True)

    def mark(self, engl, sem):
        hh, vv = self.inc(sem, 1)
        for i in range(len(self.items) - 1, -1, -1):
            eng, fn, weak = self.items[i]
            if eng == engl and not weak:
                self.items[i] = (eng, lambda e, fn=fn, hh=hh: fn(e).then_inc(hh, 1), False)
                return vv
        raise RuntimeError("no instruction to mark")

    def replay(self, blk):
        byeng = {k: [] for k in "tvsgy"}
        for eng, fn, _w in self.items:
            byeng[eng].append(fn)
        def run(fns):
            def body(e):
                for f in fns:
                    f(e)
            return body
        blk.tensor(run(byeng["t"]))
        blk.vector(run(byeng["v"]))
        blk.scalar(run(byeng["s"]))
        blk.gpsimd(run(byeng["g"]))
        blk.sync(run(byeng["y"]))

    def close(self):
        for cm in reversed(self.semctx):
            cm.__exit__(None, None, None)


class Ctx:
    pass


def build():
    nc = bass.Bass("TRN2", target_bir_lowering=False)
    c = Ctx()
    c.nc = nc

    def EIN(name, shape, dtype=F32):
        return nc.dram_tensor(name, shape, dtype, kind="ExternalInput")

    c.x_in = EIN("x", [D, TC])
    c.l0c = EIN("l0c", [D, 1]); c.l1c = EIN("l1c", [D, 1])
    c.l0wmod = EIN("l0wmod", [D, 1536]); c.l1wmod = EIN("l1wmod", [D, 1536])
    c.l0qkv = EIN("l0qkv", [768, D])
    c.l1qkv = EIN("l1qkv", [772, D])
    c.l0wo_s = EIN("l0wo_s", [D, 256])
    c.l1wo_s = EIN("l1wo_s", [D, 256])
    c.wg_in = EIN("wg", [8, D])
    c.moe1s = EIN("moe1s", [NE, HM, 256])
    c.moe2s = EIN("moe2s", [NE, HM, 256])
    c.moe3s = EIN("moe3s", [NE, D, 176])
    c.bilr_in = EIN("bilr", [1, 4])
    c.w1h_in = EIN("w1h", [256, 64])
    c.wfc_s = EIN("wfc_s", [FFH, 256])
    c.wproj_s = EIN("wproj_s", [D, 1024])
    c.out = nc.dram_tensor("out", [D, TC], F32, kind="ExternalOutput")

    I = lambda a, name: nc.inline_tensor(np.ascontiguousarray(a), name=name)
    c.ident_bf = I(np.eye(128, dtype=ml_dtypes.bfloat16), "identbf")
    c.ident_f = I(np.eye(128, dtype=np.float32), "identf")
    c.ones_col = I(np.ones((128, 1), dtype=ml_dtypes.bfloat16), "onescol")
    c.ones_row = I(np.ones((1, 128), dtype=ml_dtypes.bfloat16), "onesrow")
    c.ones_rowf = I(np.ones((1, 128), dtype=np.float32), "onesrowf")
    mk = np.zeros((128, 4, 512), dtype=ml_dtypes.bfloat16)
    for rr in range(4):
        mk[:, rr, :] = (np.arange(128)[:, None] + 128 * rr <= np.arange(512)[None, :])
    c.maskc_t = I(mk.reshape(128, 2048), "maskc")
    sub = np.arange(128) // 16
    ar = np.arange(128)
    c.maskBL_t = I((sub[:, None] > sub[None, :]).astype(np.float32), "maskBLts")
    c.maskBUn_t = I(-(sub[:, None] < sub[None, :]).astype(np.float32), "maskBUn")
    c.maskBD_t = I(((sub[:, None] == sub[None, :]) & (ar[:, None] <= ar[None, :])).astype(np.float32), "maskBD")
    c.tokidx_t = I((1.0 / ((ar % 16) + 1.0) / 64.0).astype(np.float32).reshape(128, 1), "tokidx")

    def IDR(name, shape, dtype):
        return nc.dram_tensor(name, shape, dtype, kind="Internal")

    c.agi_mod = IDR("agi_mod", [2, 1536], F32); c.ago_mod = IDR("ago_mod", [16, 1536], F32)
    c.agi_h = IDR("agi_h", [D, TC], BF16); c.ago_h = IDR("ago_h", [NCORE * D, TC], BF16)
    c.zloc0 = IDR("zloc0", [T, 256], BF16)
    c.zloc1 = IDR("zloc1", [T, 256], BF16)
    c.part = IDR("part", [T, D], BF16)
    c.wo0T = IDR("wo0T", [256, D], BF16)
    c.wo1T = IDR("wo1T", [256, D], BF16)
    c.rsout = IDR("rsout", [TC, D], BF16)
    c.agi_m1 = IDR("agi_m1", [NE, 256, HM], BF16)
    c.ago_m1 = [IDR(f"ago_m1_{e}", [D, HM], BF16) for e in range(NE)]
    c.agi_m2 = IDR("agi_m2", [NE, 256, HM], BF16)
    c.ago_m2 = [IDR(f"ago_m2_{e}", [D, HM], BF16) for e in range(NE)]
    c.agi_m3 = IDR("agi_m3", [NE, 176, D], BF16)
    c.ago_m3 = [IDR(f"ago_m3_{e}", [HM, D], BF16) for e in range(NE)]
    c.agi_fc8 = IDR("agi_fc8", [256, FFH], FP8); c.ago_fc8 = IDR("ago_fc8", [D, FFH], FP8)
    c.agi_pj8 = IDR("agi_pj8", [1024, D], FP8); c.ago_pj8 = IDR("ago_pj8", [FFH, D], FP8)
    c.agi_wmx = IDR("agi_wmx", [2, 1], F32); c.ago_wmx = IDR("ago_wmx", [2, 1], F32)
    c.agi_xmx = IDR("agi_xmx", [1, 1], F32); c.ago_xmx = IDR("ago_xmx", [1, 1], F32)
    c.agi_hmx = IDR("agi_hmx", [1, 1], F32); c.ago_hmx = IDR("ago_hmx", [1, 1], F32)
    c.hbounce = IDR("hbounce", [FFH, TC], BF16)
    c.h8b = IDR("h8b", [FFH, TC], FP8)
    c.x4f = IDR("x4f", [D, TC], F32)

    c.dbg = {}
    if DEBUG:
        for name, shape in [("dbg_h", [D, TC]), ("dbg_x0a", [D, TC]), ("dbg_x0b", [D, TC]),
                            ("dbg_x1a", [D, TC]), ("dbg_w", [128, 64]), ("dbg_z", [T, 256])]:
            c.dbg[name] = nc.dram_tensor(name, shape, F32, kind="ExternalOutput")

    c.RG = [list(range(NCORE))]

    A = nc.alloc_sbuf_tensor
    c.xres = A("xres", [128, DK * TC], F32)
    c.hbuf = A("hbuf", [128, DK * TC], BF16)
    c.wqkvT = A("wqkvT", [128, DK, 772], BF16)
    c.modb = A("modb", [128, 2 * 6 * DK], F32)
    c.cbuf = A("cbuf", [128, 2 * DK], F32)
    c.silc = A("silc", [128, 2 * DK], F32)

    c.idb = A("idb", [128, 128], BF16)
    c.idf = A("idf", [128, 128], F32)
    c.onec = A("onec", [128, 1], BF16)
    c.oner = A("oner", [1, 128], BF16)
    c.onerf = A("onerf", [1, 128], F32)
    c.mBL = A("mBL", [128, 128], F32)
    c.mBUn = A("mBUn", [128, 128], F32)
    c.mBD = A("mBD", [128, 128], F32)
    c.tkx = A("tkx", [128, 1], F32)
    c.wgT = A("wgT", [128, DK * 8], BF16)
    c.wmoe = A("wmoe", [128, 64], F32)
    c.stg = A("stg", [128, 2, 1024], BF16)
    c.stgf = A("stgf", [128, 2, 1536], F32)
    c.smx = A("smx", [128, 64], F32)
    c.smx2 = A("smx2", [128, 64], F32)
    c.rowscr = A("rowscr", [1, 1536], F32)
    c.lnt = A("lnt", [128, 2, 512], BF16)
    c.lnq = A("lnq", [128, 2, 512], BF16)
    c.ubuf = A("ubuf", [128, 2, 512], F32)
    c.arena = A("arena", [128, 26624], BF16)
    c.psA = nc.alloc_psum_tensor("psA", [128, 2048], F32)
    c.psB = nc.alloc_psum_tensor("psB", [128, 512], F32)
    c.psC = nc.alloc_psum_tensor("psC", [128, 512], F32)
    c.psD = nc.alloc_psum_tensor("psD", [128, 512], F32)
    c.psE = nc.alloc_psum_tensor("psE", [128, 512], F32)

    r = Rec(nc)
    c.r = r
    c.ubuf_guard = None
    c.stg_gate = None
    for s in ["sg", "sh", "sx", "st", "sv", "sc", "scc", "sge",
              "d0", "d1", "d2", "d3", "d4", "d5", "d6", "d7"]:
        r.S(s)

    emit_all(c)

    blk_cm = nc.Block()
    blk = blk_cm.__enter__()
    r.replay(blk)
    blk_cm.__exit__(None, None, None)
    r.close()
    return nc


def emit_all(c):
    r = c.r
    nc = c.nc

    def dma(engl, sem, outap, inap, **kw):
        hh, vv = r.inc(sem, 16)
        r.emit(engl, lambda e, o=outap, i=inap, hh=hh, kw=kw: e.dma_start(out=o, in_=i, **kw).then_inc(hh, 16))
        return vv

    def ev(engl, sem, fn):
        hh, vv = r.inc(sem, 1)
        r.emit(engl, lambda e, fn=fn, hh=hh: fn(e).then_inc(hh, 1))
        return vv

    def em(engl, fn):
        r.emit(engl, fn)

    def cc(fn):
        s = r.fresh("cc", kind="evt")
        hh, vv = r.inc(s, 1)
        r.emit("g", lambda e, fn=fn, hh=hh: fn(e).then_inc(hh, 1))
        return (s, vv)

    def waitcc(engl, tok):
        r.wait(engl, tok[0], tok[1])

    c.dma, c.ev, c.em, c.cc, c.waitcc = dma, ev, em, cc, waitcc

    for src, dst in [(c.ident_bf, c.idb), (c.ident_f, c.idf), (c.ones_col, c.onec), (c.ones_row, c.oner), (c.ones_rowf, c.onerf),
                     (c.maskBL_t, c.mBL), (c.maskBUn_t, c.mBUn),
                     (c.maskBD_t, c.mBD), (c.tokidx_t, c.tkx)]:
        dma("g", "sg", dst[:, :], src[:, :])
    dma("g", "sg", c.cbuf[:, 0:DK], c.l0c.ap().rearrange("(k p) o -> p (k o)", p=128), allow_slow_non_contiguous=True)
    dma("g", "sg", c.cbuf[:, DK:2 * DK], c.l1c.ap().rearrange("(k p) o -> p (k o)", p=128), allow_slow_non_contiguous=True)
    for k in range(DK):
        dma("g", "sg", c.xres[:, k * TC:(k + 1) * TC], c.x_in[ts(k), :])
    n_pro = r.val("sg")

    r.wait("s", "sg", n_pro)
    silu_done = ev("s", "sc", lambda e: e.activation(c.silc[:, :], c.cbuf[:, :], ACT.Silu))

    modps = c.psA
    wmsem = [r.fresh("wm"), r.fresh("wm")]
    mm_done = {}
    evac_done = 0
    for L in range(2):
        wm = c.l0wmod if L == 0 else c.l1wmod
        for k in range(DK):
            gidx = L * DK + k
            buf = gidx % 2
            if gidx >= 2:
                r.wait("y", "st", mm_done[gidx - 2])
            dma("y", wmsem[buf], c.stgf[:, buf, :], wm[ts(k), :])
            r.wait("t", wmsem[buf], r.val(wmsem[buf]))
            if gidx == 0:
                r.wait("t", "sc", silu_done)
            if k == 0 and L == 1:
                r.wait("t", "sv", evac_done)
            for nn in range(3):
                fn = lambda e, b=buf, nn=nn, k=k, g2=gidx: e.matmul(
                    modps[0:1, nn * 512:(nn + 1) * 512], c.silc[:, g2:g2 + 1],
                    c.stgf[:, b, nn * 512:(nn + 1) * 512], start=(k == 0), stop=(k == DK - 1))
                if nn == 2:
                    mm_done[gidx] = ev("t", "st", fn)
                else:
                    em("t", fn)
        r.wait("v", "st", r.val("st"))
        if L == 1:
            r.wait("v", "sg", mod_store0)
        evac_done = ev("v", "sv", lambda e: e.tensor_copy(c.rowscr[:, 0:1536], modps[0:1, 0:1536]))
        r.wait("g", "sv", evac_done)
        if L == 0:
            r.wait("g", "sg", r.val("sg"))
        mod_store0 = dma("g", "sg", c.agi_mod.ap()[L:L + 1, :], c.rowscr[:, 0:1536])
    r.wait("g", "sg", r.val("sg"))
    cc_mod = cc(lambda e: e.collective_compute(
        "AllGather", ALU.bypass, replica_groups=c.RG,
        ins=[c.agi_mod.ap().opt()], outs=[c.ago_mod.ap().opt()]))

    waitcc("g", cc_mod)
    r.release(cc_mod[0])
    for L in range(2):
        for rr in range(8):
            srcap = c.ago_mod.ap()[2 * rr + L].rearrange("(cc p) -> p cc", p=128)
            dsta = c.modb[:, L * 96 + rr * 12: L * 96 + (rr + 1) * 12]
            dma("g", "sg", dsta, srcap, allow_slow_non_contiguous=True)
    n_modload = r.val("sg")

    def gsl(L, gi):
        return c.modb[:, L * 96 + gi * 16: L * 96 + gi * 16 + DK]

    r.wait("s", "sg", n_modload)
    r.wait("v", "sg", n_modload)
    abs_done = 0
    for i, (L, gi) in enumerate([(0, 2), (0, 5), (1, 2), (1, 5)]):
        abs_done = ev("s", "sc", lambda e, L=L, gi=gi, i=i: e.activation(
            c.smx2[:, i * DK:(i + 1) * DK], gsl(L, gi), ACT.Abs))
    for i, (L, gi) in enumerate([(0, 2), (0, 5), (1, 2), (1, 5)]):
        em("v", lambda e, L=L, gi=gi, i=i: e.tensor_single_scalar(
            c.smx[:, i * DK:(i + 1) * DK], gsl(L, gi), 0.0, ALU.is_ge))
    em("v", lambda e: e.drain())
    em("v", lambda e: e.tensor_scalar(c.smx[:, 0:64], c.smx[:, 0:64], 2.0, -1.0, ALU.mult, ALU.add))
    r.wait("v", "sc", abs_done)
    em("v", lambda e: e.tensor_scalar_max(c.smx2[:, 0:64], c.smx2[:, 0:64], 0.25))
    em("v", lambda e: e.drain())
    for i, (L, gi) in enumerate([(0, 2), (0, 5), (1, 2), (1, 5)]):
        em("v", lambda e, L=L, gi=gi, i=i: e.tensor_mul(
            gsl(L, gi), c.smx[:, i * DK:(i + 1) * DK], c.smx2[:, i * DK:(i + 1) * DK]))
    em("v", lambda e: e.drain())
    for L in range(2):
        for gi in (1, 4):
            em("v", lambda e, L=L, gi=gi: e.tensor_scalar_add(gsl(L, gi), gsl(L, gi), 1.0))
    gates_done = ev("v", "sv", lambda e: e.drain())
    c.gsl = gsl
    c.gates_done = gates_done

    emit_stage_qkvT(c, c.l0qkv, 768)

    if PHASES >= 1:
        ln_done = emit_ln(c, 0, 0, 1, c.hbuf)
        r.wait("g", "sc", ln_done)
        for k in range(DK):
            dma("g", "sg", c.agi_h[ts(k), :], c.hbuf[:, k * TC:(k + 1) * TC])
        r.wait("g", "sg", r.val("sg"))
        c.cc_h0 = c.cc(lambda e: e.collective_compute(
            "AllGather", ALU.bypass, replica_groups=c.RG,
            ins=[c.agi_h.ap().opt()], outs=[c.ago_h.ap().opt()]))

        if DEBUG and "dbg_h" in c.dbg:
            emit_dump_bf16(c, c.hbuf, c.dbg["dbg_h"], after_sc=ln_done)

    if PHASES >= 2:
        emit_phase_l0attn(c)
        if DEBUG and "dbg_x0a" in c.dbg:
            emit_dump_f32(c, c.xres, c.dbg["dbg_x0a"], after_sv=c.r.val("sv"))

    if PHASES >= 3:
        emit_phase_moe(c)
        if DEBUG and "dbg_x0b" in c.dbg:
            emit_dump_f32(c, c.xres, c.dbg["dbg_x0b"], after_sv=c.r.val("sv"))

    if PHASES >= 4:
        emit_phase_l1attn(c)
        if DEBUG and "dbg_x1a" in c.dbg:
            emit_dump_f32(c, c.xres, c.dbg["dbg_x1a"], after_sv=c.r.val("sv"))

    if PHASES >= 5:
        emit_phase_ffn(c)

    r.wait("y", "sc", r.val("sc"))
    r.wait("y", "sv", r.val("sv"))
    r.wait("y", "st", r.val("st"))
    r.wait("y", "sg", r.val("sg"))
    fsem = r.fresh("fin")
    for k in range(DK):
        dma("y", fsem, c.out[ts(k), :], c.xres[:, k * TC:(k + 1) * TC])
    r.wait("y", fsem, r.val(fsem))


def emit_ln(c, L, gi_s, gi_c, outbuf, after=(), f32_out=None):
    r = c.r
    em, ev, dma = c.em, c.ev, c.dma
    gs = c.gsl(L, gi_s)
    gc = c.gsl(L, gi_c)
    r.wait("v", "sv", c.gates_done)
    r.wait("s", "sv", c.gates_done)
    for tk in after:
        r.wait("v", tk[0], tk[1])
        r.wait("s", tk[0], tk[1])
        r.wait("t", tk[0], tk[1])
    if c.ubuf_guard:
        for _tk in c.ubuf_guard:
            r.wait("v", _tk[0], _tk[1])
    if f32_out is not None:
        if not hasattr(c, "ln4sem"):
            c.ln4sem = [r.fresh("l4"), r.fresh("l4")]
        ln4_toks = {}
        ln4_red = {}
    last_apply_sc = {}
    for n in range(2):
        mmk_done = {}
        for k in range(DK):
            xsl = c.xres[:, k * TC + n * 512: k * TC + n * 512 + 512]
            if k >= 2:
                r.wait("v", "st", mmk_done[k - 2])
                r.wait("s", "st", mmk_done[k - 2])
            vv = ev("v", "sv", lambda e, xsl=xsl, k=k: e.tensor_copy(c.lnt[:, k % 2, :], xsl))
            vs = ev("s", "sc", lambda e, xsl=xsl, k=k: e.activation(c.lnq[:, k % 2, :], xsl, ACT.Square))
            r.wait("t", "sv", vv)
            r.wait("t", "sc", vs)
            em("t", lambda e, k=k: e.matmul(c.psB[0:1, :], c.onec[:, :], c.lnt[:, k % 2, :],
                                            start=(k == 0), stop=(k == DK - 1)))
            mmk_done[k] = ev("t", "st", lambda e, k=k: e.matmul(
                c.psC[0:1, :], c.onec[:, :], c.lnq[:, k % 2, :],
                start=(k == 0), stop=(k == DK - 1)))
        sums_done = mmk_done[DK - 1]
        r.wait("v", "st", sums_done)
        em("v", lambda e: e.tensor_scalar_mul(c.rowscr[:, 0:512], c.psB[0:1, :], 1.0 / D))
        em("v", lambda e: e.drain())
        em("v", lambda e: e.tensor_mul(c.rowscr[:, 512:1024], c.rowscr[:, 0:512], c.rowscr[:, 0:512]))
        em("v", lambda e: e.drain())
        em("v", lambda e: e.scalar_tensor_tensor(c.rowscr[:, 512:1024], c.psC[0:1, :], 1.0 / D,
                                                 c.rowscr[:, 512:1024], ALU.mult, ALU.subtract))
        em("v", lambda e: e.drain())
        em("v", lambda e: e.tensor_scalar_add(c.rowscr[:, 512:1024], c.rowscr[:, 512:1024], EPS))
        em("v", lambda e: e.drain())
        vv = ev("v", "sv", lambda e: e.reciprocal(c.rowscr[:, 512:1024], c.rowscr[:, 512:1024]))
        r.wait("s", "sv", vv)
        vs = ev("s", "sc", lambda e: e.activation(c.rowscr[:, 512:1024], c.rowscr[:, 512:1024], ACT.Sqrt))
        r.wait("v", "sc", vs)
        vv2 = ev("v", "sv", lambda e: e.tensor_mul(c.rowscr[:, 0:512], c.rowscr[:, 0:512], c.rowscr[:, 512:1024]))
        r.wait("t", "sv", vv2)
        em("t", lambda e: e.matmul(c.psD[:, :], c.onerf[:, :], c.rowscr[0:1, 512:1024], start=True, stop=True))
        bc_done = ev("t", "st", lambda e: e.matmul(c.psE[:, :], c.onerf[:, :], c.rowscr[0:1, 0:512], start=True, stop=True))
        r.wait("v", "st", bc_done)
        r.wait("s", "st", bc_done)
        for k in range(DK):
            xsl = c.xres[:, k * TC + n * 512: k * TC + n * 512 + 512]
            osl = outbuf[:, k * TC + n * 512: k * TC + n * 512 + 512]
            usl = c.ubuf[:, k % 2, :]
            nk = n * DK + k
            if k >= 2:
                r.wait("v", "sc", last_apply_sc[(n, k - 2)])
            if f32_out is not None and nk >= 2:
                r.wait("v", c.ln4sem[k % 2], ln4_toks[nk - 2])
                r.wait("v", "sv", ln4_red[nk - 2])
            em("v", lambda e, xsl=xsl, usl=usl: e.tensor_mul(usl, xsl, c.psD[:, :]))
            em("v", lambda e: e.drain())
            vv = ev("v", "sv", lambda e, usl=usl: e.tensor_sub(usl, usl, c.psE[:, :]))
            r.wait("s", "sv", vv)
            if f32_out is None:
                last_apply_sc[(n, k)] = ev("s", "sc", lambda e, usl=usl, osl=osl, k=k: e.activation(
                    osl, usl, ACT.Identity, bias=gs[:, k:k + 1], scale=gc[:, k:k + 1]))
            else:
                last_apply_sc[(n, k)] = ev("s", "sc", lambda e, usl=usl, k=k: e.activation(
                    usl, usl, ACT.Identity, bias=gs[:, k:k + 1], scale=gc[:, k:k + 1]))
                r.wait("y", "sc", last_apply_sc[(n, k)])
                ln4_toks[nk] = c.dma("y", c.ln4sem[k % 2],
                                     f32_out.ap()[ts(k), n * 512:(n + 1) * 512], usl)
                r.wait("v", "sc", last_apply_sc[(n, k)])
                ln4_red[nk] = ev("v", "sv", lambda e, usl=usl, k=k, n=n: e.reduce_max(
                    c.smx[:, 32 + k * 2 + n:33 + k * 2 + n], usl, axis=AXX,
                    apply_absolute_value=True))
        if n == 0:
            r.wait("t", "sv", r.val("sv"))
    c.ubuf_guard = [("sc", r.val("sc"))]
    if f32_out is not None:
        c.ln4_last = ([(s_, r.val(s_)) for s_ in c.ln4sem], ln4_red[2 * DK - 1])
    return last_apply_sc[(1, DK - 1)]


def emit_dump_bf16(c, srcbuf, dst, after_sc=0):
    r = c.r
    em, ev, dma = c.em, c.ev, c.dma
    if not hasattr(c, "du_sem"):
        c.du_sem = r.fresh("du")
    dsem = c.du_sem
    r.wait("v", "sc", after_sc)
    if c.ubuf_guard:
        for _tk in c.ubuf_guard:
            r.wait("v", _tk[0], _tk[1])
    for k in range(DK):
        vv = ev("v", "sv", lambda e, k=k: e.tensor_copy(c.ubuf[:, 0:2, :].rearrange("p a b -> p (a b)"),
                                                        srcbuf[:, k * TC:(k + 1) * TC]))
        r.wait("y", "sv", vv)
        vv2 = dma("y", dsem, dst[ts(k), :], c.ubuf[:, 0:2, :].rearrange("p a b -> p (a b)"))
        r.wait("y", dsem, vv2)
        r.wait("v", dsem, vv2)
    c.ubuf_guard = [(dsem, r.val(dsem))]
    return r.val(dsem)


def shard_inputs(I):
    maps = []
    x = np.asarray(I["L0_attn_x"])[0]
    WK, WQ, WV = np.asarray(I["L0_attn_W_K"]), np.asarray(I["L0_attn_W_Q"]), np.asarray(I["L0_attn_W_V"])
    for cid in range(NCORE):
        m = {}
        m["x"] = np.ascontiguousarray(x[cid * TC:(cid + 1) * TC, :].T)
        m["l0c"] = np.ascontiguousarray(np.asarray(I["L0_c"]).reshape(D, 1))
        m["l1c"] = np.ascontiguousarray(np.asarray(I["L1_c"]).reshape(D, 1))
        m["l0wmod"] = np.ascontiguousarray(np.asarray(I["L0_wmod"])[:, cid * 1536:(cid + 1) * 1536])
        m["l1wmod"] = np.ascontiguousarray(np.asarray(I["L1_wmod"])[:, cid * 1536:(cid + 1) * 1536])
        qkv = np.concatenate([WK[2 * cid:2 * cid + 2].reshape(256, D),
                              WQ[2 * cid:2 * cid + 2].reshape(256, D),
                              WV[2 * cid:2 * cid + 2].reshape(256, D)], axis=0)
        m["l0qkv"] = np.ascontiguousarray(qkv)
        m["l1qkv"] = np.ascontiguousarray(np.concatenate(
            [np.asarray(I["L1_attn_Wq"])[cid * 256:(cid + 1) * 256, :],
             np.asarray(I["L1_attn_Wk"])[cid * 256:(cid + 1) * 256, :],
             np.asarray(I["L1_attn_Wv"])[cid * 256:(cid + 1) * 256, :],
             np.asarray(I["L1_attn_W_ilr"])[4 * cid:4 * cid + 4, :]], axis=0))
        m["l0wo_s"] = np.ascontiguousarray(np.asarray(I["L0_attn_W_O"])[:, cid * 256:(cid + 1) * 256])
        m["l1wo_s"] = np.ascontiguousarray(np.asarray(I["L1_attn_Wo"])[:, cid * 256:(cid + 1) * 256])
        m["wg"] = np.ascontiguousarray(np.asarray(I["L0_ffn_Wg"]))
        m["moe1s"] = np.ascontiguousarray(np.asarray(I["L0_ffn_W1"])[:, :, cid * 256:(cid + 1) * 256])
        m["moe2s"] = np.ascontiguousarray(np.asarray(I["L0_ffn_W2"])[:, :, cid * 256:(cid + 1) * 256])
        m["moe3s"] = np.ascontiguousarray(np.asarray(I["L0_ffn_W3"])[:, :, cid * 176:(cid + 1) * 176])
        m["bilr"] = np.ascontiguousarray(np.asarray(I["L1_attn_b_ilr"])[4 * cid:4 * cid + 4].reshape(1, 4))
        m["w1h"] = np.ascontiguousarray(np.asarray(I["L1_attn_W1"])[4 * cid:4 * cid + 4].reshape(256, 64))
        m["wfc_s"] = np.ascontiguousarray(np.asarray(I["L1_ffn_w_fc"])[:, cid * 256:(cid + 1) * 256])
        m["wproj_s"] = np.ascontiguousarray(np.asarray(I["L1_ffn_w_proj"])[:, cid * 1024:(cid + 1) * 1024])
        maps.append(m)
    return maps


def kernel(**inputs):
    nc = build()
    maps = shard_inputs(inputs)
    res = run_bass_kernel_spmd(nc, maps, core_ids=list(range(NCORE)))
    full = np.concatenate([np.asarray(r_["out"]).T for r_ in res.results], axis=0)
    return full.reshape(1, T, D).astype(np.float32)


def kernel_debug(inputs):
    nc = build()
    maps = shard_inputs(inputs)
    return run_bass_kernel_spmd(nc, maps, core_ids=list(range(NCORE)))


def emit_dump_f32(c, srcbuf, dst, after_sv=0):
    r = c.r
    dma = c.dma
    if not hasattr(c, "du2_sem"):
        c.du2_sem = r.fresh("du2")
    dsem = c.du2_sem
    r.wait("y", "sv", after_sv)
    r.wait("y", "sc", r.val("sc"))
    for k in range(DK):
        dma("y", dsem, dst[ts(k), :], srcbuf[:, k * TC:(k + 1) * TC])
        r.wait("y", dsem, r.val(dsem))
    return r.val(dsem)


def emit_stage_qkvT(c, src, ncols, col_off=0):
    r, em, ev, dma = c.r, c.em, c.ev, c.dma
    psb = [c.psD.ap().bitcast(BF16), c.psE.ap().bitcast(BF16)]
    if not hasattr(c, "qkT_sems"):
        c.qkT_sems = [r.fresh("qk", "gdma"), r.fresh("qk", "gdma")]
        c.qkT_i = 0
    if c.stg_gate is not None:
        c.r.wait("g", c.stg_gate[0], c.stg_gate[1])
    qsem = c.qkT_sems
    nrt = ncols // 128
    r.wait("t", "sv", r.val("sv"))
    r.wait("t", "sc", r.val("sc"))

    ld_tok = {}
    ev_done = {}
    for i, (rt, half) in enumerate([(rt, h) for rt in range(nrt) for h in range(2)]):
        gi = c.qkT_i + i
        buf = gi % 2
        sem = qsem[buf]
        if i >= 2:
            r.wait("g", "st", ev_done[i - 2])
        elif c.qkT_i > 0:
            r.wait("g", "st", c.qkT_lastev)
        dma("g", sem, c.stg[:, buf, :], src[ts(rt), half * 1024:(half + 1) * 1024])
        ld_tok[i] = (sem, r.val(sem))
        r.wait("t", sem, r.val(sem))
        for grp in range(2):
            pb = psb[grp]
            if i >= 1:
                pass
            for sub in range(4):
                k = half * 8 + grp * 4 + sub
                em("t", lambda e, buf=buf, grp=grp, sub=sub, pb=pb: e.transpose(
                    pb[:, sub * 128:(sub + 1) * 128], c.stg[:, buf, (grp * 4 + sub) * 128:(grp * 4 + sub + 1) * 128],
                    c.idb[:, :]))
            tok = r.mark("t", "st")
            r.wait("v", "st", tok)
            k0 = half * 8 + grp * 4
            vv = ev("v", "sv", lambda e, pb=pb, k0=k0, rt=rt: e.tensor_copy(
                c.wqkvT[:, k0:k0 + 4, col_off + rt * 128: col_off + (rt + 1) * 128], pb[:, 0:512]))
            r.wait("t", "sv", vv)
        ev_done[i] = r.val("st")
    c.qkvT_done = r.val("sv")
    c.qkT_i += 2 * nrt
    c.qkT_lastev = ev_done[2 * nrt - 1]
    c.stg_gate = ("st", r.val("st"))


def emit_stage_agi(c, src, R, C, agi_rows_ap, ag_in_ap, ag_out_ap, tag):
    r, em, ev, dma = c.r, c.em, c.ev, c.dma
    psb = [c.psD.ap().bitcast(BF16), c.psE.ap().bitcast(BF16)]
    if not hasattr(c, "agi_sems"):
        c.agi_sems = [c.r.fresh("ag", "gdma"), c.r.fresh("ag", "gdma")]
        c.agi_i = 0
        c.agi_ev = {}
    if c.stg_gate is not None:
        c.r.wait("g", c.stg_gate[0], c.stg_gate[1])
    gsem = c.agi_sems
    r.wait("t", "sv", r.val("sv"))
    r.wait("t", "sc", r.val("sc"))
    asmv = c.stgf.ap().bitcast(BF16)
    asmv = asmv.rearrange("p a b -> p (a b)")
    cts = []
    off = 0
    while off < C:
        sz = min(128, C - off)
        cts.append((off, sz))
        off += sz
    nrt = (R + 127) // 128
    for rt in range(nrt):
        gi = c.agi_i + rt
        rsz = min(128, R - rt * 128)
        buf = gi % 2
        sem = gsem[buf]
        if gi >= 2:
            r.wait("g", "st", c.agi_ev[gi - 2])
        dma("g", sem, c.stg[0:rsz, buf, 0:C], src[rt * 128: rt * 128 + rsz, :])
        r.wait("t", sem, r.val(sem))
        for ci, (coff, csz) in enumerate(cts):
            pb = psb[gi % 2]
            em("t", lambda e, buf=buf, coff=coff, csz=csz, rsz=rsz, pb=pb, ci=ci: e.transpose(
                pb[0:csz, ci * 128: ci * 128 + rsz], c.stg[0:rsz, buf, coff:coff + csz], c.idb[:, :]))
        tok = r.mark("t", "st")
        c.agi_ev[gi] = tok
        r.wait("v", "st", tok)
        for ci, (coff, csz) in enumerate(cts):
            pb = psb[gi % 2]
            vv = ev("v", "sv", lambda e, pb=pb, ci=ci, csz=csz, rsz=rsz, rt=rt: e.tensor_copy(
                asmv[0:csz, ci * 3072 + rt * 128: ci * 3072 + rt * 128 + rsz], pb[0:csz, ci * 128: ci * 128 + rsz]))
        r.wait("t", "sv", r.val("sv"))
    r.wait("g", "sv", r.val("sv"))
    r.wait("g", "sg", r.val("sg"))
    for ci, (coff, csz) in enumerate(cts):
        dma("g", "sg", agi_rows_ap(ci, csz), asmv[0:csz, ci * 3072: ci * 3072 + R])
    r.wait("g", "sg", r.val("sg"))
    tok = c.cc(lambda e: e.collective_compute(
        "AllGather", ALU.bypass, replica_groups=c.RG,
        ins=[ag_in_ap], outs=[ag_out_ap]))
    c.agi_i += nrt
    c.stg_gate = ("st", r.val("st"))
    return tok


def emit_phase_l0attn(c):
    r, em, ev, dma = c.r, c.em, c.ev, c.dma
    nc = c.nc
    AR = c.arena
    Q0, K0, V0, PT, ZST = 0, 8192, 16384, 24576, 26624
    psb_S = c.psA
    psZ = c.psB
    psSum = c.psC

    emit_stage_woT(c, c.l0wo_s, c.wo0T)
    c.cc_moe = {}
    for e_ in range(2):
        emit_stage_moe_expert(c, e_)

    emit_wmx_pass(c)

    c.waitcc("y", c.cc_h0)
    for hd in range(2):
        emit_qkv_prod(c, hd, Q0, K0, V0, PT)
        emit_stabilizer(c, hd, Q0, K0)
        emit_attn_bands(c, hd, Q0, K0, V0, PT, ZST)
        if hd == 0:
            for e_ in range(2, 5):
                emit_stage_moe_expert(c, e_)
        else:
            for e_ in range(5, 8):
                emit_stage_moe_expert(c, e_)
    if DEBUG and "dbg_z" in c.dbg:
        r.wait("g", c.attn_done[0], c.attn_done[1])
        r.wait("g", "sg", r.val("sg"))
        dma("g", "sg", c.dbg["dbg_z"].ap(), c.zloc0.ap())
    emit_wo_partial(c, c.zloc0, c.wo0T, c.attn_done, L=0, gi_gate=2)


def emit_stage_moe_expert(c, e_):
    r = c.r
    tok1 = emit_stage_agi(c, c.moe1s.ap()[e_], HM, 256,
                          lambda ci, csz, e_=e_: c.agi_m1.ap()[e_, ci * 128: ci * 128 + csz, :],
                          c.agi_m1.ap()[e_], c.ago_m1[e_].ap().opt(), f"m1_{e_}")
    tok2 = emit_stage_agi(c, c.moe2s.ap()[e_], HM, 256,
                          lambda ci, csz, e_=e_: c.agi_m2.ap()[e_, ci * 128: ci * 128 + csz, :],
                          c.agi_m2.ap()[e_], c.ago_m2[e_].ap().opt(), f"m2_{e_}")
    tok3 = emit_stage_agi(c, c.moe3s.ap()[e_], D, 176,
                          lambda ci, csz, e_=e_: c.agi_m3.ap()[e_, ci * 128: ci * 128 + csz, :],
                          c.agi_m3.ap()[e_], c.ago_m3[e_].ap().opt(), f"m3_{e_}")
    c.cc_moe[e_] = (tok1, tok2, tok3)
def emit_wmx_pass(c):
    r, em, ev, dma = c.r, c.em, c.ev, c.dma
    for j, (src, RR, CC) in enumerate(((c.wfc_s, FFH, 256), (c.wproj_s, D, 1024))):
        flat = src.ap().rearrange("a b -> (a b)").rearrange("(n p q) -> n p q", p=128, q=512)
        nld = (RR * CC) // (128 * 512)
        toks = {}
        if not hasattr(c, "wx_sems"):
            c.wx_sems = [r.fresh("wx"), r.fresh("wx")]
        xsem = c.wx_sems
        if c.ubuf_guard:
            for _tk in c.ubuf_guard:
                r.wait("y", _tk[0], _tk[1])
        for i in range(nld):
            buf = i % 2
            sem = xsem[buf]
            if i >= 2:
                r.wait("y", "sv", toks[i - 2])
            dma("y", sem, c.ubuf[:, buf, :], flat[i])
            r.wait("v", sem, r.val(sem))
            toks[i] = ev("v", "sv", lambda e, buf=buf, j=j, i=i: e.reduce_max(
                c.smx[:, j * 32 + i:j * 32 + i + 1], c.ubuf[:, buf, :], axis=AXX,
                apply_absolute_value=True))
        em("v", lambda e: e.drain())
        em("v", lambda e, j=j, nld=nld: e.reduce_max(c.smx2[:, j:j + 1], c.smx[:, j * 32:j * 32 + nld], axis=AXX))
        vg = ev("v", "sv", lambda e: e.drain())
        c.ubuf_guard = [("sv", vg)]
    vv = ev("v", "sv", lambda e: e.drain())
    r.wait("t", "sv", vv)
    r.wait("t", "sc", r.val("sc"))
    tok = ev("t", "st", lambda e: e.transpose(c.psE[0:2, 0:128], c.smx2[:, 0:2], c.idf[:, :]))
    r.wait("v", "st", tok)
    vv = ev("v", "sv", lambda e: e.reduce_max(c.smx2[0:2, 2:3], c.psE[0:2, 0:128], axis=AXX))
    r.wait("g", "sv", vv)
    dma("g", "sg", c.agi_wmx[:, :], c.smx2[0:2, 2:3])
    r.wait("g", "sg", r.val("sg"))
    c.cc_wmx = c.cc(lambda e: e.collective_compute(
        "AllReduce", ALU.max, replica_groups=c.RG,
        ins=[c.agi_wmx.ap().opt()], outs=[c.ago_wmx.ap().opt()]))
    c.waitcc("g", c.cc_wmx)
    r.wait("g", "sg", r.val("sg"))
    dma("g", "sg", c.smx2[0:2, 3:4], c.ago_wmx[:, :])
    r.wait("v", "sg", r.val("sg"))
    em("v", lambda e: e.tensor_scalar(c.smx2[0:2, 3:4], c.smx2[0:2, 3:4], 1.0 / 448.0, 1e-6, ALU.mult, ALU.max))
    em("v", lambda e: e.drain())
    em("v", lambda e: e.reciprocal(c.smx2[0:2, 4:5], c.smx2[0:2, 3:4]))
    em("v", lambda e: e.drain())
    vv = ev("v", "sv", lambda e: e.tensor_scalar_mul(c.smx2[0:2, 4:5], c.smx2[0:2, 4:5], 0.5))
    r.wait("t", "sv", vv)
    tok = ev("t", "st", lambda e: e.transpose(c.psE[0:1, 24:26], c.smx2[0:2, 4:5], c.idf[0:2, 0:2]))
    r.wait("v", "st", tok)
    vv = ev("v", "sv", lambda e: e.tensor_copy(c.smx[0:1, 30:32], c.psE[0:1, 24:26]))
    r.wait("t", "sv", vv)
    em("t", lambda e: e.matmul(c.psE[:, 8:10], c.onerf[:, :], c.smx[0:1, 30:32], start=True, stop=True))
    tok = r.mark("t", "st")
    r.wait("v", "st", tok)
    vv = ev("v", "sv", lambda e: e.tensor_copy(c.smx2[:, 5:7], c.psE[:, 8:10]))
    c.rw_done = vv


def emit_qkv_prod(c, hd, Q0, K0, V0, PT):
    r, em, ev, dma = c.r, c.em, c.ev, c.dma
    AR = c.arena
    kcol = 128 * hd
    qcol = 256 + 128 * hd
    vcol = 512 + 128 * hd
    r.wait("t", "sv", c.qkvT_done)
    r.wait("t", "sv", r.val("sv"))
    r.wait("t", "sc", r.val("sc"))
    r.wait("y", "st", r.val("st"))
    r.wait("y", "sc", r.val("sc"))
    c.waitcc("y", c.cc_h0)
    evac_tok = 0
    if not hasattr(c, "h_sems"):
        c.h_sems = [r.fresh("hs"), r.fresh("hs")]
        c.h_i = 0
        c.h_lastmm = None
    hsem = c.h_sems
    mm_tok = {}
    for r8 in range(8):
        for k in range(DK):
            i = r8 * DK + k
            buf = (c.h_i + i) % 2
            sem = hsem[buf]
            if i >= 2:
                r.wait("y", "st", mm_tok[i - 2])
            elif c.h_lastmm is not None:
                r.wait("y", "st", c.h_lastmm)
            dma("y", sem, AR[:, PT + buf * 1024: PT + (buf + 1) * 1024],
                c.ago_h.ap()[r8 * D + k * 128: r8 * D + (k + 1) * 128, :])
            r.wait("t", sem, r.val(sem))
            if k == 0 and evac_tok:
                r.wait("t", "sv", evac_tok)
                r.wait("t", "sc", evac_sc)
            ht = AR[:, PT + buf * 1024: PT + (buf + 1) * 1024]
            for n in range(2):
                em("t", lambda e, ht=ht, n=n, k=k, qcol=qcol: e.matmul(
                    c.psA[:, n * 512:(n + 1) * 512], c.wqkvT[:, k, qcol:qcol + 128],
                    ht[:, n * 512:(n + 1) * 512], start=(k == 0), stop=(k == DK - 1)))
                em("t", lambda e, ht=ht, n=n, k=k, kcol=kcol: e.matmul(
                    c.psA[:, 1024 + n * 512: 1024 + (n + 1) * 512], c.wqkvT[:, k, kcol:kcol + 128],
                    ht[:, n * 512:(n + 1) * 512], start=(k == 0), stop=(k == DK - 1)))
            for n in range(2):
                ps = c.psB if n == 0 else c.psC
                em("t", lambda e, ht=ht, n=n, k=k, vcol=vcol, ps=ps: e.matmul(
                    ps[:, :], c.wqkvT[:, k, vcol:vcol + 128],
                    ht[:, n * 512:(n + 1) * 512], start=(k == 0), stop=(k == DK - 1)))
            mm_tok[i] = r.mark("t", "st")
        r.wait("s", "st", mm_tok[r8 * DK + DK - 1])
        r.wait("v", "st", mm_tok[r8 * DK + DK - 1])
        for n in range(2):
            em("s", lambda e, r8=r8, n=n, Q0=Q0: e.activation(
                AR[:, Q0 + r8 * 1024 + n * 512: Q0 + r8 * 1024 + (n + 1) * 512],
                c.psA[:, n * 512:(n + 1) * 512], ACT.Copy))
            em("s", lambda e, r8=r8, n=n, K0=K0: e.activation(
                AR[:, K0 + r8 * 1024 + n * 512: K0 + r8 * 1024 + (n + 1) * 512],
                c.psA[:, 1024 + n * 512: 1024 + (n + 1) * 512], ACT.Copy))
        evac_sc = r.mark("s", "sc")
        em("v", lambda e: e.tensor_copy(c.lnq[:, 0, :], c.psB[:, :]))
        vfm = ev("v", "sv", lambda e: e.tensor_copy(c.lnq[:, 1, :], c.psC[:, :]))
        r.wait("t", "sv", vfm)
        psDbf2 = c.psD.ap().bitcast(BF16)
        psEbf2 = c.psE.ap().bitcast(BF16)
        for j in range(8):
            pbx = psDbf2 if j < 4 else psEbf2
            em("t", lambda e, j=j, pbx=pbx: e.transpose(
                pbx[:, (j % 4) * 128:(j % 4 + 1) * 128],
                c.lnq[:, j // 4, (j % 4) * 128:(j % 4 + 1) * 128], c.idb[:, :]))
        ttk = r.mark("t", "st")
        r.wait("v", "st", ttk)
        em("v", lambda e, r8=r8, V0=V0, psDbf2=psDbf2: e.tensor_copy(
            AR[:, V0 + r8 * 1024: V0 + r8 * 1024 + 512], psDbf2[:, 0:512]))
        evac_tok = ev("v", "sv", lambda e, r8=r8, V0=V0, psEbf2=psEbf2: e.tensor_copy(
            AR[:, V0 + r8 * 1024 + 512: V0 + (r8 + 1) * 1024], psEbf2[:, 0:512]))
        r.wait("t", "sv", evac_tok)
    c.qkv_prod_done = (evac_tok, evac_sc)
    c.h_i += 8 * DK
    c.h_lastmm = mm_tok[8 * DK - 1]


def emit_stabilizer(c, hd, Q0, K0):
    r, em, ev, dma = c.r, c.em, c.ev, c.dma
    AR = c.arena
    r.wait("s", "sv", c.qkv_prod_done[0])
    r.wait("t", "sv", r.val("sv"))
    r.wait("t", "sc", r.val("sc"))
    r.wait("s", "st", r.val("st"))
    sq_tok = {}
    for j, base in enumerate((Q0, K0)):
        for k in range(16):
            kk = j * 16 + k
            if kk >= 2:
                r.wait("s", "st", sq_tok[kk - 2])
            vs = ev("s", "sc", lambda e, base=base, k=k: e.activation(
                c.lnq[:, k % 2, :], AR[:, base + k * 512: base + (k + 1) * 512], ACT.Square))
            r.wait("t", "sc", vs)
            sq_tok[kk] = ev("t", "st", lambda e, k=k: e.matmul(
                c.psE[0:1, :], c.onec[:, :], c.lnq[:, k % 2, :], start=True, stop=True))
            r.wait("v", "st", sq_tok[kk])
            ev("v", "sv", lambda e, j=j, k=k: e.reduce_max(
                c.smx[0:1, 32 + j * 16 + k: 33 + j * 16 + k], c.psE[0:1, :], axis=AXX))
            r.wait("t", "sv", r.val("sv"))
    em("v", lambda e: e.drain())
    em("v", lambda e: e.reduce_max(c.smx[0:1, 60:61], c.smx[0:1, 32:48], axis=AXX))
    em("v", lambda e: e.drain())
    em("v", lambda e: e.reduce_max(c.smx[0:1, 61:62], c.smx[0:1, 48:60], axis=AXX) if False else
       e.reduce_max(c.smx[0:1, 61:62], c.smx[0:1, 48:64], axis=AXX))
    em("v", lambda e: e.drain())
    em("v", lambda e: e.tensor_mul(c.smx[0:1, 62:63], c.smx[0:1, 60:61], c.smx[0:1, 61:62]))
    vv = ev("v", "sv", lambda e: e.drain())
    r.wait("s", "sv", vv)
    vs = ev("s", "sc", lambda e: e.activation(c.smx[0:1, 62:63], c.smx[0:1, 62:63], ACT.Sqrt))
    r.wait("v", "sc", vs)
    vv = ev("v", "sv", lambda e: e.tensor_scalar_mul(c.smx[0:1, 63:64], c.smx[0:1, 62:63], -RS128))
    r.wait("t", "sv", vv)
    em("t", lambda e: e.matmul(c.psE[:, 12:13], c.onerf[:, :], c.smx[0:1, 63:64], start=True, stop=True))
    tok = r.mark("t", "st")
    r.wait("v", "st", tok)
    c.bias_done = ev("v", "sv", lambda e: e.tensor_copy(c.smx2[:, 10:11], c.psE[:, 12:13]))


def emit_attn_bands(c, hd, Q0, K0, V0, PT, ZST):
    r, em, ev, dma = c.r, c.em, c.ev, c.dma
    AR = c.arena
    r.wait("t", "sv", c.bias_done)
    r.wait("s", "sv", c.bias_done)
    if not hasattr(c, "z_sem"):
        c.z_sem = c.r.fresh("zs")
    zsem = c.z_sem
    prev_av = c.band_prev_av if hasattr(c, "band_prev_av") else None
    prev_band_done = c.z_prev if hasattr(c, "z_prev") else 0
    for J in range(16):
        nkt = 4 * J + 4
        exp_toks = {}
        msk_toks = {}
        av_toks = {}
        for kt in range(nkt):
            gpar = (kt // 2) % 2
            slot = kt % 4
            if kt >= 4:
                r.wait("t", "sc", exp_toks[kt - 4])
            if kt >= 4:
                r.wait("s", "st", av_toks[kt - 4] if False else 0)
            em("t", lambda e, kt=kt, J=J, gpar=gpar, K0=K0, Q0=Q0: e.matmul(
                c.psA[:, gpar * 1024 + (kt % 2) * 512: gpar * 1024 + (kt % 2 + 1) * 512],
                AR[:, K0 + kt * 128: K0 + (kt + 1) * 128],
                AR[:, Q0 + J * 512: Q0 + (J + 1) * 512], start=True, stop=True))
            stok = r.mark("t", "st")
            r.wait("s", "st", stok)
            if kt >= 4:
                r.wait("s", "st", av_toks[kt - 4])
            elif prev_av is not None:
                r.wait("s", "st", prev_av[len(prev_av) - 4 + kt])
            exp_toks[kt] = ev("s", "sc", lambda e, kt=kt, gpar=gpar, slot=slot: e.activation(
                AR[:, PT + slot * 512: PT + (slot + 1) * 512],
                c.psA[:, gpar * 1024 + (kt % 2) * 512: gpar * 1024 + (kt % 2 + 1) * 512],
                ACT.Exp, bias=c.smx2[:, 10:11], scale=RS128))
            is_diag = (kt >= 4 * J)
            if is_diag:
                rrel = kt - 4 * J
                r.wait("g", "sc", exp_toks[kt])
                msk_toks[kt] = ev("g", "sge", lambda e, slot=slot, rrel=rrel: e.affine_select(
                    AR[:, PT + slot * 512: PT + (slot + 1) * 512],
                    AR[:, PT + slot * 512: PT + (slot + 1) * 512],
                    pattern=[[1, 512]], compare_op=ALU.is_ge, fill=0.0,
                    base=-128 * rrel, channel_multiplier=-1))
                r.wait("t", "sge", msk_toks[kt])
            else:
                r.wait("t", "sc", exp_toks[kt])
            em("t", lambda e, kt=kt, slot=slot, V0=V0, nkt=nkt: e.matmul(
                c.psB[:, :], AR[:, V0 + kt * 128: V0 + (kt + 1) * 128],
                AR[:, PT + slot * 512: PT + (slot + 1) * 512],
                start=(kt == 0), stop=(kt == nkt - 1)))
            av_toks[kt] = ev("t", "st", lambda e, kt=kt, slot=slot, nkt=nkt: e.matmul(
                c.psC[0:1, :], c.onec[:, :], AR[:, PT + slot * 512: PT + (slot + 1) * 512],
                start=(kt == 0), stop=(kt == nkt - 1)))
        last = av_toks[nkt - 1]
        r.wait("v", "st", last)
        if prev_band_done:
            r.wait("v", zsem, prev_band_done)
        zv = ev("v", "sv", lambda e: e.tensor_copy(c.lnq[:, 0, :], c.psB[:, :]))
        sv_ = ev("v", "sv", lambda e: e.tensor_copy(c.rowscr[:, 0:512], c.psC[0:1, :]))
        r.wait("t", "sv", sv_)
        psDbf = c.psD.ap().bitcast(BF16)
        for j in range(4):
            em("t", lambda e, j=j, psDbf=psDbf: e.transpose(
                psDbf[:, j * 128:(j + 1) * 128], c.lnq[:, 0, j * 128:(j + 1) * 128], c.idb[:, :]))
        for j in range(4):
            em("t", lambda e, j=j: e.transpose(c.psE[:, j:j + 1], c.rowscr[0:1, j * 128:(j + 1) * 128],
                                               c.idf[0:1, 0:1]))
        ttok = r.mark("t", "st")
        r.wait("v", "st", ttok)
        vv = ev("v", "sv", lambda e: e.reciprocal(c.smx[:, 24:28], c.psE[:, 0:4]))
        r.wait("s", "sv", vv)
        for j in range(4):
            em("s", lambda e, j=j, psDbf=psDbf: e.activation(
                c.lnt[:, 0, j * 128:(j + 1) * 128], psDbf[:, j * 128:(j + 1) * 128],
                ACT.Identity, bias=0.0, scale=c.smx[:, 24 + j:25 + j]))
        sevt = r.mark("s", "sc")
        r.wait("s", "sc", sevt)
        dst = c.zloc0.ap()[J * 512:(J + 1) * 512, hd * 128:(hd + 1) * 128].rearrange(
            "(j p) f -> p j f", p=128)
        dma("s", zsem, dst, c.lnt[:, 0, :].rearrange("p (j f) -> p j f", j=4))
        prev_band_done = r.val(zsem)
        r.wait("t", "sv", vv)
        r.wait("t", "sc", sevt)
    c.z_prev = prev_band_done
    c.band_prev_av = prev_av
    c.attn_done = (zsem, prev_band_done)


def emit_stage_wilr(c):
    r, em, ev, dma = c.r, c.em, c.ev, c.dma
    ilsem = r.fresh("il", "gdma")
    if c.stg_gate is not None:
        r.wait("g", c.stg_gate[0], c.stg_gate[1])
    dma("g", ilsem, c.stg[0:4, 0, :], c.l1qkv[768:772, 0:1024])
    dma("g", ilsem, c.stg[0:4, 1, :], c.l1qkv[768:772, 1024:2048])
    r.wait("t", "sv", r.val("sv"))
    r.wait("t", "sc", r.val("sc"))
    r.wait("t", ilsem, r.val(ilsem))
    psDbf = c.psD.ap().bitcast(BF16)
    for k in range(DK):
        half, off = divmod(k, 8)
        em("t", lambda e, half=half, off=off, k=k, psDbf=psDbf: e.transpose(
            psDbf[0:128, k * 4:(k + 1) * 4], c.stg[0:4, half, off * 128:(off + 1) * 128], c.idb[0:4, 0:4]))
    tok = r.mark("t", "st")
    r.wait("v", "st", tok)
    vv = ev("v", "sv", lambda e, psDbf=psDbf: e.tensor_copy(
        c.wqkvT[:, 0:DK, 768:772], psDbf[:, 0:64].rearrange("p (k f) -> p k f", f=4)))
    r.wait("t", "sv", vv)
    blsem = r.fresh("bl", "gdma")
    dma("g", blsem, c.smx[0:1, 28:32], c.bilr_in[:, :])
    r.wait("t", blsem, r.val(blsem))
    em("t", lambda e: e.matmul(c.psE[:, 16:20], c.onerf[:, :], c.smx[0:1, 28:32], start=True, stop=True))
    tok = r.mark("t", "st")
    r.wait("v", "st", tok)
    c.bilr_done = ev("v", "sv", lambda e: e.tensor_copy(c.smx2[:, 12:16], c.psE[:, 16:20]))
    c.stg_gate = ("st", r.val("st"))
def emit_stage_woT(c, wo_s, woT_dram):
    r, em, ev, dma = c.r, c.em, c.ev, c.dma
    psb = [c.psD.ap().bitcast(BF16), c.psE.ap().bitcast(BF16)]
    if not hasattr(c, "woT_sems"):
        c.woT_sems = [r.fresh("wo", "gdma"), r.fresh("wo", "gdma")]
        c.woT_osem = [r.fresh("woo"), r.fresh("woo")]
        c.woT_g = None
    wsem = c.woT_sems
    osem = c.woT_osem
    if c.stg_gate is not None:
        r.wait("g", c.stg_gate[0], c.stg_gate[1])
    if c.woT_g is not None:
        r.wait("g", "sv", c.woT_g)
    r.wait("t", "sv", r.val("sv"))
    r.wait("t", "sc", r.val("sc"))
    ub = c.ubuf.ap().bitcast(BF16)
    if c.ubuf_guard:
        for _tk in c.ubuf_guard:
            r.wait("v", _tk[0], _tk[1])
    evtoks = {}
    for rt in range(DK):
        buf = rt % 2
        sem = wsem[buf]
        if rt >= 2:
            r.wait("g", "sv", evtoks[rt - 2])
        dma("g", sem, c.stg[:, buf, 0:256], wo_s[ts(rt), :])
        r.wait("t", sem, r.val(sem))
        if rt >= 2:
            r.wait("t", "sv", evtoks[rt - 2])
        pb = psb[rt % 2]
        for ci in range(2):
            em("t", lambda e, buf=buf, ci=ci, pb=pb: e.transpose(
                pb[:, ci * 128:(ci + 1) * 128], c.stg[:, buf, ci * 128:(ci + 1) * 128], c.idb[:, :]))
        tok = r.mark("t", "st")
        r.wait("v", "st", tok)
        if rt >= 2:
            r.wait("v", osem[rt % 2], dmatoks[rt - 2])
        evtoks[rt] = ev("v", "sv", lambda e, pb=pb, buf=buf, ub=ub: e.tensor_copy(
            ub[:, buf, 0:256], pb[:, 0:256]))
        r.wait("s", "sv", evtoks[rt])
        if rt == 0:
            dmatoks = {}
        dmatoks[rt] = dma("s", osem[rt % 2], woT_dram.ap()[:, rt * 128:(rt + 1) * 128].rearrange(
            "(a p) f -> p a f", p=128), ub[:, buf, 0:256].rearrange("p (a f) -> p a f", a=2))
    c.woT_stage_done = [(osem[0], r.val(osem[0])), (osem[1], r.val(osem[1]))]
    c.ubuf_guard = list(c.woT_stage_done)
    c.woT_g = r.val("sv")
    c.stg_gate = ("st", r.val("st"))


def emit_wo_partial(c, zloc, woT_dram, zloc_ready, L, gi_gate):
    r, em, ev, dma = c.r, c.em, c.ev, c.dma
    AR = c.arena
    ga = c.gsl(L, gi_gate)
    r.wait("y", "sc", r.val("sc"))
    r.wait("y", "st", r.val("st"))
    r.wait("y", "sv", r.val("sv"))
    if not hasattr(c, "wop_zl"):
        c.wop_zl = r.fresh("zl")
        c.wop_wl = r.fresh("wl")
        c.wop_ps = [r.fresh("pp"), r.fresh("pp")]
        c.wop_rs = r.fresh("rs")
        c.wop_gate = None
    zl = c.wop_zl
    r.wait("y", zloc_ready[0], zloc_ready[1])
    if c.wop_gate is not None:
        r.wait("y", c.wop_gate[0], c.wop_gate[1])
    for fchunk in range(2):
        for tg in range(8):
            dma("y", zl, AR[:, fchunk * 8192 + tg * 1024: fchunk * 8192 + (tg + 1) * 1024],
                zloc.ap()[tg * 1024:(tg + 1) * 1024, fchunk * 128:(fchunk + 1) * 128],
                transpose=True)
    r.wait("t", zl, r.val(zl))
    woT = AR[:, 16384:20480].rearrange("p (a f) -> p a f", a=2)
    wl = c.wop_wl
    for tkn in c.woT_stage_done:
        r.wait("y", tkn[0], tkn[1])
    dma("y", wl, woT, woT_dram.ap().rearrange("(a p) f -> p a f", p=128))
    r.wait("t", wl, r.val(wl))
    r.wait("t", "sv", r.val("sv"))
    r.wait("t", "sc", r.val("sc"))
    psem = c.wop_ps
    if c.ubuf_guard:
        for _tk in c.ubuf_guard:
            r.wait("s", _tk[0], _tk[1])
    ev_toks = {}
    st_toks = {}
    for tt in range(64):
        for n in range(4):
            psx = c.psA[:, ((tt * 4 + n) % 2) * 1024 + (n % 2) * 512:
                        ((tt * 4 + n) % 2) * 1024 + (n % 2 + 1) * 512]
            idx = tt * 4 + n
            if idx >= 2:
                r.wait("t", "sc", ev_toks[idx - 2])
            for k in range(2):
                em("t", lambda e, tt=tt, n=n, k=k, psx=psx, woT=woT: e.matmul(
                    psx, AR[:, k * 8192 + tt * 128:k * 8192 + (tt + 1) * 128],
                    woT[:, k, n * 512:(n + 1) * 512], start=(k == 0), stop=(k == 1)))
            st_toks[idx] = r.mark("t", "st")
            r.wait("s", "st", st_toks[idx])
            if idx >= 2:
                r.wait("s", psem[(idx % 2)], part_dma[idx - 2])
            ev_toks[idx] = ev("s", "sc", lambda e, tt=tt, n=n, psx=psx: e.activation(
                c.ubuf[:, n % 2, :].bitcast(BF16)[:, 0:512], psx, ACT.Copy))
            r.wait("y", "sc", ev_toks[idx])
            if idx == 0:
                part_dma = {}
            part_dma[idx] = dma("y", psem[idx % 2],
                c.part.ap()[tt * 128:(tt + 1) * 128, n * 512:(n + 1) * 512],
                c.ubuf[:, n % 2, :].bitcast(BF16)[:, 0:512])
    r.wait("g", psem[0], r.val(psem[0]))
    r.wait("g", psem[1], r.val(psem[1]))
    cctok = c.cc(lambda e: e.collective_compute(
        "ReduceScatter", ALU.add, replica_groups=c.RG,
        ins=[c.part.ap().opt()], outs=[c.rsout.ap().opt()]))
    c.waitcc("y", cctok)
    rsem = c.wop_rs
    for k in range(DK):
        if k >= 1:
            r.wait("y", "sv", rs_toks[k - 1])
        dma("y", rsem, c.lnt[:, 0:2, :].rearrange("p a b -> p (a b)"),
            c.rsout.ap()[:, k * 128:(k + 1) * 128], transpose=True)
        r.wait("v", rsem, r.val(rsem))
        if k == 0:
            rs_toks = {}
        rs_toks[k] = ev("v", "sv", lambda e, k=k, ga=ga: e.scalar_tensor_tensor(
            c.xres[:, k * TC:(k + 1) * TC], c.lnt[:, 0:2, :].rearrange("p a b -> p (a b)"),
            ga[:, k:k + 1], c.xres[:, k * TC:(k + 1) * TC], ALU.mult, ALU.add))
    c.wo_done = r.val("sv")
    c.ubuf_guard = [("sc", r.val("sc"))]
    c.wop_gate = ("sv", r.val("sv"))


def emit_phase_moe(c):
    r, em, ev, dma = c.r, c.em, c.ev, c.dma
    AR = c.arena
    ln2_done = emit_ln(c, 0, 3, 4, c.hbuf, after=[("sv", c.wo_done)])

    wgsem = r.fresh("wg", "gdma")
    if c.stg_gate is not None:
        r.wait("g", c.stg_gate[0], c.stg_gate[1])
    dma("g", wgsem, c.stg[0:8, 0, :], c.wg_in[:, 0:1024])
    dma("g", wgsem, c.stg[0:8, 1, :], c.wg_in[:, 1024:2048])
    r.wait("t", "sv", r.val("sv"))
    r.wait("t", "sc", r.val("sc"))
    r.wait("t", wgsem, r.val(wgsem))
    psDbf = c.psD.ap().bitcast(BF16)
    for k in range(DK):
        half, off = divmod(k, 8)
        em("t", lambda e, half=half, off=off, k=k, psDbf=psDbf: e.transpose(
            psDbf[:, k * 8:(k + 1) * 8], c.stg[0:8, half, off * 128:(off + 1) * 128], c.idb[0:8, 0:8]))
    tok = r.mark("t", "st")
    r.wait("v", "st", tok)
    vv = ev("v", "sv", lambda e, psDbf=psDbf: e.tensor_copy(c.wgT[:, :], psDbf[:, 0:128]))
    r.wait("t", "sv", vv)
    c.stg_gate = ("st", r.val("st"))

    r.wait("t", "sc", ln2_done)
    if c.ubuf_guard:
        for _tk in c.ubuf_guard:
            r.wait("v", _tk[0], _tk[1])
    for n in range(2):
        ps = c.psB if n == 0 else c.psC
        for k in range(DK):
            em("t", lambda e, n=n, k=k, ps=ps: e.matmul(
                ps[0:8, :], c.wgT[:, k * 8:(k + 1) * 8],
                c.hbuf[:, k * TC + n * 512: k * TC + (n + 1) * 512],
                start=(k == 0), stop=(k == DK - 1)))
    tok = r.mark("t", "st")
    r.wait("v", "st", tok)
    em("v", lambda e: e.tensor_copy(c.ubuf[0:8, 0, :], c.psB[0:8, :]))
    vv = ev("v", "sv", lambda e: e.tensor_copy(c.ubuf[0:8, 1, :], c.psC[0:8, :]))
    r.wait("t", "sv", vv)
    for cch in range(8):
        half, off = divmod(cch, 4)
        em("t", lambda e, half=half, off=off, cch=cch: e.transpose(
            c.psD[:, cch * 8:(cch + 1) * 8], c.ubuf[0:8, half, off * 128:(off + 1) * 128],
            c.idf[0:8, 0:8]))
    tok = r.mark("t", "st")
    r.wait("v", "st", tok)
    vv = ev("v", "sv", lambda e: e.tensor_copy(c.smx[:, 0:64], c.psD[:, 0:64]))
    r.wait("t", "sv", vv)
    em("v", lambda e: e.drain())

    s3 = c.smx[:, 0:64].rearrange("p (c e) -> p c e", e=8)
    m1 = c.ubuf[:, 0, 0:8]
    b2 = c.ubuf[:, 0, 8:16]
    t64 = c.ubuf[:, 1, 0:64]
    t64r = t64.rearrange("p (c e) -> p c e", e=8)
    d64 = c.ubuf[:, 0, 64:128]
    em("v", lambda e: e.reduce_max(m1, s3, axis=AXX))
    em("v", lambda e: e.drain())
    em("v", lambda e: e.tensor_tensor(s3, s3, m1.rearrange("p c -> p c ()").broadcast_to([128, 8, 8]), ALU.subtract))
    vv = ev("v", "sv", lambda e: e.drain())
    r.wait("s", "sv", vv)
    vs = ev("s", "sc", lambda e: e.activation(d64, c.smx[:, 0:64], ACT.Relu, bias=1.0, scale=1e30))
    r.wait("v", "sc", vs)
    em("v", lambda e: e.tensor_scalar_min(d64, d64, 1.0))
    em("v", lambda e: e.drain())
    em("v", lambda e: e.scalar_tensor_tensor(t64, d64, -1e4, c.smx[:, 0:64], ALU.mult, ALU.add))
    em("v", lambda e: e.drain())
    em("v", lambda e: e.reduce_max(b2, t64r, axis=AXX))
    em("v", lambda e: e.drain())
    em("v", lambda e: e.tensor_tensor(t64r, s3, b2.rearrange("p c -> p c ()").broadcast_to([128, 8, 8]), ALU.subtract))
    vv = ev("v", "sv", lambda e: e.drain())
    r.wait("s", "sv", vv)
    vs = ev("s", "sc", lambda e: e.activation(t64, t64, ACT.Relu, bias=1.0, scale=1e30))
    em("s", lambda e: e.activation(c.smx[:, 0:64], c.smx[:, 0:64], ACT.Exp))
    vs = ev("s", "sc", lambda e: e.activation(b2, b2, ACT.Exp))
    r.wait("v", "sc", vs)
    em("v", lambda e: e.tensor_scalar_min(t64, t64, 1.0))
    em("v", lambda e: e.drain())
    em("v", lambda e: e.tensor_scalar_add(b2, b2, 1.0))
    em("v", lambda e: e.drain())
    em("v", lambda e: e.reciprocal(b2, b2))
    em("v", lambda e: e.drain())
    em("v", lambda e: e.tensor_mul(c.wmoe[:, :], t64, c.smx[:, 0:64]))
    em("v", lambda e: e.drain())
    em("v", lambda e: e.tensor_tensor(c.wmoe.ap().rearrange("p (cc ee) -> p cc ee", ee=8),
                                      c.wmoe.ap().rearrange("p (cc ee) -> p cc ee", ee=8),
                                      b2.rearrange("p c -> p c ()").broadcast_to([128, 8, 8]), ALU.mult))
    w_done = ev("v", "sv", lambda e: e.drain())
    c.ubuf_guard = [("sv", w_done)]
    if DEBUG and "dbg_w" in c.dbg:
        wds = r.fresh("wd")
        r.wait("y", "sv", w_done)
        dma("y", wds, c.dbg["dbg_w"].ap(), c.wmoe[:, :])
        r.wait("y", wds, r.val(wds))

    P = c.wqkvT.ap().rearrange("p k f -> p (k f)")
    Pv = P[:, 0:11264].rearrange("p (h t) -> p h t", t=1024)
    YB = AR[:, 0:16384].rearrange("p (tt f) -> p tt f", f=2048)
    msem = [r.fresh("mo"), r.fresh("mo"), r.fresh("mo"), r.fresh("mo")]
    w3sem = [r.fresh("w3"), r.fresh("w3")]
    r.wait("y", "st", r.val("st"))
    r.wait("y", "sc", r.val("sc"))
    r.wait("y", "sv", r.val("sv"))
    slabs = [(i, 1) for i in range(HMC)]
    pe_done = {}
    w3_done = {}
    for e_ in range(NE):
        c.waitcc("y", c.cc_moe[e_][0])
        c.waitcc("y", c.cc_moe[e_][1])
        r.release(c.cc_moe[e_][0][0]); r.release(c.cc_moe[e_][1][0])
        for si, (h0, hn) in enumerate(slabs):
            sidx = e_ * HMC + si
            bpar = sidx % 2
            if si < 2 and e_ > 0:
                r.wait("y", "st", w3_done[e_ * 8 - 1])
            if sidx >= 2:
                r.wait("y", "st", pe_done[sidx - 2])
            dma("y", msem[bpar * 2], AR[:, 16384 + bpar * 2048: 16384 + (bpar + 1) * 2048].rearrange(
                "p (k h) -> p k h", k=DK),
                c.ago_m1[e_].ap().rearrange("(k p) h -> p k h", p=128)[:, :, h0 * 128:(h0 + hn) * 128])
            dma("y", msem[bpar * 2 + 1],
                AR[:, 20480 + bpar * 2048: 20480 + (bpar + 1) * 2048].rearrange(
                "p (k h) -> p k h", k=DK),
                c.ago_m2[e_].ap().rearrange("(k p) h -> p k h", p=128)[:, :, h0 * 128:(h0 + hn) * 128])
            r.wait("t", msem[bpar * 2], r.val(msem[bpar * 2]))
            r.wait("t", msem[bpar * 2 + 1], r.val(msem[bpar * 2 + 1]))
            for hh in range(hn):
                hc = h0 + hh
                for n in range(2):
                    pgrp = (hc * 2 + n) % 2
                    ps1 = c.psB if pgrp == 0 else c.psD
                    ps2 = c.psC if pgrp == 0 else c.psE
                    if True:
                        r.wait("t", "sc", r.val("sc")) if False else None
                    for k in range(DK):
                        em("t", lambda e, bpar=bpar, k=k, n=n, ps1=ps1: e.matmul(
                            ps1[:, :],
                            AR[:, 16384 + bpar * 2048:16384 + (bpar + 1) * 2048].rearrange(
                                "p (kk h) -> p kk h", kk=DK)[:, k, :],
                            c.hbuf[:, k * TC + n * 512: k * TC + (n + 1) * 512],
                            start=(k == 0), stop=(k == DK - 1)))
                    for k in range(DK):
                        em("t", lambda e, bpar=bpar, k=k, n=n, ps2=ps2: e.matmul(
                            ps2[:, :],
                            AR[:, 20480 + bpar * 2048:20480 + (bpar + 1) * 2048].rearrange(
                                "p (kk h) -> p kk h", kk=DK)[:, k, :],
                            c.hbuf[:, k * TC + n * 512: k * TC + (n + 1) * 512],
                            start=(k == 0), stop=(k == DK - 1)))
                    mmk = r.mark("t", "st")
                    r.wait("s", "st", mmk)
                    vs = ev("s", "sc", lambda e, pgrp=pgrp, ps1=ps1: e.activation(
                        c.lnq[:, pgrp, :], ps1[:, :], ACT.Silu))
                    r.wait("v", "sc", vs)
                    pvv = ev("v", "sv", lambda e, pgrp=pgrp, hc=hc, n=n, ps2=ps2: e.tensor_tensor(
                        Pv[:, hc, n * 512:(n + 1) * 512], c.lnq[:, pgrp, :], ps2[:, :], ALU.mult))
                    r.wait("t", "sv", pvv)
            pe_done[sidx] = r.val("st")
        c.waitcc("y", c.cc_moe[e_][2])
        r.release(c.cc_moe[e_][2][0])
        for nh in range(8):
            widx = e_ * 8 + nh
            wpar = widx % 2
            if nh == 0:
                r.wait("y", "st", pe_done[e_ * HMC + HMC - 1])
            if widx >= 2:
                r.wait("y", "st", w3_done[widx - 2])
            dma("y", w3sem[wpar], AR[:, 16384 + wpar * 2816: 16384 + (wpar + 1) * 2816].rearrange(
                "p (k d) -> p k d", k=HMC),
                c.ago_m3[e_].ap().rearrange("(k p) d -> p k d", p=128)[:, :, nh * 256:(nh + 1) * 256])
            r.wait("t", w3sem[wpar], r.val(w3sem[wpar]))
            for mg in range(2):
                for mm_ in range(4):
                    tt = mg * 4 + mm_
                    psx = c.psA[:, mm_ * 512: mm_ * 512 + 256]
                    for k in range(HMC):
                        em("t", lambda e, wpar=wpar, k=k, tt=tt, psx=psx: e.matmul(
                            psx, Pv[:, k, tt * 128:(tt + 1) * 128],
                            AR[:, 16384 + wpar * 2816:16384 + (wpar + 1) * 2816].rearrange(
                                "p (kk d) -> p kk d", kk=HMC)[:, k, :],
                            start=(k == 0), stop=(k == HMC - 1)))
                    mmk = r.mark("t", "st")
                    r.wait("v", "st", mmk)
                    wcol = c.wmoe[:, tt * 8 + e_: tt * 8 + e_ + 1]
                    if e_ == 0:
                        lv = ev("v", "sv", lambda e, tt=tt, nh=nh, psx=psx, wcol=wcol: e.tensor_scalar(
                            YB[:, tt, nh * 256:(nh + 1) * 256], psx, wcol, None, ALU.mult))
                    else:
                        lv = ev("v", "sv", lambda e, tt=tt, nh=nh, psx=psx, wcol=wcol: e.scalar_tensor_tensor(
                            YB[:, tt, nh * 256:(nh + 1) * 256], psx, wcol,
                            YB[:, tt, nh * 256:(nh + 1) * 256], ALU.mult, ALU.add))
                    r.wait("t", "sv", lv)
            w3_done[widx] = r.val("st")
    gb = c.gsl(0, 5)
    ydone = r.val("sv")
    r.wait("t", "sv", ydone)
    psDbf = c.psD.ap().bitcast(BF16)
    psEbf = c.psE.ap().bitcast(BF16)
    for dc in range(DK):
        for grp in range(2):
            pb = psDbf if grp == 0 else psEbf
            for j in range(4):
                tt = grp * 4 + j
                em("t", lambda e, dc=dc, tt=tt, j=j, pb=pb: e.transpose(
                    pb[:, j * 128:(j + 1) * 128], YB[:, tt, dc * 128:(dc + 1) * 128], c.idb[:, :]))
            tok = r.mark("t", "st")
            r.wait("v", "st", tok)
            vv = ev("v", "sv", lambda e, dc=dc, grp=grp, pb=pb, gb=gb: e.scalar_tensor_tensor(
                c.xres[:, dc * TC + grp * 512: dc * TC + (grp + 1) * 512], pb[:, 0:512],
                gb[:, dc:dc + 1], c.xres[:, dc * TC + grp * 512: dc * TC + (grp + 1) * 512],
                ALU.mult, ALU.add))
            r.wait("t", "sv", vv)
    c.moe_done = r.val("sv")


def emit_phase_l1attn(c):
    r, em, ev, dma = c.r, c.em, c.ev, c.dma
    AR = c.arena
    ln3_done = emit_ln(c, 1, 0, 1, c.hbuf, after=[("sv", c.moe_done)])
    r.wait("g", "sc", ln3_done)
    r.wait("g", "sg", r.val("sg"))
    for k in range(DK):
        dma("g", "sg", c.agi_h[ts(k), :], c.hbuf[:, k * TC:(k + 1) * TC])
    r.wait("g", "sg", r.val("sg"))
    c.cc_h1 = c.cc(lambda e: e.collective_compute(
        "AllGather", ALU.bypass, replica_groups=c.RG,
        ins=[c.agi_h.ap().opt()], outs=[c.ago_h.ap().opt()]))

    emit_stage_qkvT(c, c.l1qkv, 768)
    emit_stage_wilr(c)

    emit_stage_woT(c, c.l1wo_s, c.wo1T)

    if SUBPH >= 2:
        emit_fp8_stage(c)

    if SUBPH >= 3:
        emit_l1_scan(c)

    if SUBPH >= 4:
        emit_wo_partial(c, c.zloc1, c.wo1T, c.scan_done, L=1, gi_gate=2)


def emit_fp8_stage(c):
    r, em, ev, dma = c.r, c.em, c.ev, c.dma
    AR8 = c.arena.ap().bitcast(FP8)
    fsem = [r.fresh("f8", "gdma"), r.fresh("f8", "gdma")]
    osem = r.fresh("f8o")
    r.wait("t", "sv", r.val("sv"))
    r.wait("t", "sc", r.val("sc"))
    r.wait("v", "sv", c.rw_done)
    ev_t = {}
    for rt in range(64):
        buf = rt % 2
        if rt >= 2:
            r.wait("g", "st", ev_t[rt - 2])
        dma("g", fsem[buf], c.stgf[:, buf, 0:256], c.wfc_s[ts(rt), :])
        r.wait("t", fsem[buf], r.val(fsem[buf]))
        if rt >= 2:
            r.wait("t", "sv", q_done[rt - 2])
        ps = c.psD if buf == 0 else c.psE
        for ci in range(2):
            em("t", lambda e, buf=buf, ci=ci, ps=ps: e.transpose(
                ps[:, ci * 128:(ci + 1) * 128], c.stgf[:, buf, ci * 128:(ci + 1) * 128], c.idf[:, :]))
        ev_t[rt] = r.mark("t", "st")
        r.wait("v", "st", ev_t[rt])
        for ci in range(2):
            vv = ev("v", "sv", lambda e, rt=rt, ci=ci, ps=ps: e.tensor_scalar(
                AR8[:, ci * 8192 + rt * 128: ci * 8192 + (rt + 1) * 128],
                ps[:, ci * 128:(ci + 1) * 128], c.smx2[:, 5:6], None, ALU.mult))
        if rt == 0:
            q_done = {}
        q_done[rt] = r.val("sv")
    r.wait("g", "sv", r.val("sv"))
    r.wait("g", "sg", r.val("sg"))
    for ci in range(2):
        dma("g", "sg", c.agi_fc8.ap()[ci * 128:(ci + 1) * 128, :], AR8[:, ci * 8192:(ci + 1) * 8192])
    r.wait("g", "sg", r.val("sg"))
    c.cc_fc8 = c.cc(lambda e: e.collective_compute(
        "AllGather", ALU.bypass, replica_groups=c.RG,
        ins=[c.agi_fc8.ap().opt()], outs=[c.ago_fc8.ap().opt()]))
    ev_t2 = {}
    q2 = {}
    for rt in range(16):
        buf = rt % 2
        if rt >= 2:
            r.wait("g", "st", ev_t2[rt - 2])
        dma("g", fsem[buf], c.stgf[:, buf, 0:1024], c.wproj_s[ts(rt), :])
        r.wait("t", fsem[buf], r.val(fsem[buf]))
        if rt >= 2:
            r.wait("t", "sv", q2[rt - 2])
        ps = c.psD if buf == 0 else c.psE
        for ci in range(8):
            if ci == 4:
                tk = r.mark("t", "st")
                r.wait("v", "st", tk)
                for cj in range(4):
                    vv = ev("v", "sv", lambda e, rt=rt, cj=cj, ps=ps: e.tensor_scalar(
                        AR8[:, cj * 2048 + rt * 128: cj * 2048 + (rt + 1) * 128],
                        ps[:, cj * 128:(cj + 1) * 128], c.smx2[:, 6:7], None, ALU.mult))
                r.wait("t", "sv", r.val("sv"))
            em("t", lambda e, buf=buf, ci=ci, ps=ps: e.transpose(
                ps[:, (ci % 4) * 128:(ci % 4 + 1) * 128], c.stgf[:, buf, ci * 128:(ci + 1) * 128], c.idf[:, :]))
        ev_t2[rt] = r.mark("t", "st")
        r.wait("v", "st", ev_t2[rt])
        for cj in range(4):
            vv = ev("v", "sv", lambda e, rt=rt, cj=cj, ps=ps: e.tensor_scalar(
                AR8[:, (4 + cj) * 2048 + rt * 128: (4 + cj) * 2048 + (rt + 1) * 128],
                ps[:, cj * 128:(cj + 1) * 128], c.smx2[:, 6:7], None, ALU.mult))
        q2[rt] = r.val("sv")
    r.wait("g", "sv", r.val("sv"))
    r.wait("g", "sg", r.val("sg"))
    for ci in range(8):
        dma("g", "sg", c.agi_pj8.ap()[ci * 128:(ci + 1) * 128, :], AR8[:, ci * 2048:(ci + 1) * 2048])
    r.wait("g", "sg", r.val("sg"))
    c.cc_pj8 = c.cc(lambda e: e.collective_compute(
        "AllGather", ALU.bypass, replica_groups=c.RG,
        ins=[c.agi_pj8.ap().opt()], outs=[c.ago_pj8.ap().opt()]))
def emit_l1_scan(c):
    r, em, ev, dma = c.r, c.em, c.ev, c.dma
    AR = c.arena
    ln3_sc = None
    SBB = [0, 3264]
    WS = 6528
    c.WS = WS
    HTB = WS + 4928
    arf = AR.ap().bitcast(F32)
    c.wst = arf[:, (WS + 5952) // 2:(WS + 5952) // 2 + 128]
    c.wsb2 = AR[:, WS + 6336: WS + 6592]
    wsem = r.fresh("w1l", "gdma")
    for h in range(4):
        dma("g", wsem, c.wst[(h % 2) * 64:(h % 2) * 64 + 64, (h // 2) * 64:(h // 2) * 64 + 64],
            c.w1h_in[h * 64:(h + 1) * 64, :])
    r.wait("v", wsem, r.val(wsem))
    if c.ubuf_guard:
        for _tk in c.ubuf_guard:
            r.wait("v", _tk[0], _tk[1])
    em("v", lambda e: e.memset(c.wsb2, 0.0))
    em("v", lambda e: e.drain())
    for h in range(4):
        em("v", lambda e, h=h: e.tensor_copy(
            c.wsb2[(h % 2) * 64:(h % 2) * 64 + 64,
                   (h // 2) * 128 + (h % 2) * 64:(h // 2) * 128 + (h % 2) * 64 + 64],
            c.wst[(h % 2) * 64:(h % 2) * 64 + 64, (h // 2) * 64:(h // 2) * 64 + 64]))
    wb_ready = ev("v", "sv", lambda e: e.drain())

    ssem = [r.fresh("sp"), r.fresh("sp")]
    zsem1 = r.fresh("zs1")
    c.zprev_scan = 0
    proj_done = {}

    def proj_strip(s):
        base = SBB[s % 2]
        r8, half = divmod(s, 2)
        r.wait("t", "sv", r.val("sv"))
        r.wait("t", "sc", r.val("sc"))
        ld_tok = {}
        mmk_tok = {}
        for k in range(DK):
            gk = s * DK + k
            hpar = gk % 2
            if k >= 2:
                r.wait("y", "st", mmk_tok[k - 2])
            elif s > 0:
                r.wait("y", "st", proj_done[s - 1])
            dma("y", ssem[hpar], AR[:, HTB + hpar * 512: HTB + (hpar + 1) * 512],
                c.ago_h.ap()[r8 * D + k * 128: r8 * D + (k + 1) * 128,
                             half * 512:(half + 1) * 512])
            r.wait("t", ssem[hpar], r.val(ssem[hpar]))
            ht = AR[:, HTB + hpar * 512: HTB + (hpar + 1) * 512]
            for mat, coloff, aroff in (("b", 256, 0), ("c", 0, 1024)):
                for g in range(2):
                    em("t", lambda e, k=k, g=g, coloff=coloff, aroff=aroff, ht=ht: e.matmul(
                        c.psA[:, aroff + g * 512: aroff + (g + 1) * 512],
                        c.wqkvT[:, k, coloff + g * 128: coloff + (g + 1) * 128],
                        ht, start=(k == 0), stop=(k == DK - 1)))
            psavv = [c.psB, c.psC, c.psD, c.psE]
            for sc_ in range(4):
                em("t", lambda e, k=k, sc_=sc_, ht=ht, psavv=psavv: e.matmul(
                    psavv[sc_][:, 0:260], ht[:, sc_ * 128:(sc_ + 1) * 128],
                    c.wqkvT[:, k, 512:772], start=(k == 0), stop=(k == DK - 1)))
            mmk_tok[k] = r.mark("t", "st")
        t_ = mmk_tok[DK - 1]
        r.wait("s", "st", t_)
        r.wait("v", "st", t_)
        for g in range(2):
            em("s", lambda e, g=g, base=base: e.activation(
                AR[:, base + g * 512: base + (g + 1) * 512],
                c.psA[:, g * 512:(g + 1) * 512], ACT.Copy))
            em("s", lambda e, g=g, base=base: e.activation(
                AR[:, base + 1024 + g * 512: base + 1024 + (g + 1) * 512],
                c.psA[:, 1024 + g * 512: 1024 + (g + 1) * 512], ACT.Copy))
        psavv = [c.psB, c.psC, c.psD, c.psE]
        for sc_ in range(4):
            em("s", lambda e, sc_=sc_, base=base, psavv=psavv: e.activation(
                AR[:, base + 2048 + sc_ * 256: base + 2048 + (sc_ + 1) * 256],
                psavv[sc_][:, 0:256], ACT.Copy))
            em("v", lambda e, sc_=sc_, psavv=psavv: e.tensor_tensor(
                c.ubuf[:, 1, sc_ * 4:(sc_ + 1) * 4], psavv[sc_][:, 256:260],
                c.smx2[:, 12:16], ALU.add))
        a_sc = r.mark("s", "sc")
        vv = ev("v", "sv", lambda e: e.drain())
        r.wait("s", "sv", vv)
        vs = ev("s", "sc", lambda e: e.activation(c.ubuf[:, 1, 0:16], c.ubuf[:, 1, 0:16], ACT.Sigmoid))
        r.wait("v", "sc", vs)
        em("v", lambda e: e.tensor_tensor(
            c.ubuf[:, 1, 0:16].rearrange("p (s f) -> p s f", f=4),
            c.ubuf[:, 1, 0:16].rearrange("p (s f) -> p s f", f=4),
            c.tkx.rearrange("p o -> p () o").broadcast_to([128, 4, 4]), ALU.mult))
        em("v", lambda e: e.drain())
        cof = AR.ap().bitcast(F32)
        cov = ev("v", "sv", lambda e, base=base, cof=cof: e.tensor_copy(
            cof[:, (base + 3072) // 2:(base + 3072) // 2 + 16], c.ubuf[:, 1, 0:16]))
        proj_done[s] = r.val("st")
        return cov

    r.wait("t", "sv", r.val("sv"))
    r.wait("t", "sc", r.val("sc"))
    r.wait("y", "st", r.val("st"))
    r.wait("y", "sc", r.val("sc"))
    r.wait("y", "sv", r.val("sv"))
    c.waitcc("y", c.cc_h1)
    r.wait("t", "sv", c.qkvT_done)
    r.wait("t", "sv", c.bilr_done)
    r.wait("t", "sv", wb_ready)
    cov = proj_strip(0)
    for s in range(16):
        cov_next = proj_strip(s + 1) if s < 15 else None
        base = SBB[s % 2]
        r.wait("v", "sv", cov)
        r.wait("t", "sv", cov)
        r.wait("t", "sc", r.val("sc"))
        for sc_ in range(4):
            emit_scan_sc(c, s, sc_, base, zsem1)
        cov = cov_next
    c.scan_done = (zsem1, c.zprev_scan)


def emit_scan_sc(c, s, sc_, base, zsem1):
    r, em, ev, dma = c.r, c.em, c.ev, c.dma
    AR = c.arena
    WS = c.WS
    LP, K1o, GUP, GBCN, DTT = WS, WS + 512, WS + 1024, WS + 1536, WS + 2048
    RPo, T1o, T2o, VVo, Z1o = WS + 2560, WS + 2816, WS + 3072, WS + 3328, WS + 3584
    BTMo, L2o, ZBo = WS + 3840, WS + 4096, WS + 4608
    sci = s * 4 + sc_
    if SCSTEP < 1:
        return
    cof = AR.ap().bitcast(F32)
    co4 = cof[:, (base + 3072) // 2 + sc_ * 4:(base + 3072) // 2 + (sc_ + 1) * 4]
    co4b = co4.rearrange("p f -> p f ()").broadcast_to([128, 4, 128])
    co4b64 = co4.rearrange("p f -> p f ()").broadcast_to([128, 4, 64])
    mBLb = c.mBL.ap().rearrange("p s2 -> p () s2").broadcast_to([128, 4, 128])
    mBUb = c.mBUn.ap().rearrange("p s2 -> p () s2").broadcast_to([128, 4, 128])
    mBDb = c.mBD.ap().rearrange("p s2 -> p () s2").broadcast_to([128, 4, 128])

    def bsl(h):
        g = h // 2
        return AR[(h % 2) * 64:(h % 2) * 64 + 64,
                  base + g * 512 + sc_ * 128: base + g * 512 + (sc_ + 1) * 128]

    def csl(h):
        g = h // 2
        return AR[(h % 2) * 64:(h % 2) * 64 + 64,
                  base + 1024 + g * 512 + sc_ * 128: base + 1024 + g * 512 + (sc_ + 1) * 128]

    def wsb2col(h):
        return c.wsb2[:, (h // 2) * 128 + (h % 2) * 64:(h // 2) * 128 + (h % 2) * 64 + 64]

    def pairsl(g):
        return AR[:, base + g * 512 + sc_ * 128: base + g * 512 + (sc_ + 1) * 128]

    def cpairsl(g):
        return AR[:, base + 1024 + g * 512 + sc_ * 128: base + 1024 + g * 512 + (sc_ + 1) * 128]

    for h in range(4):
        em("t", lambda e, h=h: e.matmul(c.psD[:, h * 64:(h + 1) * 64], pairsl(h // 2), wsb2col(h),
                                        start=True, stop=True))
    for h in (0, 2):
        em("t", lambda e, h=h: e.matmul(c.psB[:, h * 128:(h + 1) * 128], bsl(h), bsl(h),
                                        start=True, stop=True))
        em("t", lambda e, h=h: e.matmul(c.psC[:, h * 128:(h + 1) * 128], bsl(h), csl(h),
                                        start=True, stop=True))
    swk = r.mark("t", "st")
    r.wait("t", "st", swk)
    for h in (1, 3):
        em("t", lambda e, h=h: e.matmul(c.psB[:, h * 128:(h + 1) * 128], bsl(h), bsl(h),
                                        start=True, stop=True))
        em("t", lambda e, h=h: e.matmul(c.psC[:, h * 128:(h + 1) * 128], bsl(h), csl(h),
                                        start=True, stop=True))
    g1 = r.mark("t", "st")
    if SCSTEP < 2:
        return
    r.wait("v", "st", g1)
    psBr = c.psB.ap().rearrange("p (h t2) -> p h t2", t2=128)
    psCr = c.psC.ap().rearrange("p (h t2) -> p h t2", t2=128)
    LPr = AR[:, LP:LP + 512].rearrange("p (h t2) -> p h t2", t2=128)
    em("v", lambda e, LPr=LPr, psBr=psBr, co4b=co4b: e.tensor_tensor(LPr, psBr, co4b, ALU.mult))
    em("v", lambda e: e.drain())
    lp_done = ev("v", "sv", lambda e, LPr=LPr, mBLb=mBLb: e.tensor_tensor(LPr, LPr, mBLb, ALU.mult))
    em("v", lambda e, psBr=psBr, mBUb=mBUb: e.tensor_tensor(
        AR[:, GUP:GUP + 512].rearrange("p (h t2) -> p h t2", t2=128), psBr, mBUb, ALU.mult))
    em("v", lambda e, psCr=psCr, mBUb=mBUb: e.tensor_tensor(
        AR[:, GBCN:GBCN + 512].rearrange("p (h t2) -> p h t2", t2=128), psCr, mBUb, ALU.mult))
    gup_done = ev("v", "sv", lambda e, psCr=psCr, mBDb=mBDb: e.tensor_tensor(
        AR[:, DTT:DTT + 512].rearrange("p (h t2) -> p h t2", t2=128), psCr, mBDb, ALU.mult))
    em("v", lambda e, base=base: e.tensor_sub(
        c.ubuf[:, 0, 0:256], c.psD[:, 0:256],
        AR[:, base + 2048 + sc_ * 256: base + 2048 + (sc_ + 1) * 256]))
    rr_done = ev("v", "sv", lambda e: e.drain())
    em("v", lambda e, co4b64=co4b64: e.tensor_tensor(
        AR[:, RPo:RPo + 256].rearrange("p (h t2) -> p h t2", t2=64),
        c.ubuf[:, 0, 0:256].rearrange("p (h t2) -> p h t2", t2=64), co4b64, ALU.mult))
    em("v", lambda e, co4=co4: e.tensor_scalar_mul(c.ubuf[:, 0, 260:264], co4, -1.0))
    rp_done = ev("v", "sv", lambda e: e.drain())
    if SCSTEP < 3:
        return
    r.wait("t", "sv", lp_done)
    psEbf = c.psE.ap().bitcast(BF16)
    for h in range(4):
        em("t", lambda e, h=h, psEbf=psEbf: e.transpose(
            psEbf[:, h * 128:(h + 1) * 128], AR[:, LP + h * 128: LP + (h + 1) * 128], c.idb[:, :]))
    k1t = r.mark("t", "st")
    r.wait("v", "st", k1t)
    k1_done = ev("v", "sv", lambda e, psEbf=psEbf: e.tensor_copy(AR[:, K1o:K1o + 512], psEbf[:, 0:512]))
    if SCSTEP < 4:
        return
    r.wait("t", "sv", k1_done)
    for h in range(4):
        em("t", lambda e, h=h: e.matmul(c.psE[:, h * 128:(h + 1) * 128],
                                        AR[:, K1o + h * 128: K1o + (h + 1) * 128],
                                        AR[:, LP + h * 128: LP + (h + 1) * 128],
                                        start=True, stop=True))
    l2t = r.mark("t", "st")
    r.wait("v", "st", l2t)
    l2_done = ev("v", "sv", lambda e: e.tensor_copy(AR[:, L2o:L2o + 512], c.psE[:, 0:512]))
    r.wait("t", "sv", l2_done)
    for h in range(4):
        em("t", lambda e, h=h: e.matmul(c.psE[:, h * 128:(h + 1) * 128],
                                        AR[:, LP + h * 128: LP + (h + 1) * 128],
                                        AR[:, K1o + h * 128: K1o + (h + 1) * 128],
                                        start=True, stop=True))
    k2t = r.mark("t", "st")
    r.wait("v", "st", k2t)
    k2_done = ev("v", "sv", lambda e: e.tensor_copy(AR[:, LP:LP + 512], c.psE[:, 0:512]))
    r.wait("t", "sv", k2_done)
    for h in range(4):
        em("t", lambda e, h=h: e.matmul(c.psE[:, h * 128:(h + 1) * 128],
                                        AR[:, L2o + h * 128: L2o + (h + 1) * 128],
                                        AR[:, LP + h * 128: LP + (h + 1) * 128],
                                        start=True, stop=True))
    k4t = r.mark("t", "st")
    r.wait("v", "st", k4t)
    k4_done = ev("v", "sv", lambda e: e.tensor_copy(AR[:, L2o:L2o + 512], c.psE[:, 0:512]))
    if SCSTEP < 5:
        return
    r.wait("t", "sv", k4_done)
    r.wait("t", "sv", rp_done)
    for h in range(4):
        em("t", lambda e, h=h: e.matmul(c.psE[:, h * 64:(h + 1) * 64],
                                        AR[:, L2o + h * 128: L2o + (h + 1) * 128],
                                        AR[:, RPo + h * 64: RPo + (h + 1) * 64],
                                        start=True, stop=True))
    st1 = r.mark("t", "st")
    r.wait("v", "st", st1)
    t1_done = ev("v", "sv", lambda e: e.tensor_add(AR[:, T1o:T1o + 256], AR[:, RPo:RPo + 256],
                                                   c.psE[:, 0:256]))
    r.wait("t", "sv", t1_done)
    for h in range(4):
        em("t", lambda e, h=h: e.matmul(c.psE[:, h * 64:(h + 1) * 64],
                                        AR[:, LP + h * 128: LP + (h + 1) * 128],
                                        AR[:, T1o + h * 64: T1o + (h + 1) * 64],
                                        start=True, stop=True))
    st2 = r.mark("t", "st")
    r.wait("v", "st", st2)
    t2_done = ev("v", "sv", lambda e: e.tensor_add(AR[:, T2o:T2o + 256], AR[:, T1o:T1o + 256],
                                                   c.psE[:, 0:256]))
    r.wait("t", "sv", t2_done)
    for h in range(4):
        em("t", lambda e, h=h: e.matmul(c.psE[:, h * 64:(h + 1) * 64],
                                        AR[:, K1o + h * 128: K1o + (h + 1) * 128],
                                        AR[:, T2o + h * 64: T2o + (h + 1) * 64],
                                        start=True, stop=True))
    st3 = r.mark("t", "st")
    r.wait("v", "st", st3)
    v_done = ev("v", "sv", lambda e: e.tensor_sub(AR[:, VVo:VVo + 256], AR[:, T2o:T2o + 256],
                                                  c.psE[:, 0:256]))
    if SCSTEP < 6:
        return
    r.wait("t", "sv", v_done)
    r.wait("t", "sv", gup_done)
    for h in range(4):
        em("t", lambda e, h=h: e.matmul(c.psE[:, h * 64:(h + 1) * 64],
                                        AR[:, GUP + h * 128: GUP + (h + 1) * 128],
                                        AR[:, VVo + h * 64: VVo + (h + 1) * 64],
                                        start=True, stop=True))
    for h in range(4):
        em("t", lambda e, h=h: e.matmul(c.psB[:, h * 64:(h + 1) * 64], cpairsl(h // 2), wsb2col(h),
                                        start=True, stop=False))
        em("t", lambda e, h=h: e.matmul(c.psB[:, h * 64:(h + 1) * 64],
                                        AR[:, GBCN + h * 128: GBCN + (h + 1) * 128],
                                        AR[:, VVo + h * 64: VVo + (h + 1) * 64],
                                        start=False, stop=True))
    z1t = r.mark("t", "st")
    r.wait("v", "st", z1t)
    r.wait("v", "sv", rr_done)
    z1_done = ev("v", "sv", lambda e: e.tensor_add(AR[:, Z1o:Z1o + 256],
                                                   c.ubuf[:, 0, 0:256], c.psE[:, 0:256]))
    if SCSTEP < 7:
        return
    r.wait("t", "sv", z1_done)
    for h in range(4):
        em("t", lambda e, h=h: e.matmul(c.psC[:, h * 64:(h + 1) * 64],
                                        AR[:, DTT + h * 128: DTT + (h + 1) * 128],
                                        AR[:, Z1o + h * 64: Z1o + (h + 1) * 64],
                                        start=True, stop=True))
    psDbf = c.psD.ap().bitcast(BF16)
    for g in range(2):
        em("t", lambda e, g=g, psDbf=psDbf: e.transpose(
            psDbf[:, 512 + g * 128: 512 + (g + 1) * 128],
            AR[:, base + g * 512 + sc_ * 128: base + g * 512 + (sc_ + 1) * 128], c.idb[:, :]))
    dgt = r.mark("t", "st")
    r.wait("v", "st", dgt)
    if c.zprev_scan:
        r.wait("v", zsem1, c.zprev_scan)
    em("v", lambda e: e.tensor_tensor(
        AR[:, ZBo:ZBo + 256].rearrange("p (h t2) -> p h t2", t2=64),
        c.psC.ap().rearrange("p (h t2) -> p h t2", t2=128)[:, 0:4, 0:64],
        c.ubuf[:, 0, 260:264].rearrange("p f -> p f ()").broadcast_to([128, 4, 64]), ALU.mult))
    em("v", lambda e: e.drain())
    zb_done = ev("v", "sv", lambda e: e.tensor_add(AR[:, ZBo:ZBo + 256],
                                                   AR[:, ZBo:ZBo + 256], c.psB[:, 0:256]))
    btm_done = ev("v", "sv", lambda e, psDbf=psDbf: e.tensor_copy(AR[:, BTMo:BTMo + 256],
                                                                  psDbf[:, 512:768]))
    if SCSTEP < 8:
        return
    r.wait("t", "sv", btm_done)
    for h in range(4):
        em("t", lambda e, h=h: e.matmul(
            c.psE[(h % 2) * 64:(h % 2) * 64 + 64, 256 + (h // 2) * 64: 256 + (h // 2) * 64 + 64],
            AR[:, BTMo + (h // 2) * 128 + (h % 2) * 64: BTMo + (h // 2) * 128 + (h % 2) * 64 + 64],
            AR[:, VVo + h * 64: VVo + (h + 1) * 64],
            start=True, stop=True))
    wut = r.mark("t", "st")
    r.wait("v", "st", wut)
    em("v", lambda e: e.tensor_sub(c.wst, c.wst, c.psE[:, 256:384]))
    em("v", lambda e: e.drain())
    for h in range(3):
        em("v", lambda e, h=h: e.tensor_copy(
            c.wsb2[(h % 2) * 64:(h % 2) * 64 + 64,
                   (h // 2) * 128 + (h % 2) * 64:(h // 2) * 128 + (h % 2) * 64 + 64],
            c.wst[(h % 2) * 64:(h % 2) * 64 + 64, (h // 2) * 64:(h // 2) * 64 + 64]))
    wb_done = ev("v", "sv", lambda e: e.tensor_copy(
        c.wsb2[64:128, 192:256], c.wst[64:128, 64:128]))
    r.wait("t", "sv", wb_done)
    if SCSTEP < 9:
        return
    r.wait("s", "sv", zb_done)
    dma("s", zsem1, c.zloc1.ap()[sci * 128:(sci + 1) * 128, :], AR[:, ZBo:ZBo + 256])
    c.zprev_scan = r.val(zsem1)


def emit_phase_ffn(c):
    r, em, ev, dma = c.r, c.em, c.ev, c.dma
    AR = c.arena
    AR8 = c.arena.ap().bitcast(FP8)
    ln4_done = emit_ln(c, 1, 3, 4, c.hbuf, after=[("sv", c.wo_done)], f32_out=c.x4f)

    r.wait("v", "sv", c.ln4_last[1])
    em("v", lambda e: e.drain())
    em("v", lambda e: e.reduce_max(c.smx[:, 0:1], c.smx[:, 32:64], axis=AXX))
    xm = ev("v", "sv", lambda e: e.drain())
    r.wait("t", "sv", xm)
    r.wait("t", "sc", r.val("sc"))
    tok = ev("t", "st", lambda e: e.transpose(c.psE[0:1, 0:128], c.smx[:, 0:1], c.idf[:, :]))
    r.wait("v", "st", tok)
    vv = ev("v", "sv", lambda e: e.reduce_max(c.smx[0:1, 1:2], c.psE[0:1, 0:128], axis=AXX))
    r.wait("g", "sv", vv)
    r.wait("g", "sg", r.val("sg"))
    dma("g", "sg", c.agi_xmx[:, :], c.smx[0:1, 1:2])
    r.wait("g", "sg", r.val("sg"))
    cc_x = c.cc(lambda e: e.collective_compute(
        "AllReduce", ALU.max, replica_groups=c.RG,
        ins=[c.agi_xmx.ap().opt()], outs=[c.ago_xmx.ap().opt()]))
    c.waitcc("g", cc_x)
    r.wait("g", "sg", r.val("sg"))
    dma("g", "sg", c.smx[0:1, 2:3], c.ago_xmx[:, :])
    r.wait("v", "sg", r.val("sg"))
    em("v", lambda e: e.tensor_scalar(c.smx[0:1, 2:3], c.smx[0:1, 2:3], 1.0 / 448.0, 1e-6, ALU.mult, ALU.max))
    em("v", lambda e: e.drain())
    em("v", lambda e: e.reciprocal(c.smx[0:1, 3:4], c.smx[0:1, 2:3]))
    em("v", lambda e: e.drain())
    em("v", lambda e: e.tensor_scalar_mul(c.smx[0:1, 3:4], c.smx[0:1, 3:4], 0.5))
    vv = ev("v", "sv", lambda e: e.drain())
    r.wait("t", "sv", vv)
    tok = ev("t", "st", lambda e: e.transpose(c.psE[0:1, 20:22], c.smx2[0:2, 3:4], c.idf[0:2, 0:2]))
    r.wait("v", "st", tok)
    em("v", lambda e: e.tensor_copy(c.smx[0:1, 4:6], c.psE[0:1, 20:22]))
    em("v", lambda e: e.drain())
    em("v", lambda e: e.tensor_mul(c.smx[0:1, 6:7], c.smx[0:1, 2:3], c.smx[0:1, 4:5]))
    em("v", lambda e: e.drain())
    vv = ev("v", "sv", lambda e: e.tensor_scalar_mul(c.smx[0:1, 6:7], c.smx[0:1, 6:7], 4.0))
    r.wait("t", "sv", vv)
    em("t", lambda e: e.matmul(c.psE[:, 24:25], c.onerf[:, :], c.smx[0:1, 3:4], start=True, stop=True))
    tok = ev("t", "st", lambda e: e.matmul(c.psE[:, 25:26], c.onerf[:, :], c.smx[0:1, 6:7], start=True, stop=True))
    r.wait("v", "st", tok)
    vv = ev("v", "sv", lambda e: e.tensor_copy(c.smx2[:, 8:10], c.psE[:, 24:26]))
    em("v", lambda e: e.drain())
    x4sem = [r.fresh("x4"), r.fresh("x4")]
    if c.ubuf_guard:
        for _tk in c.ubuf_guard:
            r.wait("y", _tk[0], _tk[1])
    for _tk in c.ln4_last[0]:
        r.wait("y", _tk[0], _tk[1])
    r.wait("y", "st", r.val("st"))
    x4toks = {}
    for kk2 in range(2 * DK):
        k, half = kk2 // 2, kk2 % 2
        bpar = kk2 % 2
        if kk2 >= 2:
            r.wait("y", "sv", x4toks[kk2 - 2])
        dma("y", x4sem[bpar], c.ubuf[:, bpar, :], c.x4f.ap()[ts(k), half * 512:(half + 1) * 512])
        r.wait("v", x4sem[bpar], r.val(x4sem[bpar]))
        x4toks[kk2] = ev("v", "sv", lambda e, k=k, half=half, bpar=bpar: e.tensor_scalar(
            AR8[:, k * 1024 + half * 512: k * 1024 + (half + 1) * 512], c.ubuf[:, bpar, :],
            c.smx2[:, 8:9], None, ALU.mult))
    x8_done = ev("v", "sv", lambda e: e.drain())

    c.waitcc("y", c.cc_fc8)
    r.wait("y", "st", r.val("st"))
    if c.ubuf_guard:
        for _tk in c.ubuf_guard:
            r.wait("s", _tk[0], _tk[1])
    fsem = [r.fresh("fc", ), r.fresh("fc")]
    hsem = r.fresh("hb")
    r.wait("t", "sv", x8_done)
    X8 = AR8[:, 0:16384].rearrange("p (k t2) -> p k t2", t2=1024)
    slab_mm = {}
    strip_i = 0
    hb_toks = {}
    for hs in range(16):
        bpar = hs % 2
        if hs >= 2:
            r.wait("y", "st", slab_mm[hs - 2])
        dma("y", fsem[bpar], AR8[:, 16384 + bpar * 8192: 16384 + (bpar + 1) * 8192].rearrange(
            "p (k m) -> p k m", k=DK),
            c.ago_fc8.ap().rearrange("(k p) m -> p k m", p=128)[:, :, hs * 512:(hs + 1) * 512])
        r.wait("t", fsem[bpar], r.val(fsem[bpar]))
        WSL = AR8[:, 16384 + bpar * 8192: 16384 + (bpar + 1) * 8192].rearrange(
            "p (k m) -> p k m", k=DK)
        for mh in range(4):
            hchunk = hs * 4 + mh
            for n in range(2):
                ppar = (hchunk * 2 + n) % 2
                psx = c.psA[:, ppar * 1024:ppar * 1024 + 512]
                if hchunk * 2 + n >= 2:
                    r.wait("t", "sc", ev_sc[hchunk * 2 + n - 2])
                for k in range(DK):
                    em("t", lambda e, k=k, mh=mh, n=n, psx=psx, WSL=WSL: e.matmul(
                        psx, WSL[:, k, mh * 128:(mh + 1) * 128],
                        X8[:, k, n * 512:(n + 1) * 512], start=(k == 0), stop=(k == DK - 1)))
                mk = r.mark("t", "st")
                r.wait("s", "st", mk)
                if hchunk * 2 + n == 0:
                    ev_sc = {}
                    sq_tok = {}
                if hchunk * 2 + n >= 2:
                    r.wait("s", "sv", sq_tok[hchunk * 2 + n - 2])
                if n == 0 and hchunk >= 1:
                    r.wait("v", hsem, hb_toks[hchunk - 1])
                ev_sc[hchunk * 2 + n] = ev("s", "sc", lambda e, ppar=ppar, psx=psx: e.activation(
                    c.ubuf[:, ppar, :], psx, ACT.Relu, scale=c.smx2[:, 9:10]))
                r.wait("v", "sc", ev_sc[hchunk * 2 + n])
                sq_tok[hchunk * 2 + n] = ev("v", "sv", lambda e, ppar=ppar, n=n: e.tensor_mul(
                    c.lnt[:, n, :], c.ubuf[:, ppar, :], c.ubuf[:, ppar, :]))
            em("v", lambda e: e.drain())
            em("v", lambda e, strip_i=strip_i: e.reduce_max(
                c.smx[:, 32 + strip_i % 32:33 + strip_i % 32],
                c.lnt[:, 0:2, :].rearrange("p a b -> p (a b)"), axis=AXX,
                apply_absolute_value=True))
            if strip_i % 32 == 31:
                em("v", lambda e: e.drain())
                em("v", lambda e, strip_i=strip_i: e.tensor_scalar_max(
                    c.smx[:, 7 + strip_i // 32:8 + strip_i // 32] if False else c.smx[:, 7:8],
                    c.smx[:, 7:8] if strip_i >= 32 else c.smx[:, 32:33], 0.0) if False else e.reduce_max(
                    c.smx[:, 7:8] if strip_i // 32 == 0 else c.smx[:, 8:9],
                    c.smx[:, 32:64], axis=AXX))
            strip_i += 1
            hvv = ev("v", "sv", lambda e: e.drain())
            r.wait("y", "sv", hvv)
            if hchunk >= 2:
                r.wait("y", hsem, hb_toks[hchunk - 2])
            hb_toks[hchunk] = dma("y", hsem, c.hbounce.ap()[hchunk * 128:(hchunk + 1) * 128, :],
                                  c.lnt[:, 0:2, :].rearrange("p a b -> p (a b)"))
        slab_mm[hs] = r.val("st")
    em("v", lambda e: e.drain())
    em("v", lambda e: e.tensor_max(c.smx[:, 7:8], c.smx[:, 7:8], c.smx[:, 8:9]))
    hm = ev("v", "sv", lambda e: e.drain())
    r.wait("t", "sv", hm)
    r.wait("t", "sc", r.val("sc"))
    tok = ev("t", "st", lambda e: e.transpose(c.psE[0:1, 0:128], c.smx[:, 7:8], c.idf[:, :]))
    r.wait("v", "st", tok)
    vv = ev("v", "sv", lambda e: e.reduce_max(c.smx[0:1, 10:11], c.psE[0:1, 0:128], axis=AXX))
    r.wait("g", "sv", vv)
    r.wait("g", "sg", r.val("sg"))
    dma("g", "sg", c.agi_hmx[:, :], c.smx[0:1, 10:11])
    r.wait("g", "sg", r.val("sg"))
    cc_h = c.cc(lambda e: e.collective_compute(
        "AllReduce", ALU.max, replica_groups=c.RG,
        ins=[c.agi_hmx.ap().opt()], outs=[c.ago_hmx.ap().opt()]))
    c.waitcc("g", cc_h)
    r.wait("g", "sg", r.val("sg"))
    dma("g", "sg", c.smx[0:1, 11:12], c.ago_hmx[:, :])
    r.wait("v", "sg", r.val("sg"))
    em("v", lambda e: e.tensor_scalar(c.smx[0:1, 11:12], c.smx[0:1, 11:12], 1.0 / 448.0, 1e-6, ALU.mult, ALU.max))
    em("v", lambda e: e.drain())
    em("v", lambda e: e.reciprocal(c.smx[0:1, 12:13], c.smx[0:1, 11:12]))
    em("v", lambda e: e.drain())
    em("v", lambda e: e.tensor_scalar_mul(c.smx[0:1, 12:13], c.smx[0:1, 12:13], 0.5))
    em("v", lambda e: e.drain())
    em("v", lambda e: e.tensor_mul(c.smx[0:1, 13:14], c.smx[0:1, 11:12], c.smx[0:1, 5:6]))
    em("v", lambda e: e.drain())
    vv = ev("v", "sv", lambda e: e.tensor_scalar_mul(c.smx[0:1, 13:14], c.smx[0:1, 13:14], 4.0))
    r.wait("t", "sv", vv)
    em("t", lambda e: e.matmul(c.psE[:, 24:25], c.onerf[:, :], c.smx[0:1, 12:13], start=True, stop=True))
    tok = ev("t", "st", lambda e: e.matmul(c.psE[:, 25:26], c.onerf[:, :], c.smx[0:1, 13:14], start=True, stop=True))
    r.wait("v", "st", tok)
    vv = ev("v", "sv", lambda e: e.tensor_copy(c.smx2[:, 10:12], c.psE[:, 24:26]))
    em("v", lambda e: e.drain())
    gb1 = c.gsl(1, 5)
    em("v", lambda e, gb1=gb1: e.tensor_tensor(
        c.smx[:, 16:32], gb1, c.smx2[:, 11:12].broadcast_to([128, 16]), ALU.mult))
    gbdq = ev("v", "sv", lambda e: e.drain())

    q8sem = [r.fresh("q8"), r.fresh("q8")]
    r.wait("y", hsem, r.val(hsem))
    qs_toks = {}
    for k in range(64):
        bpar = k % 2
        if k >= 2:
            r.wait("y", "sv", qs_toks[k - 2][0])
        r.wait("y", hsem, hb_toks[k] if k in hb_toks else r.val(hsem))
        dma("y", q8sem[bpar], c.lnt[:, bpar, :].bitcast(F32)[:, 0:256] if False else c.lnq[:, bpar, :],
            c.hbounce.ap()[k * 128:(k + 1) * 128, :].rearrange("p (a b) -> p a b", a=2)[:, 0, :] if False
            else c.hbounce.ap()[k * 128:(k + 1) * 128, 0:512])
        dma("y", q8sem[bpar], c.lnt[:, bpar, :], c.hbounce.ap()[k * 128:(k + 1) * 128, 512:1024])
        r.wait("v", q8sem[bpar], r.val(q8sem[bpar]))
        vq = ev("v", "sv", lambda e, bpar=bpar, k=k: e.tensor_scalar(
            AR8[:, 49152 + bpar * 1024: 49152 + bpar * 1024 + 512], c.lnq[:, bpar, :],
            c.smx2[:, 10:11], None, ALU.mult))
        vq2 = ev("v", "sv", lambda e, bpar=bpar, k=k: e.tensor_scalar(
            AR8[:, 49152 + bpar * 1024 + 512: 49152 + (bpar + 1) * 1024], c.lnt[:, bpar, :],
            c.smx2[:, 10:11], None, ALU.mult))
        r.wait("y", "sv", vq2)
        t1 = dma("y", q8sem[bpar], c.h8b.ap()[k * 128:(k + 1) * 128, :],
                 AR8[:, 49152 + bpar * 1024: 49152 + (bpar + 1) * 1024])
        qs_toks[k] = (r.val("sv"), t1)
    c.waitcc("y", c.cc_pj8)
    r.wait("y", "sv", r.val("sv"))
    r.wait("y", q8sem[0], r.val(q8sem[0]))
    r.wait("y", q8sem[1], r.val(q8sem[1]))
    psem = [r.fresh("pj"), r.fresh("pj")]
    h8sem = [r.fresh("h8"), r.fresh("h8")]
    r.wait("v", "sv", gbdq)
    PJ = [AR8[:, 16384 + i * 16384: 16384 + (i + 1) * 16384].rearrange("p (k m) -> p k m", m=256)
          for i in range(2)]
    H8 = [AR8[:, 0 + i * 1024: 0 + (i + 1) * 1024] for i in range(2)]
    pj_mm = {}
    h8_mm = {}
    for dg in range(4):
        for half in range(2):
            sidx = dg * 2 + half
            spar = sidx % 2
            if sidx >= 2:
                r.wait("y", "st", pj_mm[sidx - 2])
            dma("y", psem[spar], PJ[spar],
                c.ago_pj8.ap().rearrange("(k p) m -> p k m", p=128)[:, :, dg * 512 + half * 256: dg * 512 + (half + 1) * 256])
            r.wait("t", psem[spar], r.val(psem[spar]))
            if sidx >= 1:
                r.wait("t", "sv", ev_last)
            mmtok = {}
            for k in range(64):
                kb = k % 2
                if k >= 2:
                    r.wait("y", "st", mmtok[k - 2])
                elif sidx >= 1:
                    r.wait("y", "st", pj_mm[sidx - 1])
                dma("y", h8sem[kb], H8[kb], c.h8b.ap()[k * 128:(k + 1) * 128, :])
                r.wait("t", h8sem[kb], r.val(h8sem[kb]))
                for dm in range(2):
                    for n in range(2):
                        em("t", lambda e, k=k, dm=dm, n=n, spar=spar, kb=kb: e.matmul(
                            c.psA[:, (dm * 2 + n) * 512:(dm * 2 + n + 1) * 512],
                            PJ[spar][:, k, dm * 128:(dm + 1) * 128],
                            H8[kb][:, n * 512:(n + 1) * 512], start=(k == 0), stop=(k == 63)))
                mmtok[k] = r.mark("t", "st")
            r.wait("v", "st", mmtok[63])
            for dm in range(2):
                for n in range(2):
                    dchunk = dg * 4 + half * 2 + dm
                    ev_last = ev("v", "sv", lambda e, dchunk=dchunk, n=n, dm=dm: e.scalar_tensor_tensor(
                        c.xres[:, dchunk * TC + n * 512: dchunk * TC + (n + 1) * 512],
                        c.psA[:, (dm * 2 + n) * 512:(dm * 2 + n + 1) * 512],
                        c.smx[:, 16 + dchunk:17 + dchunk],
                        c.xres[:, dchunk * TC + n * 512: dchunk * TC + (n + 1) * 512],
                        ALU.mult, ALU.add))
            pj_mm[sidx] = r.val("st")
    c.ffn_done = r.val("sv")


# version 03
# speedup vs baseline: 1.1194x; 1.1194x over previous
import numpy as np
import ml_dtypes
import concourse.bass as bass
import concourse.mybir as mybir
from concourse.bass_utils import run_bass_kernel_spmd

F32 = mybir.dt.float32
BF16 = mybir.dt.bfloat16
FP8 = mybir.dt.float8e4
ALU = mybir.AluOpType
ACT = mybir.ActivationFunctionType
AXX = mybir.AxisListType.X

NCORE = 8
T, D = 8192, 2048
TC = T // NCORE
DK = D // 128
HM, HMC, NE = 1408, 11, 8
FFH = 8192
RS128 = float(1.0 / np.sqrt(128.0))
EPS = 1e-5

DEBUG = False
PHASES = 99
SUBPH = 99
SCSTEP = 99


def ts(i, n=128):
    return slice(i * n, (i + 1) * n)


class Rec:
    def __init__(self, nc):
        self.nc = nc
        self.items = []
        self.sem = {}
        self.semctx = []

    def S(self, name):
        cm = self.nc.semaphore(name)
        h = cm.__enter__()
        self.semctx.append(cm)
        self.sem[name] = [h, 0]
        return name

    def fresh(self, prefix="f", kind="hdma"):
        if not hasattr(self, "pools"):
            self.pools = {"gdma": [], "hdma": [], "evt": []}
            self.kinds = {}
        if kind == "evt" and self.pools[kind]:
            return self.pools[kind].pop()
        name = f"sm{len(self.sem)}"
        self.S(name)
        self.kinds[name] = kind
        return name

    def release(self, name):
        self.pools[self.kinds[name]].append(name)

    def inc(self, name, k=1):
        e = self.sem[name]
        e[1] += k
        return e[0], e[1]

    def val(self, name):
        return self.sem[name][1]

    def emit(self, eng, fn, weak=False):
        self.items.append((eng, fn, weak))

    def wait(self, eng, name, v=None):
        hh = self.sem[name][0]
        vv = self.sem[name][1] if v is None else v
        if vv > 0:
            self.emit(eng, lambda e, hh=hh, vv=vv: e.wait_ge(hh, vv), weak=True)

    def mark(self, engl, sem):
        hh, vv = self.inc(sem, 1)
        for i in range(len(self.items) - 1, -1, -1):
            eng, fn, weak = self.items[i]
            if eng == engl and not weak:
                self.items[i] = (eng, lambda e, fn=fn, hh=hh: fn(e).then_inc(hh, 1), False)
                return vv
        raise RuntimeError("no instruction to mark")

    def replay(self, blk):
        byeng = {k: [] for k in "tvsgy"}
        for eng, fn, _w in self.items:
            byeng[eng].append(fn)
        def run(fns):
            def body(e):
                for f in fns:
                    f(e)
            return body
        blk.tensor(run(byeng["t"]))
        blk.vector(run(byeng["v"]))
        blk.scalar(run(byeng["s"]))
        blk.gpsimd(run(byeng["g"]))
        blk.sync(run(byeng["y"]))

    def close(self):
        for cm in reversed(self.semctx):
            cm.__exit__(None, None, None)


class Ctx:
    pass


def build():
    nc = bass.Bass("TRN2", target_bir_lowering=False)
    c = Ctx()
    c.nc = nc

    def EIN(name, shape, dtype=F32):
        return nc.dram_tensor(name, shape, dtype, kind="ExternalInput")

    c.x_in = EIN("x", [D, TC])
    c.l0c = EIN("l0c", [D, 1]); c.l1c = EIN("l1c", [D, 1])
    c.l0wmod = EIN("l0wmod", [D, 1536]); c.l1wmod = EIN("l1wmod", [D, 1536])
    c.l0qkv = EIN("l0qkv", [768, D])
    c.l1qkv = EIN("l1qkv", [772, D])
    c.l0wo_s = EIN("l0wo_s", [D, 256])
    c.l1wo_s = EIN("l1wo_s", [D, 256])
    c.wg_in = EIN("wg", [8, D])
    c.moe1s = EIN("moe1s", [NE, HM, 256])
    c.moe2s = EIN("moe2s", [NE, HM, 256])
    c.moe3s = EIN("moe3s", [NE, D, 176])
    c.bilr_in = EIN("bilr", [1, 4])
    c.w1h_in = EIN("w1h", [256, 64])
    c.wfc_s = EIN("wfc_s", [FFH, 256])
    c.wproj_s = EIN("wproj_s", [D, 1024])
    c.out = nc.dram_tensor("out", [D, TC], F32, kind="ExternalOutput")

    I = lambda a, name: nc.inline_tensor(np.ascontiguousarray(a), name=name)
    c.ident_bf = I(np.eye(128, dtype=ml_dtypes.bfloat16), "identbf")
    c.ident_f = I(np.eye(128, dtype=np.float32), "identf")
    c.ones_col = I(np.ones((128, 1), dtype=ml_dtypes.bfloat16), "onescol")
    c.ones_row = I(np.ones((1, 128), dtype=ml_dtypes.bfloat16), "onesrow")
    c.ones_rowf = I(np.ones((1, 128), dtype=np.float32), "onesrowf")
    mk = np.zeros((128, 4, 512), dtype=ml_dtypes.bfloat16)
    for rr in range(4):
        mk[:, rr, :] = (np.arange(128)[:, None] + 128 * rr <= np.arange(512)[None, :])
    c.maskc_t = I(mk.reshape(128, 2048), "maskc")
    sub = np.arange(128) // 16
    ar = np.arange(128)
    c.maskBL_t = I((sub[:, None] > sub[None, :]).astype(np.float32), "maskBLts")
    c.maskBUn_t = I(-(sub[:, None] < sub[None, :]).astype(np.float32), "maskBUn")
    c.maskBD_t = I(((sub[:, None] == sub[None, :]) & (ar[:, None] <= ar[None, :])).astype(np.float32), "maskBD")
    c.tokidx_t = I((1.0 / ((ar % 16) + 1.0) / 64.0).astype(np.float32).reshape(128, 1), "tokidx")

    def IDR(name, shape, dtype):
        return nc.dram_tensor(name, shape, dtype, kind="Internal")

    c.agi_mod = IDR("agi_mod", [2, 1536], F32); c.ago_mod = IDR("ago_mod", [16, 1536], F32)
    c.agi_h = IDR("agi_h", [D, TC], BF16); c.ago_h = IDR("ago_h", [NCORE * D, TC], BF16)
    c.zloc0 = IDR("zloc0", [T, 256], BF16)
    c.zloc1 = IDR("zloc1", [T, 256], BF16)
    c.part = IDR("part", [T, D], BF16)
    c.wo0T = IDR("wo0T", [256, D], BF16)
    c.wo1T = IDR("wo1T", [256, D], BF16)
    c.rsout = IDR("rsout", [TC, D], BF16)
    c.agi_m1 = IDR("agi_m1", [NE, 256, HM], BF16)
    c.ago_m1 = [IDR(f"ago_m1_{e}", [D, HM], BF16) for e in range(NE)]
    c.agi_m2 = IDR("agi_m2", [NE, 256, HM], BF16)
    c.ago_m2 = [IDR(f"ago_m2_{e}", [D, HM], BF16) for e in range(NE)]
    c.agi_m3 = IDR("agi_m3", [NE, 176, D], BF16)
    c.ago_m3 = [IDR(f"ago_m3_{e}", [HM, D], BF16) for e in range(NE)]
    c.agi_fc8 = IDR("agi_fc8", [256, FFH], FP8); c.ago_fc8 = IDR("ago_fc8", [D, FFH], FP8)
    c.agi_pj8 = IDR("agi_pj8", [1024, D], FP8); c.ago_pj8 = IDR("ago_pj8", [FFH, D], FP8)
    c.agi_wmx = IDR("agi_wmx", [2, 1], F32); c.ago_wmx = IDR("ago_wmx", [2, 1], F32)
    c.agi_xmx = IDR("agi_xmx", [1, 1], F32); c.ago_xmx = IDR("ago_xmx", [1, 1], F32)
    c.agi_hmx = IDR("agi_hmx", [1, 1], F32); c.ago_hmx = IDR("ago_hmx", [1, 1], F32)
    c.hbounce = IDR("hbounce", [FFH, TC], BF16)
    c.h8b = IDR("h8b", [FFH, TC], FP8)
    c.x4f = IDR("x4f", [D, TC], F32)

    c.dbg = {}
    if DEBUG:
        for name, shape in [("dbg_h", [D, TC]), ("dbg_x0a", [D, TC]), ("dbg_x0b", [D, TC]),
                            ("dbg_x1a", [D, TC]), ("dbg_w", [128, 64]), ("dbg_z", [T, 256])]:
            c.dbg[name] = nc.dram_tensor(name, shape, F32, kind="ExternalOutput")

    c.RG = [list(range(NCORE))]

    A = nc.alloc_sbuf_tensor
    c.xres = A("xres", [128, DK * TC], F32)
    c.hbuf = A("hbuf", [128, DK * TC], BF16)
    c.wqkvT = A("wqkvT", [128, DK, 772], BF16)
    c.modb = A("modb", [128, 2 * 6 * DK], F32)
    c.cbuf = A("cbuf", [128, 2 * DK], F32)
    c.silc = A("silc", [128, 2 * DK], F32)

    c.idb = A("idb", [128, 128], BF16)
    c.idf = A("idf", [128, 128], F32)
    c.onec = A("onec", [128, 1], BF16)
    c.oner = A("oner", [1, 128], BF16)
    c.onerf = A("onerf", [1, 128], F32)
    c.mBL = A("mBL", [128, 128], F32)
    c.mBUn = A("mBUn", [128, 128], F32)
    c.mBD = A("mBD", [128, 128], F32)
    c.tkx = A("tkx", [128, 1], F32)
    c.wgT = A("wgT", [128, DK * 8], BF16)
    c.wmoe = A("wmoe", [128, 64], F32)
    c.stg = A("stg", [128, 2, 1024], BF16)
    c.stgf = A("stgf", [128, 2, 1536], F32)
    c.smx = A("smx", [128, 64], F32)
    c.smx2 = A("smx2", [128, 64], F32)
    c.rowscr = A("rowscr", [1, 1536], F32)
    c.mtri = A("mtri", [128, 128], BF16)
    c.lnt = A("lnt", [128, 2, 512], BF16)
    c.lnq = A("lnq", [128, 2, 512], BF16)
    c.ubuf = A("ubuf", [128, 2, 512], F32)
    c.arena = A("arena", [128, 26624], BF16)
    c.psA = nc.alloc_psum_tensor("psA", [128, 2048], F32)
    c.psB = nc.alloc_psum_tensor("psB", [128, 512], F32)
    c.psC = nc.alloc_psum_tensor("psC", [128, 512], F32)
    c.psD = nc.alloc_psum_tensor("psD", [128, 512], F32)
    c.psE = nc.alloc_psum_tensor("psE", [128, 512], F32)

    r = Rec(nc)
    c.r = r
    c.ubuf_guard = None
    c.stg_gate = None
    for s in ["sg", "sh", "sx", "st", "sv", "sc", "scc", "sge",
              "d0", "d1", "d2", "d3", "d4", "d5", "d6", "d7"]:
        r.S(s)

    emit_all(c)

    blk_cm = nc.Block()
    blk = blk_cm.__enter__()
    r.replay(blk)
    blk_cm.__exit__(None, None, None)
    r.close()
    return nc


def emit_all(c):
    r = c.r
    nc = c.nc

    def dma(engl, sem, outap, inap, **kw):
        hh, vv = r.inc(sem, 16)
        r.emit(engl, lambda e, o=outap, i=inap, hh=hh, kw=kw: e.dma_start(out=o, in_=i, **kw).then_inc(hh, 16))
        return vv

    def ev(engl, sem, fn):
        hh, vv = r.inc(sem, 1)
        r.emit(engl, lambda e, fn=fn, hh=hh: fn(e).then_inc(hh, 1))
        return vv

    def em(engl, fn):
        r.emit(engl, fn)

    def cc(fn):
        s = r.fresh("cc", kind="evt")
        hh, vv = r.inc(s, 1)
        r.emit("g", lambda e, fn=fn, hh=hh: fn(e).then_inc(hh, 1))
        return (s, vv)

    def waitcc(engl, tok):
        r.wait(engl, tok[0], tok[1])

    c.dma, c.ev, c.em, c.cc, c.waitcc = dma, ev, em, cc, waitcc

    for src, dst in [(c.ident_bf, c.idb), (c.ident_f, c.idf), (c.ones_col, c.onec), (c.ones_row, c.oner), (c.ones_rowf, c.onerf),
                     (c.maskBL_t, c.mBL), (c.maskBUn_t, c.mBUn),
                     (c.maskBD_t, c.mBD), (c.tokidx_t, c.tkx)]:
        dma("g", "sg", dst[:, :], src[:, :])
    dma("g", "sg", c.mtri[:, :], c.maskc_t[:, 0:128])
    dma("g", "sg", c.cbuf[:, 0:DK], c.l0c.ap().rearrange("(k p) o -> p (k o)", p=128), allow_slow_non_contiguous=True)
    dma("g", "sg", c.cbuf[:, DK:2 * DK], c.l1c.ap().rearrange("(k p) o -> p (k o)", p=128), allow_slow_non_contiguous=True)
    for k in range(DK):
        dma("g", "sg", c.xres[:, k * TC:(k + 1) * TC], c.x_in[ts(k), :])
    n_pro = r.val("sg")

    r.wait("s", "sg", n_pro)
    silu_done = ev("s", "sc", lambda e: e.activation(c.silc[:, :], c.cbuf[:, :], ACT.Silu))

    modps = c.psA
    wmsem = [r.fresh("wm"), r.fresh("wm")]
    mm_done = {}
    evac_done = 0
    for L in range(2):
        wm = c.l0wmod if L == 0 else c.l1wmod
        for k in range(DK):
            gidx = L * DK + k
            buf = gidx % 2
            if gidx >= 2:
                r.wait("y", "st", mm_done[gidx - 2])
            dma("y", wmsem[buf], c.stgf[:, buf, :], wm[ts(k), :])
            r.wait("t", wmsem[buf], r.val(wmsem[buf]))
            if gidx == 0:
                r.wait("t", "sc", silu_done)
            if k == 0 and L == 1:
                r.wait("t", "sv", evac_done)
            for nn in range(3):
                fn = lambda e, b=buf, nn=nn, k=k, g2=gidx: e.matmul(
                    modps[0:1, nn * 512:(nn + 1) * 512], c.silc[:, g2:g2 + 1],
                    c.stgf[:, b, nn * 512:(nn + 1) * 512], start=(k == 0), stop=(k == DK - 1))
                if nn == 2:
                    mm_done[gidx] = ev("t", "st", fn)
                else:
                    em("t", fn)
        r.wait("v", "st", r.val("st"))
        if L == 1:
            r.wait("v", "sg", mod_store0)
        evac_done = ev("v", "sv", lambda e: e.tensor_copy(c.rowscr[:, 0:1536], modps[0:1, 0:1536]))
        r.wait("g", "sv", evac_done)
        if L == 0:
            r.wait("g", "sg", r.val("sg"))
        mod_store0 = dma("g", "sg", c.agi_mod.ap()[L:L + 1, :], c.rowscr[:, 0:1536])
    r.wait("g", "sg", r.val("sg"))
    cc_mod = cc(lambda e: e.collective_compute(
        "AllGather", ALU.bypass, replica_groups=c.RG,
        ins=[c.agi_mod.ap().opt()], outs=[c.ago_mod.ap().opt()]))

    waitcc("g", cc_mod)
    r.release(cc_mod[0])
    for L in range(2):
        for rr in range(8):
            srcap = c.ago_mod.ap()[2 * rr + L].rearrange("(cc p) -> p cc", p=128)
            dsta = c.modb[:, L * 96 + rr * 12: L * 96 + (rr + 1) * 12]
            dma("g", "sg", dsta, srcap, allow_slow_non_contiguous=True)
    n_modload = r.val("sg")

    def gsl(L, gi):
        return c.modb[:, L * 96 + gi * 16: L * 96 + gi * 16 + DK]

    r.wait("s", "sg", n_modload)
    r.wait("v", "sg", n_modload)
    abs_done = 0
    for i, (L, gi) in enumerate([(0, 2), (0, 5), (1, 2), (1, 5)]):
        abs_done = ev("s", "sc", lambda e, L=L, gi=gi, i=i: e.activation(
            c.smx2[:, i * DK:(i + 1) * DK], gsl(L, gi), ACT.Abs))
    for i, (L, gi) in enumerate([(0, 2), (0, 5), (1, 2), (1, 5)]):
        em("v", lambda e, L=L, gi=gi, i=i: e.tensor_single_scalar(
            c.smx[:, i * DK:(i + 1) * DK], gsl(L, gi), 0.0, ALU.is_ge))
    em("v", lambda e: e.drain())
    em("v", lambda e: e.tensor_scalar(c.smx[:, 0:64], c.smx[:, 0:64], 2.0, -1.0, ALU.mult, ALU.add))
    r.wait("v", "sc", abs_done)
    em("v", lambda e: e.tensor_scalar_max(c.smx2[:, 0:64], c.smx2[:, 0:64], 0.25))
    em("v", lambda e: e.drain())
    for i, (L, gi) in enumerate([(0, 2), (0, 5), (1, 2), (1, 5)]):
        em("v", lambda e, L=L, gi=gi, i=i: e.tensor_mul(
            gsl(L, gi), c.smx[:, i * DK:(i + 1) * DK], c.smx2[:, i * DK:(i + 1) * DK]))
    em("v", lambda e: e.drain())
    for L in range(2):
        for gi in (1, 4):
            em("v", lambda e, L=L, gi=gi: e.tensor_scalar_add(gsl(L, gi), gsl(L, gi), 1.0))
    gates_done = ev("v", "sv", lambda e: e.drain())
    c.gsl = gsl
    c.gates_done = gates_done

    emit_stage_qkvT(c, c.l0qkv, 768)

    if PHASES >= 1:
        ln_done = emit_ln(c, 0, 0, 1, c.hbuf)
        r.wait("g", "sc", ln_done)
        for k in range(DK):
            dma("g", "sg", c.agi_h[ts(k), :], c.hbuf[:, k * TC:(k + 1) * TC])
        r.wait("g", "sg", r.val("sg"))
        c.cc_h0 = c.cc(lambda e: e.collective_compute(
            "AllGather", ALU.bypass, replica_groups=c.RG,
            ins=[c.agi_h.ap().opt()], outs=[c.ago_h.ap().opt()]))

        if DEBUG and "dbg_h" in c.dbg:
            emit_dump_bf16(c, c.hbuf, c.dbg["dbg_h"], after_sc=ln_done)

    if PHASES >= 2:
        emit_phase_l0attn(c)
        if DEBUG and "dbg_x0a" in c.dbg:
            emit_dump_f32(c, c.xres, c.dbg["dbg_x0a"], after_sv=c.r.val("sv"))

    if PHASES >= 3:
        emit_phase_moe(c)
        if DEBUG and "dbg_x0b" in c.dbg:
            emit_dump_f32(c, c.xres, c.dbg["dbg_x0b"], after_sv=c.r.val("sv"))

    if PHASES >= 4:
        emit_phase_l1attn(c)
        if DEBUG and "dbg_x1a" in c.dbg:
            emit_dump_f32(c, c.xres, c.dbg["dbg_x1a"], after_sv=c.r.val("sv"))

    if PHASES >= 5:
        emit_phase_ffn(c)

    r.wait("y", "sc", r.val("sc"))
    r.wait("y", "sv", r.val("sv"))
    r.wait("y", "st", r.val("st"))
    r.wait("y", "sg", r.val("sg"))
    fsem = r.fresh("fin")
    for k in range(DK):
        dma("y", fsem, c.out[ts(k), :], c.xres[:, k * TC:(k + 1) * TC])
    r.wait("y", fsem, r.val(fsem))


def emit_ln(c, L, gi_s, gi_c, outbuf, after=(), f32_out=None):
    r = c.r
    em, ev, dma = c.em, c.ev, c.dma
    gs = c.gsl(L, gi_s)
    gc = c.gsl(L, gi_c)
    r.wait("v", "sv", c.gates_done)
    r.wait("s", "sv", c.gates_done)
    for tk in after:
        r.wait("v", tk[0], tk[1])
        r.wait("s", tk[0], tk[1])
        r.wait("t", tk[0], tk[1])
    if c.ubuf_guard:
        for _tk in c.ubuf_guard:
            r.wait("v", _tk[0], _tk[1])
    if f32_out is not None:
        if not hasattr(c, "ln4sem"):
            c.ln4sem = [r.fresh("l4"), r.fresh("l4")]
        ln4_toks = {}
        ln4_red = {}
    last_apply_sc = {}
    for n in range(2):
        mmk_done = {}
        for k in range(DK):
            xsl = c.xres[:, k * TC + n * 512: k * TC + n * 512 + 512]
            if k >= 2:
                r.wait("v", "st", mmk_done[k - 2])
                r.wait("s", "st", mmk_done[k - 2])
            vv = ev("v", "sv", lambda e, xsl=xsl, k=k: e.tensor_copy(c.lnt[:, k % 2, :], xsl))
            vs = ev("s", "sc", lambda e, xsl=xsl, k=k: e.activation(c.lnq[:, k % 2, :], xsl, ACT.Square))
            r.wait("t", "sv", vv)
            r.wait("t", "sc", vs)
            em("t", lambda e, k=k: e.matmul(c.psB[0:1, :], c.onec[:, :], c.lnt[:, k % 2, :],
                                            start=(k == 0), stop=(k == DK - 1)))
            mmk_done[k] = ev("t", "st", lambda e, k=k: e.matmul(
                c.psC[0:1, :], c.onec[:, :], c.lnq[:, k % 2, :],
                start=(k == 0), stop=(k == DK - 1)))
        sums_done = mmk_done[DK - 1]
        r.wait("v", "st", sums_done)
        em("v", lambda e: e.tensor_scalar_mul(c.rowscr[:, 0:512], c.psB[0:1, :], 1.0 / D))
        em("v", lambda e: e.drain())
        em("v", lambda e: e.tensor_mul(c.rowscr[:, 512:1024], c.rowscr[:, 0:512], c.rowscr[:, 0:512]))
        em("v", lambda e: e.drain())
        em("v", lambda e: e.scalar_tensor_tensor(c.rowscr[:, 512:1024], c.psC[0:1, :], 1.0 / D,
                                                 c.rowscr[:, 512:1024], ALU.mult, ALU.subtract))
        em("v", lambda e: e.drain())
        em("v", lambda e: e.tensor_scalar_add(c.rowscr[:, 512:1024], c.rowscr[:, 512:1024], EPS))
        em("v", lambda e: e.drain())
        vv = ev("v", "sv", lambda e: e.reciprocal(c.rowscr[:, 512:1024], c.rowscr[:, 512:1024]))
        r.wait("s", "sv", vv)
        vs = ev("s", "sc", lambda e: e.activation(c.rowscr[:, 512:1024], c.rowscr[:, 512:1024], ACT.Sqrt))
        r.wait("v", "sc", vs)
        vv2 = ev("v", "sv", lambda e: e.tensor_mul(c.rowscr[:, 0:512], c.rowscr[:, 0:512], c.rowscr[:, 512:1024]))
        r.wait("t", "sv", vv2)
        em("t", lambda e: e.matmul(c.psD[:, :], c.onerf[:, :], c.rowscr[0:1, 512:1024], start=True, stop=True))
        bc_done = ev("t", "st", lambda e: e.matmul(c.psE[:, :], c.onerf[:, :], c.rowscr[0:1, 0:512], start=True, stop=True))
        r.wait("v", "st", bc_done)
        r.wait("s", "st", bc_done)
        for k in range(DK):
            xsl = c.xres[:, k * TC + n * 512: k * TC + n * 512 + 512]
            osl = outbuf[:, k * TC + n * 512: k * TC + n * 512 + 512]
            usl = c.ubuf[:, k % 2, :]
            nk = n * DK + k
            if k >= 2:
                r.wait("v", "sc", last_apply_sc[(n, k - 2)])
            if f32_out is not None and nk >= 2:
                r.wait("v", c.ln4sem[k % 2], ln4_toks[nk - 2])
                r.wait("v", "sv", ln4_red[nk - 2])
            em("v", lambda e, xsl=xsl, usl=usl: e.tensor_mul(usl, xsl, c.psD[:, :]))
            em("v", lambda e: e.drain())
            vv = ev("v", "sv", lambda e, usl=usl: e.tensor_sub(usl, usl, c.psE[:, :]))
            r.wait("s", "sv", vv)
            if f32_out is None:
                last_apply_sc[(n, k)] = ev("s", "sc", lambda e, usl=usl, osl=osl, k=k: e.activation(
                    osl, usl, ACT.Identity, bias=gs[:, k:k + 1], scale=gc[:, k:k + 1]))
            else:
                last_apply_sc[(n, k)] = ev("s", "sc", lambda e, usl=usl, k=k: e.activation(
                    usl, usl, ACT.Identity, bias=gs[:, k:k + 1], scale=gc[:, k:k + 1]))
                r.wait("y", "sc", last_apply_sc[(n, k)])
                ln4_toks[nk] = c.dma("y", c.ln4sem[k % 2],
                                     f32_out.ap()[ts(k), n * 512:(n + 1) * 512], usl)
                r.wait("v", "sc", last_apply_sc[(n, k)])
                ln4_red[nk] = ev("v", "sv", lambda e, usl=usl, k=k, n=n: e.reduce_max(
                    c.smx[:, 32 + k * 2 + n:33 + k * 2 + n], usl, axis=AXX,
                    apply_absolute_value=True))
        if n == 0:
            r.wait("t", "sv", r.val("sv"))
    c.ubuf_guard = [("sc", r.val("sc"))]
    if f32_out is not None:
        c.ln4_last = ([(s_, r.val(s_)) for s_ in c.ln4sem], ln4_red[2 * DK - 1])
    return last_apply_sc[(1, DK - 1)]


def emit_dump_bf16(c, srcbuf, dst, after_sc=0):
    r = c.r
    em, ev, dma = c.em, c.ev, c.dma
    if not hasattr(c, "du_sem"):
        c.du_sem = r.fresh("du")
    dsem = c.du_sem
    r.wait("v", "sc", after_sc)
    if c.ubuf_guard:
        for _tk in c.ubuf_guard:
            r.wait("v", _tk[0], _tk[1])
    for k in range(DK):
        vv = ev("v", "sv", lambda e, k=k: e.tensor_copy(c.ubuf[:, 0:2, :].rearrange("p a b -> p (a b)"),
                                                        srcbuf[:, k * TC:(k + 1) * TC]))
        r.wait("y", "sv", vv)
        vv2 = dma("y", dsem, dst[ts(k), :], c.ubuf[:, 0:2, :].rearrange("p a b -> p (a b)"))
        r.wait("y", dsem, vv2)
        r.wait("v", dsem, vv2)
    c.ubuf_guard = [(dsem, r.val(dsem))]
    return r.val(dsem)


def shard_inputs(I):
    maps = []
    x = np.asarray(I["L0_attn_x"])[0]
    WK, WQ, WV = np.asarray(I["L0_attn_W_K"]), np.asarray(I["L0_attn_W_Q"]), np.asarray(I["L0_attn_W_V"])
    for cid in range(NCORE):
        m = {}
        m["x"] = np.ascontiguousarray(x[cid * TC:(cid + 1) * TC, :].T)
        m["l0c"] = np.ascontiguousarray(np.asarray(I["L0_c"]).reshape(D, 1))
        m["l1c"] = np.ascontiguousarray(np.asarray(I["L1_c"]).reshape(D, 1))
        m["l0wmod"] = np.ascontiguousarray(np.asarray(I["L0_wmod"])[:, cid * 1536:(cid + 1) * 1536])
        m["l1wmod"] = np.ascontiguousarray(np.asarray(I["L1_wmod"])[:, cid * 1536:(cid + 1) * 1536])
        qkv = np.concatenate([WK[2 * cid:2 * cid + 2].reshape(256, D),
                              WQ[2 * cid:2 * cid + 2].reshape(256, D),
                              WV[2 * cid:2 * cid + 2].reshape(256, D)], axis=0)
        m["l0qkv"] = np.ascontiguousarray(qkv)
        m["l1qkv"] = np.ascontiguousarray(np.concatenate(
            [np.asarray(I["L1_attn_Wq"])[cid * 256:(cid + 1) * 256, :],
             np.asarray(I["L1_attn_Wk"])[cid * 256:(cid + 1) * 256, :],
             np.asarray(I["L1_attn_Wv"])[cid * 256:(cid + 1) * 256, :],
             np.asarray(I["L1_attn_W_ilr"])[4 * cid:4 * cid + 4, :]], axis=0))
        m["l0wo_s"] = np.ascontiguousarray(np.asarray(I["L0_attn_W_O"])[:, cid * 256:(cid + 1) * 256])
        m["l1wo_s"] = np.ascontiguousarray(np.asarray(I["L1_attn_Wo"])[:, cid * 256:(cid + 1) * 256])
        m["wg"] = np.ascontiguousarray(np.asarray(I["L0_ffn_Wg"]))
        m["moe1s"] = np.ascontiguousarray(np.asarray(I["L0_ffn_W1"])[:, :, cid * 256:(cid + 1) * 256])
        m["moe2s"] = np.ascontiguousarray(np.asarray(I["L0_ffn_W2"])[:, :, cid * 256:(cid + 1) * 256])
        m["moe3s"] = np.ascontiguousarray(np.asarray(I["L0_ffn_W3"])[:, :, cid * 176:(cid + 1) * 176])
        m["bilr"] = np.ascontiguousarray(np.asarray(I["L1_attn_b_ilr"])[4 * cid:4 * cid + 4].reshape(1, 4))
        m["w1h"] = np.ascontiguousarray(np.asarray(I["L1_attn_W1"])[4 * cid:4 * cid + 4].reshape(256, 64))
        m["wfc_s"] = np.ascontiguousarray(np.asarray(I["L1_ffn_w_fc"])[:, cid * 256:(cid + 1) * 256])
        m["wproj_s"] = np.ascontiguousarray(np.asarray(I["L1_ffn_w_proj"])[:, cid * 1024:(cid + 1) * 1024])
        maps.append(m)
    return maps


def kernel(**inputs):
    nc = build()
    maps = shard_inputs(inputs)
    res = run_bass_kernel_spmd(nc, maps, core_ids=list(range(NCORE)))
    full = np.concatenate([np.asarray(r_["out"]).T for r_ in res.results], axis=0)
    return full.reshape(1, T, D).astype(np.float32)


def kernel_debug(inputs):
    nc = build()
    maps = shard_inputs(inputs)
    return run_bass_kernel_spmd(nc, maps, core_ids=list(range(NCORE)))


def emit_dump_f32(c, srcbuf, dst, after_sv=0):
    r = c.r
    dma = c.dma
    if not hasattr(c, "du2_sem"):
        c.du2_sem = r.fresh("du2")
    dsem = c.du2_sem
    r.wait("y", "sv", after_sv)
    r.wait("y", "sc", r.val("sc"))
    for k in range(DK):
        dma("y", dsem, dst[ts(k), :], srcbuf[:, k * TC:(k + 1) * TC])
        r.wait("y", dsem, r.val(dsem))
    return r.val(dsem)


def emit_stage_qkvT(c, src, ncols, col_off=0):
    r, em, ev, dma = c.r, c.em, c.ev, c.dma
    psb = [c.psD.ap().bitcast(BF16), c.psE.ap().bitcast(BF16)]
    if not hasattr(c, "qkT_sems"):
        c.qkT_sems = [r.fresh("qk", "gdma"), r.fresh("qk", "gdma")]
        c.qkT_i = 0
    if c.stg_gate is not None:
        c.r.wait("g", c.stg_gate[0], c.stg_gate[1])
    qsem = c.qkT_sems
    nrt = ncols // 128
    r.wait("t", "sv", r.val("sv"))
    r.wait("t", "sc", r.val("sc"))

    ld_tok = {}
    ev_done = {}
    for i, (rt, half) in enumerate([(rt, h) for rt in range(nrt) for h in range(2)]):
        gi = c.qkT_i + i
        buf = gi % 2
        sem = qsem[buf]
        if i >= 2:
            r.wait("g", "st", ev_done[i - 2])
        elif c.qkT_i > 0:
            r.wait("g", "st", c.qkT_lastev)
        dma("g", sem, c.stg[:, buf, :], src[ts(rt), half * 1024:(half + 1) * 1024])
        ld_tok[i] = (sem, r.val(sem))
        r.wait("t", sem, r.val(sem))
        for grp in range(2):
            pb = psb[grp]
            if i >= 1:
                pass
            for sub in range(4):
                k = half * 8 + grp * 4 + sub
                em("t", lambda e, buf=buf, grp=grp, sub=sub, pb=pb: e.transpose(
                    pb[:, sub * 128:(sub + 1) * 128], c.stg[:, buf, (grp * 4 + sub) * 128:(grp * 4 + sub + 1) * 128],
                    c.idb[:, :]))
            tok = r.mark("t", "st")
            r.wait("v", "st", tok)
            k0 = half * 8 + grp * 4
            vv = ev("v", "sv", lambda e, pb=pb, k0=k0, rt=rt: e.tensor_copy(
                c.wqkvT[:, k0:k0 + 4, col_off + rt * 128: col_off + (rt + 1) * 128], pb[:, 0:512]))
            r.wait("t", "sv", vv)
        ev_done[i] = r.val("st")
    c.qkvT_done = r.val("sv")
    c.qkT_i += 2 * nrt
    c.qkT_lastev = ev_done[2 * nrt - 1]
    c.stg_gate = ("st", r.val("st"))


def emit_stage_agi(c, src, R, C, agi_rows_ap, ag_in_ap, ag_out_ap, tag):
    r, em, ev, dma = c.r, c.em, c.ev, c.dma
    psb = [c.psD.ap().bitcast(BF16), c.psE.ap().bitcast(BF16)]
    if not hasattr(c, "agi_sems"):
        c.agi_sems = [c.r.fresh("ag", "gdma"), c.r.fresh("ag", "gdma")]
        c.agi_i = 0
        c.agi_ev = {}
    if c.stg_gate is not None:
        c.r.wait("g", c.stg_gate[0], c.stg_gate[1])
    gsem = c.agi_sems
    r.wait("t", "sv", r.val("sv"))
    r.wait("t", "sc", r.val("sc"))
    asmv = c.stgf.ap().bitcast(BF16)
    asmv = asmv.rearrange("p a b -> p (a b)")
    cts = []
    off = 0
    while off < C:
        sz = min(128, C - off)
        cts.append((off, sz))
        off += sz
    nrt = (R + 127) // 128
    for rt in range(nrt):
        gi = c.agi_i + rt
        rsz = min(128, R - rt * 128)
        buf = gi % 2
        sem = gsem[buf]
        if gi >= 2:
            r.wait("g", "st", c.agi_ev[gi - 2])
        dma("g", sem, c.stg[0:rsz, buf, 0:C], src[rt * 128: rt * 128 + rsz, :])
        r.wait("t", sem, r.val(sem))
        for ci, (coff, csz) in enumerate(cts):
            pb = psb[gi % 2]
            em("t", lambda e, buf=buf, coff=coff, csz=csz, rsz=rsz, pb=pb, ci=ci: e.transpose(
                pb[0:csz, ci * 128: ci * 128 + rsz], c.stg[0:rsz, buf, coff:coff + csz], c.idb[:, :]))
        tok = r.mark("t", "st")
        c.agi_ev[gi] = tok
        r.wait("v", "st", tok)
        for ci, (coff, csz) in enumerate(cts):
            pb = psb[gi % 2]
            vv = ev("v", "sv", lambda e, pb=pb, ci=ci, csz=csz, rsz=rsz, rt=rt: e.tensor_copy(
                asmv[0:csz, ci * 3072 + rt * 128: ci * 3072 + rt * 128 + rsz], pb[0:csz, ci * 128: ci * 128 + rsz]))
        r.wait("t", "sv", r.val("sv"))
    r.wait("g", "sv", r.val("sv"))
    r.wait("g", "sg", r.val("sg"))
    for ci, (coff, csz) in enumerate(cts):
        dma("g", "sg", agi_rows_ap(ci, csz), asmv[0:csz, ci * 3072: ci * 3072 + R])
    r.wait("g", "sg", r.val("sg"))
    tok = c.cc(lambda e: e.collective_compute(
        "AllGather", ALU.bypass, replica_groups=c.RG,
        ins=[ag_in_ap], outs=[ag_out_ap]))
    c.agi_i += nrt
    c.stg_gate = ("st", r.val("st"))
    return tok


def emit_phase_l0attn(c):
    r, em, ev, dma = c.r, c.em, c.ev, c.dma
    nc = c.nc
    AR = c.arena
    Q0, K0, V0, PT, ZST = 0, 8192, 16384, 24576, 26624
    psb_S = c.psA
    psZ = c.psB
    psSum = c.psC

    emit_stage_woT(c, c.l0wo_s, c.wo0T)
    c.cc_moe = {}
    for e_ in range(2):
        emit_stage_moe_expert(c, e_)

    emit_wmx_pass(c)

    c.waitcc("y", c.cc_h0)
    for hd in range(2):
        emit_qkv_prod(c, hd, Q0, K0, V0, PT)
        emit_stabilizer(c, hd, Q0, K0)
        emit_attn_bands(c, hd, Q0, K0, V0, PT, ZST)
        if hd == 0:
            for e_ in range(2, 5):
                emit_stage_moe_expert(c, e_)
        else:
            for e_ in range(5, 8):
                emit_stage_moe_expert(c, e_)
    if DEBUG and "dbg_z" in c.dbg:
        r.wait("g", c.attn_done[0], c.attn_done[1])
        r.wait("g", "sg", r.val("sg"))
        dma("g", "sg", c.dbg["dbg_z"].ap(), c.zloc0.ap())
    emit_wo_partial(c, c.zloc0, c.wo0T, c.attn_done, L=0, gi_gate=2)


def emit_stage_moe_expert(c, e_):
    r = c.r
    tok1 = emit_stage_agi(c, c.moe1s.ap()[e_], HM, 256,
                          lambda ci, csz, e_=e_: c.agi_m1.ap()[e_, ci * 128: ci * 128 + csz, :],
                          c.agi_m1.ap()[e_], c.ago_m1[e_].ap().opt(), f"m1_{e_}")
    tok2 = emit_stage_agi(c, c.moe2s.ap()[e_], HM, 256,
                          lambda ci, csz, e_=e_: c.agi_m2.ap()[e_, ci * 128: ci * 128 + csz, :],
                          c.agi_m2.ap()[e_], c.ago_m2[e_].ap().opt(), f"m2_{e_}")
    tok3 = emit_stage_agi(c, c.moe3s.ap()[e_], D, 176,
                          lambda ci, csz, e_=e_: c.agi_m3.ap()[e_, ci * 128: ci * 128 + csz, :],
                          c.agi_m3.ap()[e_], c.ago_m3[e_].ap().opt(), f"m3_{e_}")
    c.cc_moe[e_] = (tok1, tok2, tok3)
def emit_wmx_pass(c):
    r, em, ev, dma = c.r, c.em, c.ev, c.dma
    for j, (src, RR, CC) in enumerate(((c.wfc_s, FFH, 256), (c.wproj_s, D, 1024))):
        flat = src.ap().rearrange("a b -> (a b)").rearrange("(n p q) -> n p q", p=128, q=512)
        nld = (RR * CC) // (128 * 512)
        toks = {}
        if not hasattr(c, "wx_sems"):
            c.wx_sems = [r.fresh("wx"), r.fresh("wx")]
        xsem = c.wx_sems
        if c.ubuf_guard:
            for _tk in c.ubuf_guard:
                r.wait("y", _tk[0], _tk[1])
        for i in range(nld):
            buf = i % 2
            sem = xsem[buf]
            if i >= 2:
                r.wait("y", "sv", toks[i - 2])
            dma("y", sem, c.ubuf[:, buf, :], flat[i])
            r.wait("v", sem, r.val(sem))
            toks[i] = ev("v", "sv", lambda e, buf=buf, j=j, i=i: e.reduce_max(
                c.smx[:, j * 32 + i:j * 32 + i + 1], c.ubuf[:, buf, :], axis=AXX,
                apply_absolute_value=True))
        em("v", lambda e: e.drain())
        em("v", lambda e, j=j, nld=nld: e.reduce_max(c.smx2[:, j:j + 1], c.smx[:, j * 32:j * 32 + nld], axis=AXX))
        vg = ev("v", "sv", lambda e: e.drain())
        c.ubuf_guard = [("sv", vg)]
    vv = ev("v", "sv", lambda e: e.drain())
    r.wait("t", "sv", vv)
    r.wait("t", "sc", r.val("sc"))
    tok = ev("t", "st", lambda e: e.transpose(c.psE[0:2, 0:128], c.smx2[:, 0:2], c.idf[:, :]))
    r.wait("v", "st", tok)
    vv = ev("v", "sv", lambda e: e.reduce_max(c.smx2[0:2, 2:3], c.psE[0:2, 0:128], axis=AXX))
    r.wait("g", "sv", vv)
    dma("g", "sg", c.agi_wmx[:, :], c.smx2[0:2, 2:3])
    r.wait("g", "sg", r.val("sg"))
    c.cc_wmx = c.cc(lambda e: e.collective_compute(
        "AllReduce", ALU.max, replica_groups=c.RG,
        ins=[c.agi_wmx.ap().opt()], outs=[c.ago_wmx.ap().opt()]))
    c.waitcc("g", c.cc_wmx)
    r.wait("g", "sg", r.val("sg"))
    dma("g", "sg", c.smx2[0:2, 3:4], c.ago_wmx[:, :])
    r.wait("v", "sg", r.val("sg"))
    em("v", lambda e: e.tensor_scalar(c.smx2[0:2, 3:4], c.smx2[0:2, 3:4], 1.0 / 448.0, 1e-6, ALU.mult, ALU.max))
    em("v", lambda e: e.drain())
    em("v", lambda e: e.reciprocal(c.smx2[0:2, 4:5], c.smx2[0:2, 3:4]))
    em("v", lambda e: e.drain())
    vv = ev("v", "sv", lambda e: e.tensor_scalar_mul(c.smx2[0:2, 4:5], c.smx2[0:2, 4:5], 0.5))
    r.wait("t", "sv", vv)
    tok = ev("t", "st", lambda e: e.transpose(c.psE[0:1, 24:26], c.smx2[0:2, 4:5], c.idf[0:2, 0:2]))
    r.wait("v", "st", tok)
    vv = ev("v", "sv", lambda e: e.tensor_copy(c.smx[0:1, 30:32], c.psE[0:1, 24:26]))
    r.wait("t", "sv", vv)
    em("t", lambda e: e.matmul(c.psE[:, 8:10], c.onerf[:, :], c.smx[0:1, 30:32], start=True, stop=True))
    tok = r.mark("t", "st")
    r.wait("v", "st", tok)
    vv = ev("v", "sv", lambda e: e.tensor_copy(c.smx2[:, 5:7], c.psE[:, 8:10]))
    c.rw_done = vv


def emit_qkv_prod(c, hd, Q0, K0, V0, PT):
    r, em, ev, dma = c.r, c.em, c.ev, c.dma
    AR = c.arena
    kcol = 128 * hd
    qcol = 256 + 128 * hd
    vcol = 512 + 128 * hd
    r.wait("t", "sv", c.qkvT_done)
    r.wait("t", "sv", r.val("sv"))
    r.wait("t", "sc", r.val("sc"))
    r.wait("y", "st", r.val("st"))
    r.wait("y", "sc", r.val("sc"))
    r.wait("y", "sv", r.val("sv"))
    c.waitcc("y", c.cc_h0)
    evac_tok = 0
    if not hasattr(c, "h_sems"):
        c.h_sems = [r.fresh("hs"), r.fresh("hs")]
        c.h_i = 0
        c.h_lastmm = None
    hsem = c.h_sems
    mm_tok = {}
    for r8 in range(8):
        for k in range(DK):
            i = r8 * DK + k
            buf = (c.h_i + i) % 2
            sem = hsem[buf]
            if i >= 2:
                r.wait("y", "st", mm_tok[i - 2])
            elif c.h_lastmm is not None:
                r.wait("y", "st", c.h_lastmm)
            dma("y", sem, AR[:, PT + buf * 1024: PT + (buf + 1) * 1024],
                c.ago_h.ap()[r8 * D + k * 128: r8 * D + (k + 1) * 128, :])
            r.wait("t", sem, r.val(sem))
            if k == 0 and evac_tok:
                r.wait("t", "sv", evac_tok)
                r.wait("t", "sc", evac_sc)
            ht = AR[:, PT + buf * 1024: PT + (buf + 1) * 1024]
            for n in range(2):
                em("t", lambda e, ht=ht, n=n, k=k, qcol=qcol: e.matmul(
                    c.psA[:, n * 512:(n + 1) * 512], c.wqkvT[:, k, qcol:qcol + 128],
                    ht[:, n * 512:(n + 1) * 512], start=(k == 0), stop=(k == DK - 1)))
                em("t", lambda e, ht=ht, n=n, k=k, kcol=kcol: e.matmul(
                    c.psA[:, 1024 + n * 512: 1024 + (n + 1) * 512], c.wqkvT[:, k, kcol:kcol + 128],
                    ht[:, n * 512:(n + 1) * 512], start=(k == 0), stop=(k == DK - 1)))
            for n in range(2):
                ps = c.psB if n == 0 else c.psC
                em("t", lambda e, ht=ht, n=n, k=k, vcol=vcol, ps=ps: e.matmul(
                    ps[:, :], c.wqkvT[:, k, vcol:vcol + 128],
                    ht[:, n * 512:(n + 1) * 512], start=(k == 0), stop=(k == DK - 1)))
            mm_tok[i] = r.mark("t", "st")
        r.wait("s", "st", mm_tok[r8 * DK + DK - 1])
        r.wait("v", "st", mm_tok[r8 * DK + DK - 1])
        for n in range(2):
            em("s", lambda e, r8=r8, n=n, Q0=Q0: e.activation(
                AR[:, Q0 + r8 * 1024 + n * 512: Q0 + r8 * 1024 + (n + 1) * 512],
                c.psA[:, n * 512:(n + 1) * 512], ACT.Copy))
            em("s", lambda e, r8=r8, n=n, K0=K0: e.activation(
                AR[:, K0 + r8 * 1024 + n * 512: K0 + r8 * 1024 + (n + 1) * 512],
                c.psA[:, 1024 + n * 512: 1024 + (n + 1) * 512], ACT.Copy))
        evac_sc = r.mark("s", "sc")
        em("v", lambda e: e.tensor_copy(c.lnq[:, 0, :], c.psB[:, :]))
        vfm = ev("v", "sv", lambda e: e.tensor_copy(c.lnq[:, 1, :], c.psC[:, :]))
        r.wait("t", "sv", vfm)
        psDbf2 = c.psD.ap().bitcast(BF16)
        psEbf2 = c.psE.ap().bitcast(BF16)
        for j in range(8):
            pbx = psDbf2 if j < 4 else psEbf2
            em("t", lambda e, j=j, pbx=pbx: e.transpose(
                pbx[:, (j % 4) * 128:(j % 4 + 1) * 128],
                c.lnq[:, j // 4, (j % 4) * 128:(j % 4 + 1) * 128], c.idb[:, :]))
        ttk = r.mark("t", "st")
        r.wait("v", "st", ttk)
        em("v", lambda e, r8=r8, V0=V0, psDbf2=psDbf2: e.tensor_copy(
            AR[:, V0 + r8 * 1024: V0 + r8 * 1024 + 512], psDbf2[:, 0:512]))
        evac_tok = ev("v", "sv", lambda e, r8=r8, V0=V0, psEbf2=psEbf2: e.tensor_copy(
            AR[:, V0 + r8 * 1024 + 512: V0 + (r8 + 1) * 1024], psEbf2[:, 0:512]))
        r.wait("t", "sv", evac_tok)
    c.qkv_prod_done = (evac_tok, evac_sc)
    c.h_i += 8 * DK
    c.h_lastmm = mm_tok[8 * DK - 1]


def emit_stabilizer(c, hd, Q0, K0):
    r, em, ev, dma = c.r, c.em, c.ev, c.dma
    AR = c.arena
    r.wait("s", "sv", c.qkv_prod_done[0])
    r.wait("t", "sv", r.val("sv"))
    r.wait("t", "sc", r.val("sc"))
    r.wait("s", "st", r.val("st"))
    sq_tok = {}
    for j, base in enumerate((Q0, K0)):
        for k in range(16):
            kk = j * 16 + k
            if kk >= 2:
                r.wait("s", "st", sq_tok[kk - 2])
            vs = ev("s", "sc", lambda e, base=base, k=k: e.activation(
                c.lnq[:, k % 2, :], AR[:, base + k * 512: base + (k + 1) * 512], ACT.Square))
            r.wait("t", "sc", vs)
            sq_tok[kk] = ev("t", "st", lambda e, k=k: e.matmul(
                c.psE[0:1, :], c.onec[:, :], c.lnq[:, k % 2, :], start=True, stop=True))
            r.wait("v", "st", sq_tok[kk])
            ev("v", "sv", lambda e, j=j, k=k: e.reduce_max(
                c.smx[0:1, 32 + j * 16 + k: 33 + j * 16 + k], c.psE[0:1, :], axis=AXX))
            r.wait("t", "sv", r.val("sv"))
    em("v", lambda e: e.drain())
    em("v", lambda e: e.reduce_max(c.smx[0:1, 60:61], c.smx[0:1, 32:48], axis=AXX))
    em("v", lambda e: e.drain())
    em("v", lambda e: e.reduce_max(c.smx[0:1, 61:62], c.smx[0:1, 48:60], axis=AXX) if False else
       e.reduce_max(c.smx[0:1, 61:62], c.smx[0:1, 48:64], axis=AXX))
    em("v", lambda e: e.drain())
    em("v", lambda e: e.tensor_mul(c.smx[0:1, 62:63], c.smx[0:1, 60:61], c.smx[0:1, 61:62]))
    vv = ev("v", "sv", lambda e: e.drain())
    r.wait("s", "sv", vv)
    vs = ev("s", "sc", lambda e: e.activation(c.smx[0:1, 62:63], c.smx[0:1, 62:63], ACT.Sqrt))
    r.wait("v", "sc", vs)
    vv = ev("v", "sv", lambda e: e.tensor_scalar_mul(c.smx[0:1, 63:64], c.smx[0:1, 62:63], -RS128))
    r.wait("t", "sv", vv)
    em("t", lambda e: e.matmul(c.psE[:, 12:13], c.onerf[:, :], c.smx[0:1, 63:64], start=True, stop=True))
    tok = r.mark("t", "st")
    r.wait("v", "st", tok)
    c.bias_done = ev("v", "sv", lambda e: e.tensor_copy(c.smx2[:, 10:11], c.psE[:, 12:13]))


def emit_attn_bands(c, hd, Q0, K0, V0, PT, ZST):
    r, em, ev, dma = c.r, c.em, c.ev, c.dma
    AR = c.arena
    r.wait("t", "sv", c.bias_done)
    r.wait("s", "sv", c.bias_done)
    if not hasattr(c, "z_sem"):
        c.z_sem = c.r.fresh("zs")
        c.lnt1_guard = None
    zsem = c.z_sem
    prev_band_done = c.z_prev if hasattr(c, "z_prev") else 0
    prev_vadds = c.band_prev_vadds if hasattr(c, "band_prev_vadds") else None
    for J in range(16):
        nkt = 4 * J + 4
        exp_toks = {}
        vmsk_toks = {}
        vadd_toks = {}
        av_last = 0
        for step in range(nkt + 2):
            kt = step
            if kt < nkt:
                gpar = (kt // 2) % 2
                slot = kt % 4
                if kt >= 4:
                    r.wait("t", "sc", exp_toks[kt - 4])
                em("t", lambda e, kt=kt, J=J, gpar=gpar, K0=K0, Q0=Q0: e.matmul(
                    c.psA[:, gpar * 1024 + (kt % 2) * 512: gpar * 1024 + (kt % 2 + 1) * 512],
                    AR[:, K0 + kt * 128: K0 + (kt + 1) * 128],
                    AR[:, Q0 + J * 512: Q0 + (J + 1) * 512], start=True, stop=True))
                stok = r.mark("t", "st")
                r.wait("s", "st", stok)
                if kt >= 4:
                    r.wait("s", "sv", vadd_toks[kt - 4])
                elif prev_vadds is not None:
                    r.wait("s", "sv", prev_vadds[kt])
                exp_toks[kt] = ev("s", "sc", lambda e, kt=kt, gpar=gpar, slot=slot: e.activation(
                    AR[:, PT + slot * 512: PT + (slot + 1) * 512],
                    c.psA[:, gpar * 1024 + (kt % 2) * 512: gpar * 1024 + (kt % 2 + 1) * 512],
                    ACT.Exp, bias=c.smx2[:, 10:11], scale=RS128))
                r.wait("v", "sc", exp_toks[kt])
                if kt >= 4 * J:
                    rrel = kt - 4 * J
                    if rrel:
                        em("v", lambda e, slot=slot, rrel=rrel: e.memset(
                            AR[:, PT + slot * 512: PT + slot * 512 + rrel * 128], 0.0))
                    em("v", lambda e, slot=slot, rrel=rrel: e.tensor_tensor(
                        AR[:, PT + slot * 512 + rrel * 128: PT + slot * 512 + (rrel + 1) * 128],
                        AR[:, PT + slot * 512 + rrel * 128: PT + slot * 512 + (rrel + 1) * 128],
                        c.mtri[:, :], ALU.mult))
                    vmsk_toks[kt] = ev("v", "sv", lambda e: e.drain())
                if kt == 0:
                    em("v", lambda e, slot=slot: e.tensor_copy(
                        c.psC[:, :], AR[:, PT + slot * 512: PT + (slot + 1) * 512]))
                else:
                    em("v", lambda e, slot=slot: e.tensor_tensor(
                        c.psC[:, :], c.psC[:, :],
                        AR[:, PT + slot * 512: PT + (slot + 1) * 512], ALU.add))
                vadd_toks[kt] = ev("v", "sv", lambda e: e.drain())
            j = step - 2
            if 0 <= j < nkt:
                slot = j % 4
                if j >= 4 * J:
                    r.wait("t", "sv", vmsk_toks[j])
                else:
                    r.wait("t", "sc", exp_toks[j])
                em("t", lambda e, j=j, slot=slot, V0=V0, nkt=nkt: e.matmul(
                    c.psB[:, :], AR[:, V0 + j * 128: V0 + (j + 1) * 128],
                    AR[:, PT + slot * 512: PT + (slot + 1) * 512],
                    start=(j == 0), stop=(j == nkt - 1)))
                av_last = r.mark("t", "st")
        r.wait("v", "st", av_last)
        if prev_band_done:
            r.wait("v", zsem, prev_band_done)
        zv = ev("v", "sv", lambda e: e.tensor_copy(c.lnq[:, 0, :], c.psB[:, :]))
        if c.lnt1_guard is not None:
            r.wait("v", c.lnt1_guard[0], c.lnt1_guard[1])
        vsum = ev("v", "sv", lambda e: e.tensor_copy(c.lnt[:, 1, :], c.psC[:, :]))
        em("v", lambda e: e.drain())
        r.wait("t", "sv", vsum)
        psDbf = c.psD.ap().bitcast(BF16)
        for j in range(4):
            em("t", lambda e, j=j, psDbf=psDbf: e.transpose(
                psDbf[:, j * 128:(j + 1) * 128], c.lnq[:, 0, j * 128:(j + 1) * 128], c.idb[:, :]))
        em("t", lambda e: e.matmul(c.psE[0:1, 0:512], c.onec[:, :], c.lnt[:, 1, :],
                                   start=True, stop=True))
        ttok = r.mark("t", "st")
        c.lnt1_guard = ("st", ttok)
        r.wait("v", "st", ttok)
        vv1 = ev("v", "sv", lambda e: e.tensor_copy(c.rowscr[:, 0:512], c.psE[0:1, 0:512]))
        r.wait("t", "sv", vv1)
        for j in range(4):
            em("t", lambda e, j=j: e.transpose(c.psE[:, j:j + 1], c.rowscr[0:1, j * 128:(j + 1) * 128],
                                               c.idf[0:1, 0:1]))
        ttok2 = r.mark("t", "st")
        r.wait("v", "st", ttok2)
        vv = ev("v", "sv", lambda e: e.reciprocal(c.smx[:, 24:28], c.psE[:, 0:4]))
        r.wait("s", "sv", vv)
        for j in range(4):
            em("s", lambda e, j=j, psDbf=psDbf: e.activation(
                c.lnt[:, 0, j * 128:(j + 1) * 128], psDbf[:, j * 128:(j + 1) * 128],
                ACT.Identity, bias=0.0, scale=c.smx[:, 24 + j:25 + j]))
        sevt = r.mark("s", "sc")
        r.wait("s", "sc", sevt)
        dst = c.zloc0.ap()[J * 512:(J + 1) * 512, hd * 128:(hd + 1) * 128].rearrange(
            "(j p) f -> p j f", p=128)
        dma("s", zsem, dst, c.lnt[:, 0, :].rearrange("p (j f) -> p j f", j=4))
        prev_band_done = r.val(zsem)
        prev_vadds = [vadd_toks[nkt - 4 + i] for i in range(4)]
        r.wait("t", "sv", vv)
        r.wait("t", "sc", sevt)
    c.z_prev = prev_band_done
    c.band_prev_vadds = prev_vadds
    c.attn_done = (zsem, prev_band_done)


def emit_stage_wilr(c):
    r, em, ev, dma = c.r, c.em, c.ev, c.dma
    ilsem = r.fresh("il", "gdma")
    if c.stg_gate is not None:
        r.wait("g", c.stg_gate[0], c.stg_gate[1])
    dma("g", ilsem, c.stg[0:4, 0, :], c.l1qkv[768:772, 0:1024])
    dma("g", ilsem, c.stg[0:4, 1, :], c.l1qkv[768:772, 1024:2048])
    r.wait("t", "sv", r.val("sv"))
    r.wait("t", "sc", r.val("sc"))
    r.wait("t", ilsem, r.val(ilsem))
    psDbf = c.psD.ap().bitcast(BF16)
    for k in range(DK):
        half, off = divmod(k, 8)
        em("t", lambda e, half=half, off=off, k=k, psDbf=psDbf: e.transpose(
            psDbf[0:128, k * 4:(k + 1) * 4], c.stg[0:4, half, off * 128:(off + 1) * 128], c.idb[0:4, 0:4]))
    tok = r.mark("t", "st")
    r.wait("v", "st", tok)
    vv = ev("v", "sv", lambda e, psDbf=psDbf: e.tensor_copy(
        c.wqkvT[:, 0:DK, 768:772], psDbf[:, 0:64].rearrange("p (k f) -> p k f", f=4)))
    r.wait("t", "sv", vv)
    blsem = r.fresh("bl", "gdma")
    dma("g", blsem, c.smx[0:1, 28:32], c.bilr_in[:, :])
    r.wait("t", blsem, r.val(blsem))
    em("t", lambda e: e.matmul(c.psE[:, 16:20], c.onerf[:, :], c.smx[0:1, 28:32], start=True, stop=True))
    tok = r.mark("t", "st")
    r.wait("v", "st", tok)
    c.bilr_done = ev("v", "sv", lambda e: e.tensor_copy(c.smx2[:, 12:16], c.psE[:, 16:20]))
    c.stg_gate = ("st", r.val("st"))
def emit_stage_woT(c, wo_s, woT_dram):
    r, em, ev, dma = c.r, c.em, c.ev, c.dma
    psb = [c.psD.ap().bitcast(BF16), c.psE.ap().bitcast(BF16)]
    if not hasattr(c, "woT_sems"):
        c.woT_sems = [r.fresh("wo", "gdma"), r.fresh("wo", "gdma")]
        c.woT_osem = [r.fresh("woo"), r.fresh("woo")]
        c.woT_g = None
    wsem = c.woT_sems
    osem = c.woT_osem
    if c.stg_gate is not None:
        r.wait("g", c.stg_gate[0], c.stg_gate[1])
    if c.woT_g is not None:
        r.wait("g", "sv", c.woT_g)
    r.wait("t", "sv", r.val("sv"))
    r.wait("t", "sc", r.val("sc"))
    ub = c.ubuf.ap().bitcast(BF16)
    if c.ubuf_guard:
        for _tk in c.ubuf_guard:
            r.wait("v", _tk[0], _tk[1])
    evtoks = {}
    for rt in range(DK):
        buf = rt % 2
        sem = wsem[buf]
        if rt >= 2:
            r.wait("g", "sv", evtoks[rt - 2])
        dma("g", sem, c.stg[:, buf, 0:256], wo_s[ts(rt), :])
        r.wait("t", sem, r.val(sem))
        if rt >= 2:
            r.wait("t", "sv", evtoks[rt - 2])
        pb = psb[rt % 2]
        for ci in range(2):
            em("t", lambda e, buf=buf, ci=ci, pb=pb: e.transpose(
                pb[:, ci * 128:(ci + 1) * 128], c.stg[:, buf, ci * 128:(ci + 1) * 128], c.idb[:, :]))
        tok = r.mark("t", "st")
        r.wait("v", "st", tok)
        if rt >= 2:
            r.wait("v", osem[rt % 2], dmatoks[rt - 2])
        evtoks[rt] = ev("v", "sv", lambda e, pb=pb, buf=buf, ub=ub: e.tensor_copy(
            ub[:, buf, 0:256], pb[:, 0:256]))
        r.wait("s", "sv", evtoks[rt])
        if rt == 0:
            dmatoks = {}
        dmatoks[rt] = dma("s", osem[rt % 2], woT_dram.ap()[:, rt * 128:(rt + 1) * 128].rearrange(
            "(a p) f -> p a f", p=128), ub[:, buf, 0:256].rearrange("p (a f) -> p a f", a=2))
    c.woT_stage_done = [(osem[0], r.val(osem[0])), (osem[1], r.val(osem[1]))]
    c.ubuf_guard = list(c.woT_stage_done)
    c.woT_g = r.val("sv")
    c.stg_gate = ("st", r.val("st"))


def emit_wo_partial(c, zloc, woT_dram, zloc_ready, L, gi_gate):
    r, em, ev, dma = c.r, c.em, c.ev, c.dma
    AR = c.arena
    ga = c.gsl(L, gi_gate)
    r.wait("y", "sc", r.val("sc"))
    r.wait("y", "st", r.val("st"))
    r.wait("y", "sv", r.val("sv"))
    if not hasattr(c, "wop_zl"):
        c.wop_zl = r.fresh("zl")
        c.wop_wl = r.fresh("wl")
        c.wop_ps = [r.fresh("pp"), r.fresh("pp")]
        c.wop_rs = r.fresh("rs")
        c.wop_gate = None
    zl = c.wop_zl
    r.wait("y", zloc_ready[0], zloc_ready[1])
    if c.wop_gate is not None:
        r.wait("y", c.wop_gate[0], c.wop_gate[1])
    for fchunk in range(2):
        for tg in range(8):
            dma("y", zl, AR[:, fchunk * 8192 + tg * 1024: fchunk * 8192 + (tg + 1) * 1024],
                zloc.ap()[tg * 1024:(tg + 1) * 1024, fchunk * 128:(fchunk + 1) * 128],
                transpose=True)
    r.wait("t", zl, r.val(zl))
    woT = AR[:, 16384:20480].rearrange("p (a f) -> p a f", a=2)
    wl = c.wop_wl
    for tkn in c.woT_stage_done:
        r.wait("y", tkn[0], tkn[1])
    dma("y", wl, woT, woT_dram.ap().rearrange("(a p) f -> p a f", p=128))
    r.wait("t", wl, r.val(wl))
    r.wait("t", "sv", r.val("sv"))
    r.wait("t", "sc", r.val("sc"))
    psem = c.wop_ps
    if c.ubuf_guard:
        for _tk in c.ubuf_guard:
            r.wait("s", _tk[0], _tk[1])
    ev_toks = {}
    st_toks = {}
    for tt in range(64):
        for n in range(4):
            psx = c.psA[:, ((tt * 4 + n) % 2) * 1024 + (n % 2) * 512:
                        ((tt * 4 + n) % 2) * 1024 + (n % 2 + 1) * 512]
            idx = tt * 4 + n
            if idx >= 2:
                r.wait("t", "sc", ev_toks[idx - 2])
            for k in range(2):
                em("t", lambda e, tt=tt, n=n, k=k, psx=psx, woT=woT: e.matmul(
                    psx, AR[:, k * 8192 + tt * 128:k * 8192 + (tt + 1) * 128],
                    woT[:, k, n * 512:(n + 1) * 512], start=(k == 0), stop=(k == 1)))
            st_toks[idx] = r.mark("t", "st")
            r.wait("s", "st", st_toks[idx])
            if idx >= 2:
                r.wait("s", psem[(idx % 2)], part_dma[idx - 2])
            ev_toks[idx] = ev("s", "sc", lambda e, tt=tt, n=n, psx=psx: e.activation(
                c.ubuf[:, n % 2, :].bitcast(BF16)[:, 0:512], psx, ACT.Copy))
            r.wait("y", "sc", ev_toks[idx])
            if idx == 0:
                part_dma = {}
            part_dma[idx] = dma("y", psem[idx % 2],
                c.part.ap()[tt * 128:(tt + 1) * 128, n * 512:(n + 1) * 512],
                c.ubuf[:, n % 2, :].bitcast(BF16)[:, 0:512])
    r.wait("g", psem[0], r.val(psem[0]))
    r.wait("g", psem[1], r.val(psem[1]))
    cctok = c.cc(lambda e: e.collective_compute(
        "ReduceScatter", ALU.add, replica_groups=c.RG,
        ins=[c.part.ap().opt()], outs=[c.rsout.ap().opt()]))
    c.waitcc("y", cctok)
    rsem = c.wop_rs
    for k in range(DK):
        if k >= 1:
            r.wait("y", "sv", rs_toks[k - 1])
        dma("y", rsem, c.lnt[:, 0:2, :].rearrange("p a b -> p (a b)"),
            c.rsout.ap()[:, k * 128:(k + 1) * 128], transpose=True)
        r.wait("v", rsem, r.val(rsem))
        if k == 0:
            rs_toks = {}
        rs_toks[k] = ev("v", "sv", lambda e, k=k, ga=ga: e.scalar_tensor_tensor(
            c.xres[:, k * TC:(k + 1) * TC], c.lnt[:, 0:2, :].rearrange("p a b -> p (a b)"),
            ga[:, k:k + 1], c.xres[:, k * TC:(k + 1) * TC], ALU.mult, ALU.add))
    c.wo_done = r.val("sv")
    c.ubuf_guard = [("sc", r.val("sc"))]
    c.wop_gate = ("sv", r.val("sv"))


def emit_phase_moe(c):
    r, em, ev, dma = c.r, c.em, c.ev, c.dma
    AR = c.arena
    ln2_done = emit_ln(c, 0, 3, 4, c.hbuf, after=[("sv", c.wo_done)])

    wgsem = r.fresh("wg", "gdma")
    if c.stg_gate is not None:
        r.wait("g", c.stg_gate[0], c.stg_gate[1])
    dma("g", wgsem, c.stg[0:8, 0, :], c.wg_in[:, 0:1024])
    dma("g", wgsem, c.stg[0:8, 1, :], c.wg_in[:, 1024:2048])
    r.wait("t", "sv", r.val("sv"))
    r.wait("t", "sc", r.val("sc"))
    r.wait("t", wgsem, r.val(wgsem))
    psDbf = c.psD.ap().bitcast(BF16)
    for k in range(DK):
        half, off = divmod(k, 8)
        em("t", lambda e, half=half, off=off, k=k, psDbf=psDbf: e.transpose(
            psDbf[:, k * 8:(k + 1) * 8], c.stg[0:8, half, off * 128:(off + 1) * 128], c.idb[0:8, 0:8]))
    tok = r.mark("t", "st")
    r.wait("v", "st", tok)
    vv = ev("v", "sv", lambda e, psDbf=psDbf: e.tensor_copy(c.wgT[:, :], psDbf[:, 0:128]))
    r.wait("t", "sv", vv)
    c.stg_gate = ("st", r.val("st"))

    r.wait("t", "sc", ln2_done)
    if c.ubuf_guard:
        for _tk in c.ubuf_guard:
            r.wait("v", _tk[0], _tk[1])
    for n in range(2):
        ps = c.psB if n == 0 else c.psC
        for k in range(DK):
            em("t", lambda e, n=n, k=k, ps=ps: e.matmul(
                ps[0:8, :], c.wgT[:, k * 8:(k + 1) * 8],
                c.hbuf[:, k * TC + n * 512: k * TC + (n + 1) * 512],
                start=(k == 0), stop=(k == DK - 1)))
    tok = r.mark("t", "st")
    r.wait("v", "st", tok)
    em("v", lambda e: e.tensor_copy(c.ubuf[0:8, 0, :], c.psB[0:8, :]))
    vv = ev("v", "sv", lambda e: e.tensor_copy(c.ubuf[0:8, 1, :], c.psC[0:8, :]))
    r.wait("t", "sv", vv)
    for cch in range(8):
        half, off = divmod(cch, 4)
        em("t", lambda e, half=half, off=off, cch=cch: e.transpose(
            c.psD[:, cch * 8:(cch + 1) * 8], c.ubuf[0:8, half, off * 128:(off + 1) * 128],
            c.idf[0:8, 0:8]))
    tok = r.mark("t", "st")
    r.wait("v", "st", tok)
    vv = ev("v", "sv", lambda e: e.tensor_copy(c.smx[:, 0:64], c.psD[:, 0:64]))
    r.wait("t", "sv", vv)
    em("v", lambda e: e.drain())

    s3 = c.smx[:, 0:64].rearrange("p (c e) -> p c e", e=8)
    m1 = c.ubuf[:, 0, 0:8]
    b2 = c.ubuf[:, 0, 8:16]
    t64 = c.ubuf[:, 1, 0:64]
    t64r = t64.rearrange("p (c e) -> p c e", e=8)
    d64 = c.ubuf[:, 0, 64:128]
    em("v", lambda e: e.reduce_max(m1, s3, axis=AXX))
    em("v", lambda e: e.drain())
    em("v", lambda e: e.tensor_tensor(s3, s3, m1.rearrange("p c -> p c ()").broadcast_to([128, 8, 8]), ALU.subtract))
    vv = ev("v", "sv", lambda e: e.drain())
    r.wait("s", "sv", vv)
    vs = ev("s", "sc", lambda e: e.activation(d64, c.smx[:, 0:64], ACT.Relu, bias=1.0, scale=1e30))
    r.wait("v", "sc", vs)
    em("v", lambda e: e.tensor_scalar_min(d64, d64, 1.0))
    em("v", lambda e: e.drain())
    em("v", lambda e: e.scalar_tensor_tensor(t64, d64, -1e4, c.smx[:, 0:64], ALU.mult, ALU.add))
    em("v", lambda e: e.drain())
    em("v", lambda e: e.reduce_max(b2, t64r, axis=AXX))
    em("v", lambda e: e.drain())
    em("v", lambda e: e.tensor_tensor(t64r, s3, b2.rearrange("p c -> p c ()").broadcast_to([128, 8, 8]), ALU.subtract))
    vv = ev("v", "sv", lambda e: e.drain())
    r.wait("s", "sv", vv)
    vs = ev("s", "sc", lambda e: e.activation(t64, t64, ACT.Relu, bias=1.0, scale=1e30))
    em("s", lambda e: e.activation(c.smx[:, 0:64], c.smx[:, 0:64], ACT.Exp))
    vs = ev("s", "sc", lambda e: e.activation(b2, b2, ACT.Exp))
    r.wait("v", "sc", vs)
    em("v", lambda e: e.tensor_scalar_min(t64, t64, 1.0))
    em("v", lambda e: e.drain())
    em("v", lambda e: e.tensor_scalar_add(b2, b2, 1.0))
    em("v", lambda e: e.drain())
    em("v", lambda e: e.reciprocal(b2, b2))
    em("v", lambda e: e.drain())
    em("v", lambda e: e.tensor_mul(c.wmoe[:, :], t64, c.smx[:, 0:64]))
    em("v", lambda e: e.drain())
    em("v", lambda e: e.tensor_tensor(c.wmoe.ap().rearrange("p (cc ee) -> p cc ee", ee=8),
                                      c.wmoe.ap().rearrange("p (cc ee) -> p cc ee", ee=8),
                                      b2.rearrange("p c -> p c ()").broadcast_to([128, 8, 8]), ALU.mult))
    w_done = ev("v", "sv", lambda e: e.drain())
    c.ubuf_guard = [("sv", w_done)]
    if DEBUG and "dbg_w" in c.dbg:
        wds = r.fresh("wd")
        r.wait("y", "sv", w_done)
        dma("y", wds, c.dbg["dbg_w"].ap(), c.wmoe[:, :])
        r.wait("y", wds, r.val(wds))

    P = c.wqkvT.ap().rearrange("p k f -> p (k f)")
    Pv = P[:, 0:11264].rearrange("p (h t) -> p h t", t=1024)
    YB = AR[:, 0:16384].rearrange("p (tt f) -> p tt f", f=2048)
    msem = [r.fresh("mo"), r.fresh("mo"), r.fresh("mo"), r.fresh("mo")]
    w3sem = [r.fresh("w3"), r.fresh("w3")]
    r.wait("y", "st", r.val("st"))
    r.wait("y", "sc", r.val("sc"))
    r.wait("y", "sv", r.val("sv"))
    slabs = [(i, 1) for i in range(HMC)]
    pe_done = {}
    w3_done = {}
    for e_ in range(NE):
        c.waitcc("y", c.cc_moe[e_][0])
        c.waitcc("y", c.cc_moe[e_][1])
        r.release(c.cc_moe[e_][0][0]); r.release(c.cc_moe[e_][1][0])
        for si, (h0, hn) in enumerate(slabs):
            sidx = e_ * HMC + si
            bpar = sidx % 2
            if si < 2 and e_ > 0:
                r.wait("y", "st", w3_done[e_ * 8 - 1])
            if sidx >= 2:
                r.wait("y", "st", pe_done[sidx - 2])
            dma("y", msem[bpar * 2], AR[:, 16384 + bpar * 2048: 16384 + (bpar + 1) * 2048].rearrange(
                "p (k h) -> p k h", k=DK),
                c.ago_m1[e_].ap().rearrange("(k p) h -> p k h", p=128)[:, :, h0 * 128:(h0 + hn) * 128])
            dma("y", msem[bpar * 2 + 1],
                AR[:, 20480 + bpar * 2048: 20480 + (bpar + 1) * 2048].rearrange(
                "p (k h) -> p k h", k=DK),
                c.ago_m2[e_].ap().rearrange("(k p) h -> p k h", p=128)[:, :, h0 * 128:(h0 + hn) * 128])
            r.wait("t", msem[bpar * 2], r.val(msem[bpar * 2]))
            r.wait("t", msem[bpar * 2 + 1], r.val(msem[bpar * 2 + 1]))
            for hh in range(hn):
                hc = h0 + hh
                for n in range(2):
                    pgrp = (hc * 2 + n) % 2
                    ps1 = c.psB if pgrp == 0 else c.psD
                    ps2 = c.psC if pgrp == 0 else c.psE
                    if True:
                        r.wait("t", "sc", r.val("sc")) if False else None
                    for k in range(DK):
                        em("t", lambda e, bpar=bpar, k=k, n=n, ps1=ps1: e.matmul(
                            ps1[:, :],
                            AR[:, 16384 + bpar * 2048:16384 + (bpar + 1) * 2048].rearrange(
                                "p (kk h) -> p kk h", kk=DK)[:, k, :],
                            c.hbuf[:, k * TC + n * 512: k * TC + (n + 1) * 512],
                            start=(k == 0), stop=(k == DK - 1)))
                    for k in range(DK):
                        em("t", lambda e, bpar=bpar, k=k, n=n, ps2=ps2: e.matmul(
                            ps2[:, :],
                            AR[:, 20480 + bpar * 2048:20480 + (bpar + 1) * 2048].rearrange(
                                "p (kk h) -> p kk h", kk=DK)[:, k, :],
                            c.hbuf[:, k * TC + n * 512: k * TC + (n + 1) * 512],
                            start=(k == 0), stop=(k == DK - 1)))
                    mmk = r.mark("t", "st")
                    r.wait("s", "st", mmk)
                    vs = ev("s", "sc", lambda e, pgrp=pgrp, ps1=ps1: e.activation(
                        c.lnq[:, pgrp, :], ps1[:, :], ACT.Silu))
                    r.wait("v", "sc", vs)
                    pvv = ev("v", "sv", lambda e, pgrp=pgrp, hc=hc, n=n, ps2=ps2: e.tensor_tensor(
                        Pv[:, hc, n * 512:(n + 1) * 512], c.lnq[:, pgrp, :], ps2[:, :], ALU.mult))
                    r.wait("t", "sv", pvv)
            pe_done[sidx] = r.val("st")
        c.waitcc("y", c.cc_moe[e_][2])
        r.release(c.cc_moe[e_][2][0])
        for nh in range(8):
            widx = e_ * 8 + nh
            wpar = widx % 2
            if nh == 0:
                r.wait("y", "st", pe_done[e_ * HMC + HMC - 1])
            if widx >= 2:
                r.wait("y", "st", w3_done[widx - 2])
            dma("y", w3sem[wpar], AR[:, 16384 + wpar * 2816: 16384 + (wpar + 1) * 2816].rearrange(
                "p (k d) -> p k d", k=HMC),
                c.ago_m3[e_].ap().rearrange("(k p) d -> p k d", p=128)[:, :, nh * 256:(nh + 1) * 256])
            r.wait("t", w3sem[wpar], r.val(w3sem[wpar]))
            for mg in range(2):
                for mm_ in range(4):
                    tt = mg * 4 + mm_
                    psx = c.psA[:, mm_ * 512: mm_ * 512 + 256]
                    for k in range(HMC):
                        em("t", lambda e, wpar=wpar, k=k, tt=tt, psx=psx: e.matmul(
                            psx, Pv[:, k, tt * 128:(tt + 1) * 128],
                            AR[:, 16384 + wpar * 2816:16384 + (wpar + 1) * 2816].rearrange(
                                "p (kk d) -> p kk d", kk=HMC)[:, k, :],
                            start=(k == 0), stop=(k == HMC - 1)))
                    mmk = r.mark("t", "st")
                    r.wait("v", "st", mmk)
                    wcol = c.wmoe[:, tt * 8 + e_: tt * 8 + e_ + 1]
                    if e_ == 0:
                        lv = ev("v", "sv", lambda e, tt=tt, nh=nh, psx=psx, wcol=wcol: e.tensor_scalar(
                            YB[:, tt, nh * 256:(nh + 1) * 256], psx, wcol, None, ALU.mult))
                    else:
                        lv = ev("v", "sv", lambda e, tt=tt, nh=nh, psx=psx, wcol=wcol: e.scalar_tensor_tensor(
                            YB[:, tt, nh * 256:(nh + 1) * 256], psx, wcol,
                            YB[:, tt, nh * 256:(nh + 1) * 256], ALU.mult, ALU.add))
                    r.wait("t", "sv", lv)
            w3_done[widx] = r.val("st")
    gb = c.gsl(0, 5)
    ydone = r.val("sv")
    r.wait("t", "sv", ydone)
    psDbf = c.psD.ap().bitcast(BF16)
    psEbf = c.psE.ap().bitcast(BF16)
    for dc in range(DK):
        for grp in range(2):
            pb = psDbf if grp == 0 else psEbf
            for j in range(4):
                tt = grp * 4 + j
                em("t", lambda e, dc=dc, tt=tt, j=j, pb=pb: e.transpose(
                    pb[:, j * 128:(j + 1) * 128], YB[:, tt, dc * 128:(dc + 1) * 128], c.idb[:, :]))
            tok = r.mark("t", "st")
            r.wait("v", "st", tok)
            vv = ev("v", "sv", lambda e, dc=dc, grp=grp, pb=pb, gb=gb: e.scalar_tensor_tensor(
                c.xres[:, dc * TC + grp * 512: dc * TC + (grp + 1) * 512], pb[:, 0:512],
                gb[:, dc:dc + 1], c.xres[:, dc * TC + grp * 512: dc * TC + (grp + 1) * 512],
                ALU.mult, ALU.add))
            r.wait("t", "sv", vv)
    c.moe_done = r.val("sv")


def emit_phase_l1attn(c):
    r, em, ev, dma = c.r, c.em, c.ev, c.dma
    AR = c.arena
    ln3_done = emit_ln(c, 1, 0, 1, c.hbuf, after=[("sv", c.moe_done)])
    r.wait("g", "sc", ln3_done)
    r.wait("g", "sg", r.val("sg"))
    for k in range(DK):
        dma("g", "sg", c.agi_h[ts(k), :], c.hbuf[:, k * TC:(k + 1) * TC])
    r.wait("g", "sg", r.val("sg"))
    c.cc_h1 = c.cc(lambda e: e.collective_compute(
        "AllGather", ALU.bypass, replica_groups=c.RG,
        ins=[c.agi_h.ap().opt()], outs=[c.ago_h.ap().opt()]))

    emit_stage_qkvT(c, c.l1qkv, 768)
    emit_stage_wilr(c)

    emit_stage_woT(c, c.l1wo_s, c.wo1T)

    if SUBPH >= 2:
        emit_fp8_stage(c)

    if SUBPH >= 3:
        emit_l1_scan(c)

    if SUBPH >= 4:
        emit_wo_partial(c, c.zloc1, c.wo1T, c.scan_done, L=1, gi_gate=2)


def emit_fp8_stage(c):
    r, em, ev, dma = c.r, c.em, c.ev, c.dma
    AR8 = c.arena.ap().bitcast(FP8)
    fsem = [r.fresh("f8", "gdma"), r.fresh("f8", "gdma")]
    osem = r.fresh("f8o")
    r.wait("t", "sv", r.val("sv"))
    r.wait("t", "sc", r.val("sc"))
    r.wait("v", "sv", c.rw_done)
    ev_t = {}
    for rt in range(64):
        buf = rt % 2
        if rt >= 2:
            r.wait("g", "st", ev_t[rt - 2])
        dma("g", fsem[buf], c.stgf[:, buf, 0:256], c.wfc_s[ts(rt), :])
        r.wait("t", fsem[buf], r.val(fsem[buf]))
        if rt >= 2:
            r.wait("t", "sv", q_done[rt - 2])
        ps = c.psD if buf == 0 else c.psE
        for ci in range(2):
            em("t", lambda e, buf=buf, ci=ci, ps=ps: e.transpose(
                ps[:, ci * 128:(ci + 1) * 128], c.stgf[:, buf, ci * 128:(ci + 1) * 128], c.idf[:, :]))
        ev_t[rt] = r.mark("t", "st")
        r.wait("v", "st", ev_t[rt])
        for ci in range(2):
            vv = ev("v", "sv", lambda e, rt=rt, ci=ci, ps=ps: e.tensor_scalar(
                AR8[:, ci * 8192 + rt * 128: ci * 8192 + (rt + 1) * 128],
                ps[:, ci * 128:(ci + 1) * 128], c.smx2[:, 5:6], None, ALU.mult))
        if rt == 0:
            q_done = {}
        q_done[rt] = r.val("sv")
    r.wait("g", "sv", r.val("sv"))
    r.wait("g", "sg", r.val("sg"))
    for ci in range(2):
        dma("g", "sg", c.agi_fc8.ap()[ci * 128:(ci + 1) * 128, :], AR8[:, ci * 8192:(ci + 1) * 8192])
    r.wait("g", "sg", r.val("sg"))
    c.cc_fc8 = c.cc(lambda e: e.collective_compute(
        "AllGather", ALU.bypass, replica_groups=c.RG,
        ins=[c.agi_fc8.ap().opt()], outs=[c.ago_fc8.ap().opt()]))
    ev_t2 = {}
    q2 = {}
    for rt in range(16):
        buf = rt % 2
        if rt >= 2:
            r.wait("g", "st", ev_t2[rt - 2])
        dma("g", fsem[buf], c.stgf[:, buf, 0:1024], c.wproj_s[ts(rt), :])
        r.wait("t", fsem[buf], r.val(fsem[buf]))
        if rt >= 2:
            r.wait("t", "sv", q2[rt - 2])
        ps = c.psD if buf == 0 else c.psE
        for ci in range(8):
            if ci == 4:
                tk = r.mark("t", "st")
                r.wait("v", "st", tk)
                for cj in range(4):
                    vv = ev("v", "sv", lambda e, rt=rt, cj=cj, ps=ps: e.tensor_scalar(
                        AR8[:, cj * 2048 + rt * 128: cj * 2048 + (rt + 1) * 128],
                        ps[:, cj * 128:(cj + 1) * 128], c.smx2[:, 6:7], None, ALU.mult))
                r.wait("t", "sv", r.val("sv"))
            em("t", lambda e, buf=buf, ci=ci, ps=ps: e.transpose(
                ps[:, (ci % 4) * 128:(ci % 4 + 1) * 128], c.stgf[:, buf, ci * 128:(ci + 1) * 128], c.idf[:, :]))
        ev_t2[rt] = r.mark("t", "st")
        r.wait("v", "st", ev_t2[rt])
        for cj in range(4):
            vv = ev("v", "sv", lambda e, rt=rt, cj=cj, ps=ps: e.tensor_scalar(
                AR8[:, (4 + cj) * 2048 + rt * 128: (4 + cj) * 2048 + (rt + 1) * 128],
                ps[:, cj * 128:(cj + 1) * 128], c.smx2[:, 6:7], None, ALU.mult))
        q2[rt] = r.val("sv")
    r.wait("g", "sv", r.val("sv"))
    r.wait("g", "sg", r.val("sg"))
    for ci in range(8):
        dma("g", "sg", c.agi_pj8.ap()[ci * 128:(ci + 1) * 128, :], AR8[:, ci * 2048:(ci + 1) * 2048])
    r.wait("g", "sg", r.val("sg"))
    c.cc_pj8 = c.cc(lambda e: e.collective_compute(
        "AllGather", ALU.bypass, replica_groups=c.RG,
        ins=[c.agi_pj8.ap().opt()], outs=[c.ago_pj8.ap().opt()]))
def emit_l1_scan(c):
    r, em, ev, dma = c.r, c.em, c.ev, c.dma
    AR = c.arena
    ln3_sc = None
    SBB = [0, 3264]
    WS = 6528
    c.WS = WS
    HTB = WS + 4928
    arf = AR.ap().bitcast(F32)
    c.wst = arf[:, (WS + 5952) // 2:(WS + 5952) // 2 + 128]
    c.wsb2 = AR[:, WS + 6336: WS + 6592]
    wsem = r.fresh("w1l", "gdma")
    for h in range(4):
        dma("g", wsem, c.wst[(h % 2) * 64:(h % 2) * 64 + 64, (h // 2) * 64:(h // 2) * 64 + 64],
            c.w1h_in[h * 64:(h + 1) * 64, :])
    r.wait("v", wsem, r.val(wsem))
    if c.ubuf_guard:
        for _tk in c.ubuf_guard:
            r.wait("v", _tk[0], _tk[1])
    em("v", lambda e: e.memset(c.wsb2, 0.0))
    em("v", lambda e: e.drain())
    for h in range(4):
        em("v", lambda e, h=h: e.tensor_copy(
            c.wsb2[(h % 2) * 64:(h % 2) * 64 + 64,
                   (h // 2) * 128 + (h % 2) * 64:(h // 2) * 128 + (h % 2) * 64 + 64],
            c.wst[(h % 2) * 64:(h % 2) * 64 + 64, (h // 2) * 64:(h // 2) * 64 + 64]))
    wb_ready = ev("v", "sv", lambda e: e.drain())

    ssem = [r.fresh("sp"), r.fresh("sp")]
    zsem1 = r.fresh("zs1")
    c.zprev_scan = 0
    proj_done = {}

    def proj_strip(s):
        base = SBB[s % 2]
        r8, half = divmod(s, 2)
        r.wait("t", "sv", r.val("sv"))
        r.wait("t", "sc", r.val("sc"))
        ld_tok = {}
        mmk_tok = {}
        for k in range(DK):
            gk = s * DK + k
            hpar = gk % 2
            if k >= 2:
                r.wait("y", "st", mmk_tok[k - 2])
            elif s > 0:
                r.wait("y", "st", proj_done[s - 1])
            dma("y", ssem[hpar], AR[:, HTB + hpar * 512: HTB + (hpar + 1) * 512],
                c.ago_h.ap()[r8 * D + k * 128: r8 * D + (k + 1) * 128,
                             half * 512:(half + 1) * 512])
            r.wait("t", ssem[hpar], r.val(ssem[hpar]))
            ht = AR[:, HTB + hpar * 512: HTB + (hpar + 1) * 512]
            for mat, coloff, aroff in (("b", 256, 0), ("c", 0, 1024)):
                for g in range(2):
                    em("t", lambda e, k=k, g=g, coloff=coloff, aroff=aroff, ht=ht: e.matmul(
                        c.psA[:, aroff + g * 512: aroff + (g + 1) * 512],
                        c.wqkvT[:, k, coloff + g * 128: coloff + (g + 1) * 128],
                        ht, start=(k == 0), stop=(k == DK - 1)))
            psavv = [c.psB, c.psC, c.psD, c.psE]
            for sc_ in range(4):
                em("t", lambda e, k=k, sc_=sc_, ht=ht, psavv=psavv: e.matmul(
                    psavv[sc_][:, 0:260], ht[:, sc_ * 128:(sc_ + 1) * 128],
                    c.wqkvT[:, k, 512:772], start=(k == 0), stop=(k == DK - 1)))
            mmk_tok[k] = r.mark("t", "st")
        t_ = mmk_tok[DK - 1]
        r.wait("s", "st", t_)
        r.wait("v", "st", t_)
        for g in range(2):
            em("s", lambda e, g=g, base=base: e.activation(
                AR[:, base + g * 512: base + (g + 1) * 512],
                c.psA[:, g * 512:(g + 1) * 512], ACT.Copy))
            em("s", lambda e, g=g, base=base: e.activation(
                AR[:, base + 1024 + g * 512: base + 1024 + (g + 1) * 512],
                c.psA[:, 1024 + g * 512: 1024 + (g + 1) * 512], ACT.Copy))
        psavv = [c.psB, c.psC, c.psD, c.psE]
        for sc_ in range(4):
            em("s", lambda e, sc_=sc_, base=base, psavv=psavv: e.activation(
                AR[:, base + 2048 + sc_ * 256: base + 2048 + (sc_ + 1) * 256],
                psavv[sc_][:, 0:256], ACT.Copy))
            em("v", lambda e, sc_=sc_, psavv=psavv: e.tensor_tensor(
                c.ubuf[:, 1, sc_ * 4:(sc_ + 1) * 4], psavv[sc_][:, 256:260],
                c.smx2[:, 12:16], ALU.add))
        a_sc = r.mark("s", "sc")
        vv = ev("v", "sv", lambda e: e.drain())
        r.wait("s", "sv", vv)
        vs = ev("s", "sc", lambda e: e.activation(c.ubuf[:, 1, 0:16], c.ubuf[:, 1, 0:16], ACT.Sigmoid))
        r.wait("v", "sc", vs)
        em("v", lambda e: e.tensor_tensor(
            c.ubuf[:, 1, 0:16].rearrange("p (s f) -> p s f", f=4),
            c.ubuf[:, 1, 0:16].rearrange("p (s f) -> p s f", f=4),
            c.tkx.rearrange("p o -> p () o").broadcast_to([128, 4, 4]), ALU.mult))
        em("v", lambda e: e.drain())
        cof = AR.ap().bitcast(F32)
        cov = ev("v", "sv", lambda e, base=base, cof=cof: e.tensor_copy(
            cof[:, (base + 3072) // 2:(base + 3072) // 2 + 16], c.ubuf[:, 1, 0:16]))
        proj_done[s] = r.val("st")
        return cov

    r.wait("t", "sv", r.val("sv"))
    r.wait("t", "sc", r.val("sc"))
    r.wait("y", "st", r.val("st"))
    r.wait("y", "sc", r.val("sc"))
    r.wait("y", "sv", r.val("sv"))
    c.waitcc("y", c.cc_h1)
    r.wait("t", "sv", c.qkvT_done)
    r.wait("t", "sv", c.bilr_done)
    r.wait("t", "sv", wb_ready)
    cov = proj_strip(0)
    for s in range(16):
        cov_next = proj_strip(s + 1) if s < 15 else None
        base = SBB[s % 2]
        r.wait("v", "sv", cov)
        r.wait("t", "sv", cov)
        r.wait("t", "sc", r.val("sc"))
        for sc_ in range(4):
            emit_scan_sc(c, s, sc_, base, zsem1)
        cov = cov_next
    c.scan_done = (zsem1, c.zprev_scan)


def emit_scan_sc(c, s, sc_, base, zsem1):
    r, em, ev, dma = c.r, c.em, c.ev, c.dma
    AR = c.arena
    WS = c.WS
    LP, K1o, GUP, GBCN, DTT = WS, WS + 512, WS + 1024, WS + 1536, WS + 2048
    RPo, T1o, T2o, VVo, Z1o = WS + 2560, WS + 2816, WS + 3072, WS + 3328, WS + 3584
    BTMo, L2o, ZBo = WS + 3840, WS + 4096, WS + 4608
    sci = s * 4 + sc_
    if SCSTEP < 1:
        return
    cof = AR.ap().bitcast(F32)
    co4 = cof[:, (base + 3072) // 2 + sc_ * 4:(base + 3072) // 2 + (sc_ + 1) * 4]
    co4b = co4.rearrange("p f -> p f ()").broadcast_to([128, 4, 128])
    co4b64 = co4.rearrange("p f -> p f ()").broadcast_to([128, 4, 64])
    mBLb = c.mBL.ap().rearrange("p s2 -> p () s2").broadcast_to([128, 4, 128])
    mBUb = c.mBUn.ap().rearrange("p s2 -> p () s2").broadcast_to([128, 4, 128])
    mBDb = c.mBD.ap().rearrange("p s2 -> p () s2").broadcast_to([128, 4, 128])

    def bsl(h):
        g = h // 2
        return AR[(h % 2) * 64:(h % 2) * 64 + 64,
                  base + g * 512 + sc_ * 128: base + g * 512 + (sc_ + 1) * 128]

    def csl(h):
        g = h // 2
        return AR[(h % 2) * 64:(h % 2) * 64 + 64,
                  base + 1024 + g * 512 + sc_ * 128: base + 1024 + g * 512 + (sc_ + 1) * 128]

    def wsb2col(h):
        return c.wsb2[:, (h // 2) * 128 + (h % 2) * 64:(h // 2) * 128 + (h % 2) * 64 + 64]

    def pairsl(g):
        return AR[:, base + g * 512 + sc_ * 128: base + g * 512 + (sc_ + 1) * 128]

    def cpairsl(g):
        return AR[:, base + 1024 + g * 512 + sc_ * 128: base + 1024 + g * 512 + (sc_ + 1) * 128]

    for h in range(4):
        em("t", lambda e, h=h: e.matmul(c.psD[:, h * 64:(h + 1) * 64], pairsl(h // 2), wsb2col(h),
                                        start=True, stop=True))
    for h in (0, 2):
        em("t", lambda e, h=h: e.matmul(c.psB[:, h * 128:(h + 1) * 128], bsl(h), bsl(h),
                                        start=True, stop=True))
        em("t", lambda e, h=h: e.matmul(c.psC[:, h * 128:(h + 1) * 128], bsl(h), csl(h),
                                        start=True, stop=True))
    swk = r.mark("t", "st")
    r.wait("t", "st", swk)
    for h in (1, 3):
        em("t", lambda e, h=h: e.matmul(c.psB[:, h * 128:(h + 1) * 128], bsl(h), bsl(h),
                                        start=True, stop=True))
        em("t", lambda e, h=h: e.matmul(c.psC[:, h * 128:(h + 1) * 128], bsl(h), csl(h),
                                        start=True, stop=True))
    g1 = r.mark("t", "st")
    if SCSTEP < 2:
        return
    r.wait("v", "st", g1)
    psBr = c.psB.ap().rearrange("p (h t2) -> p h t2", t2=128)
    psCr = c.psC.ap().rearrange("p (h t2) -> p h t2", t2=128)
    LPr = AR[:, LP:LP + 512].rearrange("p (h t2) -> p h t2", t2=128)
    em("v", lambda e, LPr=LPr, psBr=psBr, co4b=co4b: e.tensor_tensor(LPr, psBr, co4b, ALU.mult))
    em("v", lambda e: e.drain())
    lp_done = ev("v", "sv", lambda e, LPr=LPr, mBLb=mBLb: e.tensor_tensor(LPr, LPr, mBLb, ALU.mult))
    em("v", lambda e, psBr=psBr, mBUb=mBUb: e.tensor_tensor(
        AR[:, GUP:GUP + 512].rearrange("p (h t2) -> p h t2", t2=128), psBr, mBUb, ALU.mult))
    em("v", lambda e, psCr=psCr, mBUb=mBUb: e.tensor_tensor(
        AR[:, GBCN:GBCN + 512].rearrange("p (h t2) -> p h t2", t2=128), psCr, mBUb, ALU.mult))
    gup_done = ev("v", "sv", lambda e, psCr=psCr, mBDb=mBDb: e.tensor_tensor(
        AR[:, DTT:DTT + 512].rearrange("p (h t2) -> p h t2", t2=128), psCr, mBDb, ALU.mult))
    em("v", lambda e, base=base: e.tensor_sub(
        c.ubuf[:, 0, 0:256], c.psD[:, 0:256],
        AR[:, base + 2048 + sc_ * 256: base + 2048 + (sc_ + 1) * 256]))
    rr_done = ev("v", "sv", lambda e: e.drain())
    em("v", lambda e, co4b64=co4b64: e.tensor_tensor(
        AR[:, RPo:RPo + 256].rearrange("p (h t2) -> p h t2", t2=64),
        c.ubuf[:, 0, 0:256].rearrange("p (h t2) -> p h t2", t2=64), co4b64, ALU.mult))
    em("v", lambda e, co4=co4: e.tensor_scalar_mul(c.ubuf[:, 0, 260:264], co4, -1.0))
    rp_done = ev("v", "sv", lambda e: e.drain())
    if SCSTEP < 3:
        return
    r.wait("t", "sv", lp_done)
    psEbf = c.psE.ap().bitcast(BF16)
    for h in range(4):
        em("t", lambda e, h=h, psEbf=psEbf: e.transpose(
            psEbf[:, h * 128:(h + 1) * 128], AR[:, LP + h * 128: LP + (h + 1) * 128], c.idb[:, :]))
    k1t = r.mark("t", "st")
    r.wait("v", "st", k1t)
    k1_done = ev("v", "sv", lambda e, psEbf=psEbf: e.tensor_copy(AR[:, K1o:K1o + 512], psEbf[:, 0:512]))
    if SCSTEP < 4:
        return
    r.wait("t", "sv", k1_done)
    for h in range(4):
        em("t", lambda e, h=h: e.matmul(c.psE[:, h * 128:(h + 1) * 128],
                                        AR[:, K1o + h * 128: K1o + (h + 1) * 128],
                                        AR[:, LP + h * 128: LP + (h + 1) * 128],
                                        start=True, stop=True))
    l2t = r.mark("t", "st")
    r.wait("v", "st", l2t)
    l2_done = ev("v", "sv", lambda e: e.tensor_copy(AR[:, L2o:L2o + 512], c.psE[:, 0:512]))
    r.wait("t", "sv", l2_done)
    for h in range(4):
        em("t", lambda e, h=h: e.matmul(c.psE[:, h * 128:(h + 1) * 128],
                                        AR[:, LP + h * 128: LP + (h + 1) * 128],
                                        AR[:, K1o + h * 128: K1o + (h + 1) * 128],
                                        start=True, stop=True))
    k2t = r.mark("t", "st")
    r.wait("v", "st", k2t)
    k2_done = ev("v", "sv", lambda e: e.tensor_copy(AR[:, LP:LP + 512], c.psE[:, 0:512]))
    r.wait("t", "sv", k2_done)
    for h in range(4):
        em("t", lambda e, h=h: e.matmul(c.psE[:, h * 128:(h + 1) * 128],
                                        AR[:, L2o + h * 128: L2o + (h + 1) * 128],
                                        AR[:, LP + h * 128: LP + (h + 1) * 128],
                                        start=True, stop=True))
    k4t = r.mark("t", "st")
    r.wait("v", "st", k4t)
    k4_done = ev("v", "sv", lambda e: e.tensor_copy(AR[:, L2o:L2o + 512], c.psE[:, 0:512]))
    if SCSTEP < 5:
        return
    r.wait("t", "sv", k4_done)
    r.wait("t", "sv", rp_done)
    for h in range(4):
        em("t", lambda e, h=h: e.matmul(c.psE[:, h * 64:(h + 1) * 64],
                                        AR[:, L2o + h * 128: L2o + (h + 1) * 128],
                                        AR[:, RPo + h * 64: RPo + (h + 1) * 64],
                                        start=True, stop=True))
    st1 = r.mark("t", "st")
    r.wait("v", "st", st1)
    t1_done = ev("v", "sv", lambda e: e.tensor_add(AR[:, T1o:T1o + 256], AR[:, RPo:RPo + 256],
                                                   c.psE[:, 0:256]))
    r.wait("t", "sv", t1_done)
    for h in range(4):
        em("t", lambda e, h=h: e.matmul(c.psE[:, h * 64:(h + 1) * 64],
                                        AR[:, LP + h * 128: LP + (h + 1) * 128],
                                        AR[:, T1o + h * 64: T1o + (h + 1) * 64],
                                        start=True, stop=True))
    st2 = r.mark("t", "st")
    r.wait("v", "st", st2)
    t2_done = ev("v", "sv", lambda e: e.tensor_add(AR[:, T2o:T2o + 256], AR[:, T1o:T1o + 256],
                                                   c.psE[:, 0:256]))
    r.wait("t", "sv", t2_done)
    for h in range(4):
        em("t", lambda e, h=h: e.matmul(c.psE[:, h * 64:(h + 1) * 64],
                                        AR[:, K1o + h * 128: K1o + (h + 1) * 128],
                                        AR[:, T2o + h * 64: T2o + (h + 1) * 64],
                                        start=True, stop=True))
    st3 = r.mark("t", "st")
    r.wait("v", "st", st3)
    v_done = ev("v", "sv", lambda e: e.tensor_sub(AR[:, VVo:VVo + 256], AR[:, T2o:T2o + 256],
                                                  c.psE[:, 0:256]))
    if SCSTEP < 6:
        return
    r.wait("t", "sv", v_done)
    r.wait("t", "sv", gup_done)
    for h in range(4):
        em("t", lambda e, h=h: e.matmul(c.psE[:, h * 64:(h + 1) * 64],
                                        AR[:, GUP + h * 128: GUP + (h + 1) * 128],
                                        AR[:, VVo + h * 64: VVo + (h + 1) * 64],
                                        start=True, stop=True))
    for h in range(4):
        em("t", lambda e, h=h: e.matmul(c.psB[:, h * 64:(h + 1) * 64], cpairsl(h // 2), wsb2col(h),
                                        start=True, stop=False))
        em("t", lambda e, h=h: e.matmul(c.psB[:, h * 64:(h + 1) * 64],
                                        AR[:, GBCN + h * 128: GBCN + (h + 1) * 128],
                                        AR[:, VVo + h * 64: VVo + (h + 1) * 64],
                                        start=False, stop=True))
    z1t = r.mark("t", "st")
    r.wait("v", "st", z1t)
    r.wait("v", "sv", rr_done)
    z1_done = ev("v", "sv", lambda e: e.tensor_add(AR[:, Z1o:Z1o + 256],
                                                   c.ubuf[:, 0, 0:256], c.psE[:, 0:256]))
    if SCSTEP < 7:
        return
    r.wait("t", "sv", z1_done)
    for h in range(4):
        em("t", lambda e, h=h: e.matmul(c.psC[:, h * 64:(h + 1) * 64],
                                        AR[:, DTT + h * 128: DTT + (h + 1) * 128],
                                        AR[:, Z1o + h * 64: Z1o + (h + 1) * 64],
                                        start=True, stop=True))
    psDbf = c.psD.ap().bitcast(BF16)
    for g in range(2):
        em("t", lambda e, g=g, psDbf=psDbf: e.transpose(
            psDbf[:, 512 + g * 128: 512 + (g + 1) * 128],
            AR[:, base + g * 512 + sc_ * 128: base + g * 512 + (sc_ + 1) * 128], c.idb[:, :]))
    dgt = r.mark("t", "st")
    r.wait("v", "st", dgt)
    if c.zprev_scan:
        r.wait("v", zsem1, c.zprev_scan)
    em("v", lambda e: e.tensor_tensor(
        AR[:, ZBo:ZBo + 256].rearrange("p (h t2) -> p h t2", t2=64),
        c.psC.ap().rearrange("p (h t2) -> p h t2", t2=128)[:, 0:4, 0:64],
        c.ubuf[:, 0, 260:264].rearrange("p f -> p f ()").broadcast_to([128, 4, 64]), ALU.mult))
    em("v", lambda e: e.drain())
    zb_done = ev("v", "sv", lambda e: e.tensor_add(AR[:, ZBo:ZBo + 256],
                                                   AR[:, ZBo:ZBo + 256], c.psB[:, 0:256]))
    btm_done = ev("v", "sv", lambda e, psDbf=psDbf: e.tensor_copy(AR[:, BTMo:BTMo + 256],
                                                                  psDbf[:, 512:768]))
    if SCSTEP < 8:
        return
    r.wait("t", "sv", btm_done)
    for h in range(4):
        em("t", lambda e, h=h: e.matmul(
            c.psE[(h % 2) * 64:(h % 2) * 64 + 64, 256 + (h // 2) * 64: 256 + (h // 2) * 64 + 64],
            AR[:, BTMo + (h // 2) * 128 + (h % 2) * 64: BTMo + (h // 2) * 128 + (h % 2) * 64 + 64],
            AR[:, VVo + h * 64: VVo + (h + 1) * 64],
            start=True, stop=True))
    wut = r.mark("t", "st")
    r.wait("v", "st", wut)
    em("v", lambda e: e.tensor_sub(c.wst, c.wst, c.psE[:, 256:384]))
    em("v", lambda e: e.drain())
    for h in range(3):
        em("v", lambda e, h=h: e.tensor_copy(
            c.wsb2[(h % 2) * 64:(h % 2) * 64 + 64,
                   (h // 2) * 128 + (h % 2) * 64:(h // 2) * 128 + (h % 2) * 64 + 64],
            c.wst[(h % 2) * 64:(h % 2) * 64 + 64, (h // 2) * 64:(h // 2) * 64 + 64]))
    wb_done = ev("v", "sv", lambda e: e.tensor_copy(
        c.wsb2[64:128, 192:256], c.wst[64:128, 64:128]))
    r.wait("t", "sv", wb_done)
    if SCSTEP < 9:
        return
    r.wait("s", "sv", zb_done)
    dma("s", zsem1, c.zloc1.ap()[sci * 128:(sci + 1) * 128, :], AR[:, ZBo:ZBo + 256])
    c.zprev_scan = r.val(zsem1)


def emit_phase_ffn(c):
    r, em, ev, dma = c.r, c.em, c.ev, c.dma
    AR = c.arena
    AR8 = c.arena.ap().bitcast(FP8)
    ln4_done = emit_ln(c, 1, 3, 4, c.hbuf, after=[("sv", c.wo_done)], f32_out=c.x4f)

    r.wait("v", "sv", c.ln4_last[1])
    em("v", lambda e: e.drain())
    em("v", lambda e: e.reduce_max(c.smx[:, 0:1], c.smx[:, 32:64], axis=AXX))
    xm = ev("v", "sv", lambda e: e.drain())
    r.wait("t", "sv", xm)
    r.wait("t", "sc", r.val("sc"))
    tok = ev("t", "st", lambda e: e.transpose(c.psE[0:1, 0:128], c.smx[:, 0:1], c.idf[:, :]))
    r.wait("v", "st", tok)
    vv = ev("v", "sv", lambda e: e.reduce_max(c.smx[0:1, 1:2], c.psE[0:1, 0:128], axis=AXX))
    r.wait("g", "sv", vv)
    r.wait("g", "sg", r.val("sg"))
    dma("g", "sg", c.agi_xmx[:, :], c.smx[0:1, 1:2])
    r.wait("g", "sg", r.val("sg"))
    cc_x = c.cc(lambda e: e.collective_compute(
        "AllReduce", ALU.max, replica_groups=c.RG,
        ins=[c.agi_xmx.ap().opt()], outs=[c.ago_xmx.ap().opt()]))
    c.waitcc("g", cc_x)
    r.wait("g", "sg", r.val("sg"))
    dma("g", "sg", c.smx[0:1, 2:3], c.ago_xmx[:, :])
    r.wait("v", "sg", r.val("sg"))
    em("v", lambda e: e.tensor_scalar(c.smx[0:1, 2:3], c.smx[0:1, 2:3], 1.0 / 448.0, 1e-6, ALU.mult, ALU.max))
    em("v", lambda e: e.drain())
    em("v", lambda e: e.reciprocal(c.smx[0:1, 3:4], c.smx[0:1, 2:3]))
    em("v", lambda e: e.drain())
    em("v", lambda e: e.tensor_scalar_mul(c.smx[0:1, 3:4], c.smx[0:1, 3:4], 0.5))
    vv = ev("v", "sv", lambda e: e.drain())
    r.wait("t", "sv", vv)
    tok = ev("t", "st", lambda e: e.transpose(c.psE[0:1, 20:22], c.smx2[0:2, 3:4], c.idf[0:2, 0:2]))
    r.wait("v", "st", tok)
    em("v", lambda e: e.tensor_copy(c.smx[0:1, 4:6], c.psE[0:1, 20:22]))
    em("v", lambda e: e.drain())
    em("v", lambda e: e.tensor_mul(c.smx[0:1, 6:7], c.smx[0:1, 2:3], c.smx[0:1, 4:5]))
    em("v", lambda e: e.drain())
    vv = ev("v", "sv", lambda e: e.tensor_scalar_mul(c.smx[0:1, 6:7], c.smx[0:1, 6:7], 4.0))
    r.wait("t", "sv", vv)
    em("t", lambda e: e.matmul(c.psE[:, 24:25], c.onerf[:, :], c.smx[0:1, 3:4], start=True, stop=True))
    tok = ev("t", "st", lambda e: e.matmul(c.psE[:, 25:26], c.onerf[:, :], c.smx[0:1, 6:7], start=True, stop=True))
    r.wait("v", "st", tok)
    vv = ev("v", "sv", lambda e: e.tensor_copy(c.smx2[:, 8:10], c.psE[:, 24:26]))
    em("v", lambda e: e.drain())
    x4sem = [r.fresh("x4"), r.fresh("x4")]
    if c.ubuf_guard:
        for _tk in c.ubuf_guard:
            r.wait("y", _tk[0], _tk[1])
    for _tk in c.ln4_last[0]:
        r.wait("y", _tk[0], _tk[1])
    r.wait("y", "st", r.val("st"))
    x4toks = {}
    for kk2 in range(2 * DK):
        k, half = kk2 // 2, kk2 % 2
        bpar = kk2 % 2
        if kk2 >= 2:
            r.wait("y", "sv", x4toks[kk2 - 2])
        dma("y", x4sem[bpar], c.ubuf[:, bpar, :], c.x4f.ap()[ts(k), half * 512:(half + 1) * 512])
        r.wait("v", x4sem[bpar], r.val(x4sem[bpar]))
        x4toks[kk2] = ev("v", "sv", lambda e, k=k, half=half, bpar=bpar: e.tensor_scalar(
            AR8[:, k * 1024 + half * 512: k * 1024 + (half + 1) * 512], c.ubuf[:, bpar, :],
            c.smx2[:, 8:9], None, ALU.mult))
    x8_done = ev("v", "sv", lambda e: e.drain())

    c.waitcc("y", c.cc_fc8)
    r.wait("y", "st", r.val("st"))
    if c.ubuf_guard:
        for _tk in c.ubuf_guard:
            r.wait("s", _tk[0], _tk[1])
    fsem = [r.fresh("fc", ), r.fresh("fc")]
    hsem = r.fresh("hb")
    r.wait("t", "sv", x8_done)
    X8 = AR8[:, 0:16384].rearrange("p (k t2) -> p k t2", t2=1024)
    slab_mm = {}
    strip_i = 0
    hb_toks = {}
    for hs in range(16):
        bpar = hs % 2
        if hs >= 2:
            r.wait("y", "st", slab_mm[hs - 2])
        dma("y", fsem[bpar], AR8[:, 16384 + bpar * 8192: 16384 + (bpar + 1) * 8192].rearrange(
            "p (k m) -> p k m", k=DK),
            c.ago_fc8.ap().rearrange("(k p) m -> p k m", p=128)[:, :, hs * 512:(hs + 1) * 512])
        r.wait("t", fsem[bpar], r.val(fsem[bpar]))
        WSL = AR8[:, 16384 + bpar * 8192: 16384 + (bpar + 1) * 8192].rearrange(
            "p (k m) -> p k m", k=DK)
        for mh in range(4):
            hchunk = hs * 4 + mh
            for n in range(2):
                ppar = (hchunk * 2 + n) % 2
                psx = c.psA[:, ppar * 1024:ppar * 1024 + 512]
                if hchunk * 2 + n >= 2:
                    r.wait("t", "sc", ev_sc[hchunk * 2 + n - 2])
                for k in range(DK):
                    em("t", lambda e, k=k, mh=mh, n=n, psx=psx, WSL=WSL: e.matmul(
                        psx, WSL[:, k, mh * 128:(mh + 1) * 128],
                        X8[:, k, n * 512:(n + 1) * 512], start=(k == 0), stop=(k == DK - 1)))
                mk = r.mark("t", "st")
                r.wait("s", "st", mk)
                if hchunk * 2 + n == 0:
                    ev_sc = {}
                    sq_tok = {}
                if hchunk * 2 + n >= 2:
                    r.wait("s", "sv", sq_tok[hchunk * 2 + n - 2])
                if n == 0 and hchunk >= 1:
                    r.wait("v", hsem, hb_toks[hchunk - 1])
                ev_sc[hchunk * 2 + n] = ev("s", "sc", lambda e, ppar=ppar, psx=psx: e.activation(
                    c.ubuf[:, ppar, :], psx, ACT.Relu, scale=c.smx2[:, 9:10]))
                r.wait("v", "sc", ev_sc[hchunk * 2 + n])
                sq_tok[hchunk * 2 + n] = ev("v", "sv", lambda e, ppar=ppar, n=n: e.tensor_mul(
                    c.lnt[:, n, :], c.ubuf[:, ppar, :], c.ubuf[:, ppar, :]))
            em("v", lambda e: e.drain())
            em("v", lambda e, strip_i=strip_i: e.reduce_max(
                c.smx[:, 32 + strip_i % 32:33 + strip_i % 32],
                c.lnt[:, 0:2, :].rearrange("p a b -> p (a b)"), axis=AXX,
                apply_absolute_value=True))
            if strip_i % 32 == 31:
                em("v", lambda e: e.drain())
                em("v", lambda e, strip_i=strip_i: e.tensor_scalar_max(
                    c.smx[:, 7 + strip_i // 32:8 + strip_i // 32] if False else c.smx[:, 7:8],
                    c.smx[:, 7:8] if strip_i >= 32 else c.smx[:, 32:33], 0.0) if False else e.reduce_max(
                    c.smx[:, 7:8] if strip_i // 32 == 0 else c.smx[:, 8:9],
                    c.smx[:, 32:64], axis=AXX))
            strip_i += 1
            hvv = ev("v", "sv", lambda e: e.drain())
            r.wait("y", "sv", hvv)
            if hchunk >= 2:
                r.wait("y", hsem, hb_toks[hchunk - 2])
            hb_toks[hchunk] = dma("y", hsem, c.hbounce.ap()[hchunk * 128:(hchunk + 1) * 128, :],
                                  c.lnt[:, 0:2, :].rearrange("p a b -> p (a b)"))
        slab_mm[hs] = r.val("st")
    em("v", lambda e: e.drain())
    em("v", lambda e: e.tensor_max(c.smx[:, 7:8], c.smx[:, 7:8], c.smx[:, 8:9]))
    hm = ev("v", "sv", lambda e: e.drain())
    r.wait("t", "sv", hm)
    r.wait("t", "sc", r.val("sc"))
    tok = ev("t", "st", lambda e: e.transpose(c.psE[0:1, 0:128], c.smx[:, 7:8], c.idf[:, :]))
    r.wait("v", "st", tok)
    vv = ev("v", "sv", lambda e: e.reduce_max(c.smx[0:1, 10:11], c.psE[0:1, 0:128], axis=AXX))
    r.wait("g", "sv", vv)
    r.wait("g", "sg", r.val("sg"))
    dma("g", "sg", c.agi_hmx[:, :], c.smx[0:1, 10:11])
    r.wait("g", "sg", r.val("sg"))
    cc_h = c.cc(lambda e: e.collective_compute(
        "AllReduce", ALU.max, replica_groups=c.RG,
        ins=[c.agi_hmx.ap().opt()], outs=[c.ago_hmx.ap().opt()]))
    c.waitcc("g", cc_h)
    r.wait("g", "sg", r.val("sg"))
    dma("g", "sg", c.smx[0:1, 11:12], c.ago_hmx[:, :])
    r.wait("v", "sg", r.val("sg"))
    em("v", lambda e: e.tensor_scalar(c.smx[0:1, 11:12], c.smx[0:1, 11:12], 1.0 / 448.0, 1e-6, ALU.mult, ALU.max))
    em("v", lambda e: e.drain())
    em("v", lambda e: e.reciprocal(c.smx[0:1, 12:13], c.smx[0:1, 11:12]))
    em("v", lambda e: e.drain())
    em("v", lambda e: e.tensor_scalar_mul(c.smx[0:1, 12:13], c.smx[0:1, 12:13], 0.5))
    em("v", lambda e: e.drain())
    em("v", lambda e: e.tensor_mul(c.smx[0:1, 13:14], c.smx[0:1, 11:12], c.smx[0:1, 5:6]))
    em("v", lambda e: e.drain())
    vv = ev("v", "sv", lambda e: e.tensor_scalar_mul(c.smx[0:1, 13:14], c.smx[0:1, 13:14], 4.0))
    r.wait("t", "sv", vv)
    em("t", lambda e: e.matmul(c.psE[:, 24:25], c.onerf[:, :], c.smx[0:1, 12:13], start=True, stop=True))
    tok = ev("t", "st", lambda e: e.matmul(c.psE[:, 25:26], c.onerf[:, :], c.smx[0:1, 13:14], start=True, stop=True))
    r.wait("v", "st", tok)
    vv = ev("v", "sv", lambda e: e.tensor_copy(c.smx2[:, 10:12], c.psE[:, 24:26]))
    em("v", lambda e: e.drain())
    gb1 = c.gsl(1, 5)
    em("v", lambda e, gb1=gb1: e.tensor_tensor(
        c.smx[:, 16:32], gb1, c.smx2[:, 11:12].broadcast_to([128, 16]), ALU.mult))
    gbdq = ev("v", "sv", lambda e: e.drain())

    q8sem = [r.fresh("q8"), r.fresh("q8")]
    r.wait("y", hsem, r.val(hsem))
    qs_toks = {}
    for k in range(64):
        bpar = k % 2
        if k >= 2:
            r.wait("y", "sv", qs_toks[k - 2][0])
        r.wait("y", hsem, hb_toks[k] if k in hb_toks else r.val(hsem))
        dma("y", q8sem[bpar], c.lnt[:, bpar, :].bitcast(F32)[:, 0:256] if False else c.lnq[:, bpar, :],
            c.hbounce.ap()[k * 128:(k + 1) * 128, :].rearrange("p (a b) -> p a b", a=2)[:, 0, :] if False
            else c.hbounce.ap()[k * 128:(k + 1) * 128, 0:512])
        dma("y", q8sem[bpar], c.lnt[:, bpar, :], c.hbounce.ap()[k * 128:(k + 1) * 128, 512:1024])
        r.wait("v", q8sem[bpar], r.val(q8sem[bpar]))
        vq = ev("v", "sv", lambda e, bpar=bpar, k=k: e.tensor_scalar(
            AR8[:, 49152 + bpar * 1024: 49152 + bpar * 1024 + 512], c.lnq[:, bpar, :],
            c.smx2[:, 10:11], None, ALU.mult))
        vq2 = ev("v", "sv", lambda e, bpar=bpar, k=k: e.tensor_scalar(
            AR8[:, 49152 + bpar * 1024 + 512: 49152 + (bpar + 1) * 1024], c.lnt[:, bpar, :],
            c.smx2[:, 10:11], None, ALU.mult))
        r.wait("y", "sv", vq2)
        t1 = dma("y", q8sem[bpar], c.h8b.ap()[k * 128:(k + 1) * 128, :],
                 AR8[:, 49152 + bpar * 1024: 49152 + (bpar + 1) * 1024])
        qs_toks[k] = (r.val("sv"), t1)
    c.waitcc("y", c.cc_pj8)
    r.wait("y", "sv", r.val("sv"))
    r.wait("y", q8sem[0], r.val(q8sem[0]))
    r.wait("y", q8sem[1], r.val(q8sem[1]))
    psem = [r.fresh("pj"), r.fresh("pj")]
    h8sem = [r.fresh("h8"), r.fresh("h8")]
    r.wait("v", "sv", gbdq)
    PJ = [AR8[:, 16384 + i * 16384: 16384 + (i + 1) * 16384].rearrange("p (k m) -> p k m", m=256)
          for i in range(2)]
    H8 = [AR8[:, 0 + i * 1024: 0 + (i + 1) * 1024] for i in range(2)]
    pj_mm = {}
    h8_mm = {}
    for dg in range(4):
        for half in range(2):
            sidx = dg * 2 + half
            spar = sidx % 2
            if sidx >= 2:
                r.wait("y", "st", pj_mm[sidx - 2])
            dma("y", psem[spar], PJ[spar],
                c.ago_pj8.ap().rearrange("(k p) m -> p k m", p=128)[:, :, dg * 512 + half * 256: dg * 512 + (half + 1) * 256])
            r.wait("t", psem[spar], r.val(psem[spar]))
            if sidx >= 1:
                r.wait("t", "sv", ev_last)
            mmtok = {}
            for k in range(64):
                kb = k % 2
                if k >= 2:
                    r.wait("y", "st", mmtok[k - 2])
                elif sidx >= 1:
                    r.wait("y", "st", pj_mm[sidx - 1])
                dma("y", h8sem[kb], H8[kb], c.h8b.ap()[k * 128:(k + 1) * 128, :])
                r.wait("t", h8sem[kb], r.val(h8sem[kb]))
                for dm in range(2):
                    for n in range(2):
                        em("t", lambda e, k=k, dm=dm, n=n, spar=spar, kb=kb: e.matmul(
                            c.psA[:, (dm * 2 + n) * 512:(dm * 2 + n + 1) * 512],
                            PJ[spar][:, k, dm * 128:(dm + 1) * 128],
                            H8[kb][:, n * 512:(n + 1) * 512], start=(k == 0), stop=(k == 63)))
                mmtok[k] = r.mark("t", "st")
            r.wait("v", "st", mmtok[63])
            for dm in range(2):
                for n in range(2):
                    dchunk = dg * 4 + half * 2 + dm
                    ev_last = ev("v", "sv", lambda e, dchunk=dchunk, n=n, dm=dm: e.scalar_tensor_tensor(
                        c.xres[:, dchunk * TC + n * 512: dchunk * TC + (n + 1) * 512],
                        c.psA[:, (dm * 2 + n) * 512:(dm * 2 + n + 1) * 512],
                        c.smx[:, 16 + dchunk:17 + dchunk],
                        c.xres[:, dchunk * TC + n * 512: dchunk * TC + (n + 1) * 512],
                        ALU.mult, ALU.add))
            pj_mm[sidx] = r.val("st")
    c.ffn_done = r.val("sv")


# version 04
# speedup vs baseline: 1.1258x; 1.0057x over previous
import numpy as np
import ml_dtypes
import concourse.bass as bass
import concourse.mybir as mybir
from concourse.bass_utils import run_bass_kernel_spmd

F32 = mybir.dt.float32
BF16 = mybir.dt.bfloat16
FP8 = mybir.dt.float8e4
ALU = mybir.AluOpType
ACT = mybir.ActivationFunctionType
AXX = mybir.AxisListType.X

NCORE = 8
T, D = 8192, 2048
TC = T // NCORE
DK = D // 128
HM, HMC, NE = 1408, 11, 8
FFH = 8192
RS128 = float(1.0 / np.sqrt(128.0))
EPS = 1e-5

DEBUG = False
PHASES = 99
SUBPH = 99
SCSTEP = 99


def ts(i, n=128):
    return slice(i * n, (i + 1) * n)


class Rec:
    def __init__(self, nc):
        self.nc = nc
        self.items = []
        self.sem = {}
        self.semctx = []

    def S(self, name):
        cm = self.nc.semaphore(name)
        h = cm.__enter__()
        self.semctx.append(cm)
        self.sem[name] = [h, 0]
        return name

    def fresh(self, prefix="f", kind="hdma"):
        if not hasattr(self, "pools"):
            self.pools = {"gdma": [], "hdma": [], "evt": []}
            self.kinds = {}
        if kind == "evt" and self.pools[kind]:
            return self.pools[kind].pop()
        name = f"sm{len(self.sem)}"
        self.S(name)
        self.kinds[name] = kind
        return name

    def release(self, name):
        self.pools[self.kinds[name]].append(name)

    def inc(self, name, k=1):
        e = self.sem[name]
        e[1] += k
        return e[0], e[1]

    def val(self, name):
        return self.sem[name][1]

    def emit(self, eng, fn, weak=False):
        self.items.append((eng, fn, weak))

    def wait(self, eng, name, v=None):
        hh = self.sem[name][0]
        vv = self.sem[name][1] if v is None else v
        if vv > 0:
            self.emit(eng, lambda e, hh=hh, vv=vv: e.wait_ge(hh, vv), weak=True)

    def mark(self, engl, sem):
        hh, vv = self.inc(sem, 1)
        for i in range(len(self.items) - 1, -1, -1):
            eng, fn, weak = self.items[i]
            if eng == engl and not weak:
                self.items[i] = (eng, lambda e, fn=fn, hh=hh: fn(e).then_inc(hh, 1), False)
                return vv
        raise RuntimeError("no instruction to mark")

    def replay(self, blk):
        byeng = {k: [] for k in "tvsgy"}
        for eng, fn, _w in self.items:
            byeng[eng].append(fn)
        def run(fns):
            def body(e):
                for f in fns:
                    f(e)
            return body
        blk.tensor(run(byeng["t"]))
        blk.vector(run(byeng["v"]))
        blk.scalar(run(byeng["s"]))
        blk.gpsimd(run(byeng["g"]))
        blk.sync(run(byeng["y"]))

    def close(self):
        for cm in reversed(self.semctx):
            cm.__exit__(None, None, None)


class Ctx:
    pass


def build():
    nc = bass.Bass("TRN2", target_bir_lowering=False)
    c = Ctx()
    c.nc = nc

    def EIN(name, shape, dtype=F32):
        return nc.dram_tensor(name, shape, dtype, kind="ExternalInput")

    c.x_in = EIN("x", [D, TC])
    c.l0c = EIN("l0c", [D, 1]); c.l1c = EIN("l1c", [D, 1])
    c.l0wmod = EIN("l0wmod", [D, 1536]); c.l1wmod = EIN("l1wmod", [D, 1536])
    c.l0qkv = EIN("l0qkv", [768, D])
    c.l1qkv = EIN("l1qkv", [772, D])
    c.l0wo_s = EIN("l0wo_s", [D, 256])
    c.l1wo_s = EIN("l1wo_s", [D, 256])
    c.wg_in = EIN("wg", [8, D])
    c.moe1s = EIN("moe1s", [NE, HM, 256])
    c.moe2s = EIN("moe2s", [NE, HM, 256])
    c.moe3s = EIN("moe3s", [NE, D, 176])
    c.bilr_in = EIN("bilr", [1, 4])
    c.w1h_in = EIN("w1h", [256, 64])
    c.wfc_s = EIN("wfc_s", [FFH, 256])
    c.wproj_s = EIN("wproj_s", [D, 1024])
    c.out = nc.dram_tensor("out", [D, TC], F32, kind="ExternalOutput")

    I = lambda a, name: nc.inline_tensor(np.ascontiguousarray(a), name=name)
    c.ident_bf = I(np.eye(128, dtype=ml_dtypes.bfloat16), "identbf")
    c.ident_f = I(np.eye(128, dtype=np.float32), "identf")
    c.ones_col = I(np.ones((128, 1), dtype=ml_dtypes.bfloat16), "onescol")
    c.ones_row = I(np.ones((1, 128), dtype=ml_dtypes.bfloat16), "onesrow")
    c.ones_rowf = I(np.ones((1, 128), dtype=np.float32), "onesrowf")
    mk = np.zeros((128, 4, 512), dtype=ml_dtypes.bfloat16)
    for rr in range(4):
        mk[:, rr, :] = (np.arange(128)[:, None] + 128 * rr <= np.arange(512)[None, :])
    c.maskc_t = I(mk.reshape(128, 2048), "maskc")
    sub = np.arange(128) // 16
    ar = np.arange(128)
    c.maskBL_t = I((sub[:, None] > sub[None, :]).astype(np.float32), "maskBLts")
    c.maskBUn_t = I(-(sub[:, None] < sub[None, :]).astype(np.float32), "maskBUn")
    c.maskBD_t = I(((sub[:, None] == sub[None, :]) & (ar[:, None] <= ar[None, :])).astype(np.float32), "maskBD")
    c.tokidx_t = I((1.0 / ((ar % 16) + 1.0) / 64.0).astype(np.float32).reshape(128, 1), "tokidx")

    def IDR(name, shape, dtype):
        return nc.dram_tensor(name, shape, dtype, kind="Internal")

    c.agi_mod = IDR("agi_mod", [2, 1536], F32); c.ago_mod = IDR("ago_mod", [16, 1536], F32)
    c.agi_h = IDR("agi_h", [D, TC], BF16); c.ago_h = IDR("ago_h", [NCORE * D, TC], BF16)
    c.zloc0 = IDR("zloc0", [T, 256], BF16)
    c.zloc1 = IDR("zloc1", [T, 256], BF16)
    c.part = IDR("part", [T, D], BF16)
    c.wo0T = IDR("wo0T", [256, D], BF16)
    c.wo1T = IDR("wo1T", [256, D], BF16)
    c.rsout = IDR("rsout", [TC, D], BF16)
    c.agi_m1 = IDR("agi_m1", [NE, 256, HM], BF16)
    c.ago_m1 = [IDR(f"ago_m1_{e}", [D, HM], BF16) for e in range(NE)]
    c.agi_m2 = IDR("agi_m2", [NE, 256, HM], BF16)
    c.ago_m2 = [IDR(f"ago_m2_{e}", [D, HM], BF16) for e in range(NE)]
    c.agi_m3 = IDR("agi_m3", [NE, 176, D], BF16)
    c.ago_m3 = [IDR(f"ago_m3_{e}", [HM, D], BF16) for e in range(NE)]
    c.agi_fc8 = IDR("agi_fc8", [256, FFH], FP8); c.ago_fc8 = IDR("ago_fc8", [D, FFH], FP8)
    c.agi_pj8 = IDR("agi_pj8", [1024, D], FP8); c.ago_pj8 = IDR("ago_pj8", [FFH, D], FP8)
    c.agi_wmx = IDR("agi_wmx", [2, 1], F32); c.ago_wmx = IDR("ago_wmx", [2, 1], F32)
    c.agi_xmx = IDR("agi_xmx", [1, 1], F32); c.ago_xmx = IDR("ago_xmx", [1, 1], F32)
    c.agi_hmx = IDR("agi_hmx", [1, 1], F32); c.ago_hmx = IDR("ago_hmx", [1, 1], F32)
    c.hbounce = IDR("hbounce", [FFH, TC], BF16)
    c.h8b = IDR("h8b", [FFH, TC], FP8)
    c.x4f = IDR("x4f", [D, TC], F32)

    c.dbg = {}
    if DEBUG:
        for name, shape in [("dbg_h", [D, TC]), ("dbg_x0a", [D, TC]), ("dbg_x0b", [D, TC]),
                            ("dbg_x1a", [D, TC]), ("dbg_w", [128, 64]), ("dbg_z", [T, 256])]:
            c.dbg[name] = nc.dram_tensor(name, shape, F32, kind="ExternalOutput")

    c.RG = [list(range(NCORE))]

    A = nc.alloc_sbuf_tensor
    c.xres = A("xres", [128, DK * TC], F32)
    c.hbuf = A("hbuf", [128, DK * TC], BF16)
    c.wqkvT = A("wqkvT", [128, DK, 772], BF16)
    c.modb = A("modb", [128, 2 * 6 * DK], F32)
    c.cbuf = A("cbuf", [128, 2 * DK], F32)
    c.silc = A("silc", [128, 2 * DK], F32)

    c.idb = A("idb", [128, 128], BF16)
    c.idf = A("idf", [128, 128], F32)
    c.onec = A("onec", [128, 1], BF16)
    c.oner = A("oner", [1, 128], BF16)
    c.onerf = A("onerf", [1, 128], F32)
    c.mBL = A("mBL", [128, 128], F32)
    c.mBUn = A("mBUn", [128, 128], F32)
    c.mBD = A("mBD", [128, 128], F32)
    c.tkx = A("tkx", [128, 1], F32)
    c.wgT = A("wgT", [128, DK * 8], BF16)
    c.wmoe = A("wmoe", [128, 64], F32)
    c.stg = A("stg", [128, 2, 1024], BF16)
    c.stgf = A("stgf", [128, 2, 1536], F32)
    c.smx = A("smx", [128, 64], F32)
    c.smx2 = A("smx2", [128, 64], F32)
    c.rowscr = A("rowscr", [1, 1536], F32)
    c.mtri = A("mtri", [128, 128], BF16)
    c.lnt = A("lnt", [128, 2, 512], BF16)
    c.lnq = A("lnq", [128, 2, 512], BF16)
    c.ubuf = A("ubuf", [128, 2, 512], F32)
    c.arena = A("arena", [128, 26624], BF16)
    c.psA = nc.alloc_psum_tensor("psA", [128, 2048], F32)
    c.psB = nc.alloc_psum_tensor("psB", [128, 512], F32)
    c.psC = nc.alloc_psum_tensor("psC", [128, 512], F32)
    c.psD = nc.alloc_psum_tensor("psD", [128, 512], F32)
    c.psE = nc.alloc_psum_tensor("psE", [128, 512], F32)

    r = Rec(nc)
    c.r = r
    c.ubuf_guard = None
    c.stg_gate = None
    for s in ["sg", "sh", "sx", "st", "sv", "sc", "scc", "sge",
              "d0", "d1", "d2", "d3", "d4", "d5", "d6", "d7"]:
        r.S(s)

    emit_all(c)

    blk_cm = nc.Block()
    blk = blk_cm.__enter__()
    r.replay(blk)
    blk_cm.__exit__(None, None, None)
    r.close()
    return nc


def emit_all(c):
    r = c.r
    nc = c.nc

    def dma(engl, sem, outap, inap, **kw):
        hh, vv = r.inc(sem, 16)
        r.emit(engl, lambda e, o=outap, i=inap, hh=hh, kw=kw: e.dma_start(out=o, in_=i, **kw).then_inc(hh, 16))
        return vv

    def ev(engl, sem, fn):
        hh, vv = r.inc(sem, 1)
        r.emit(engl, lambda e, fn=fn, hh=hh: fn(e).then_inc(hh, 1))
        return vv

    def em(engl, fn):
        r.emit(engl, fn)

    def cc(fn):
        s = r.fresh("cc", kind="evt")
        hh, vv = r.inc(s, 1)
        r.emit("g", lambda e, fn=fn, hh=hh: fn(e).then_inc(hh, 1))
        return (s, vv)

    def waitcc(engl, tok):
        r.wait(engl, tok[0], tok[1])

    c.dma, c.ev, c.em, c.cc, c.waitcc = dma, ev, em, cc, waitcc

    for src, dst in [(c.ident_bf, c.idb), (c.ident_f, c.idf), (c.ones_col, c.onec), (c.ones_row, c.oner), (c.ones_rowf, c.onerf),
                     (c.maskBL_t, c.mBL), (c.maskBUn_t, c.mBUn),
                     (c.maskBD_t, c.mBD), (c.tokidx_t, c.tkx)]:
        dma("g", "sg", dst[:, :], src[:, :])
    dma("g", "sg", c.mtri[:, :], c.maskc_t[:, 0:128])
    dma("g", "sg", c.cbuf[:, 0:DK], c.l0c.ap().rearrange("(k p) o -> p (k o)", p=128), allow_slow_non_contiguous=True)
    dma("g", "sg", c.cbuf[:, DK:2 * DK], c.l1c.ap().rearrange("(k p) o -> p (k o)", p=128), allow_slow_non_contiguous=True)
    for k in range(DK):
        dma("g", "sg", c.xres[:, k * TC:(k + 1) * TC], c.x_in[ts(k), :])
    n_pro = r.val("sg")

    r.wait("s", "sg", n_pro)
    silu_done = ev("s", "sc", lambda e: e.activation(c.silc[:, :], c.cbuf[:, :], ACT.Silu))

    modps = c.psA
    wmsem = [r.fresh("wm"), r.fresh("wm")]
    mm_done = {}
    evac_done = 0
    for L in range(2):
        wm = c.l0wmod if L == 0 else c.l1wmod
        for k in range(DK):
            gidx = L * DK + k
            buf = gidx % 2
            if gidx >= 2:
                r.wait("y", "st", mm_done[gidx - 2])
            dma("y", wmsem[buf], c.stgf[:, buf, :], wm[ts(k), :])
            r.wait("t", wmsem[buf], r.val(wmsem[buf]))
            if gidx == 0:
                r.wait("t", "sc", silu_done)
            if k == 0 and L == 1:
                r.wait("t", "sv", evac_done)
            for nn in range(3):
                fn = lambda e, b=buf, nn=nn, k=k, g2=gidx: e.matmul(
                    modps[0:1, nn * 512:(nn + 1) * 512], c.silc[:, g2:g2 + 1],
                    c.stgf[:, b, nn * 512:(nn + 1) * 512], start=(k == 0), stop=(k == DK - 1))
                if nn == 2:
                    mm_done[gidx] = ev("t", "st", fn)
                else:
                    em("t", fn)
        r.wait("v", "st", r.val("st"))
        if L == 1:
            r.wait("v", "sg", mod_store0)
        evac_done = ev("v", "sv", lambda e: e.tensor_copy(c.rowscr[:, 0:1536], modps[0:1, 0:1536]))
        r.wait("g", "sv", evac_done)
        if L == 0:
            r.wait("g", "sg", r.val("sg"))
        mod_store0 = dma("g", "sg", c.agi_mod.ap()[L:L + 1, :], c.rowscr[:, 0:1536])
    r.wait("g", "sg", r.val("sg"))
    cc_mod = cc(lambda e: e.collective_compute(
        "AllGather", ALU.bypass, replica_groups=c.RG,
        ins=[c.agi_mod.ap().opt()], outs=[c.ago_mod.ap().opt()]))

    waitcc("g", cc_mod)
    r.release(cc_mod[0])
    for L in range(2):
        for rr in range(8):
            srcap = c.ago_mod.ap()[2 * rr + L].rearrange("(cc p) -> p cc", p=128)
            dsta = c.modb[:, L * 96 + rr * 12: L * 96 + (rr + 1) * 12]
            dma("g", "sg", dsta, srcap, allow_slow_non_contiguous=True)
    n_modload = r.val("sg")

    def gsl(L, gi):
        return c.modb[:, L * 96 + gi * 16: L * 96 + gi * 16 + DK]

    r.wait("s", "sg", n_modload)
    r.wait("v", "sg", n_modload)
    abs_done = 0
    for i, (L, gi) in enumerate([(0, 2), (0, 5), (1, 2), (1, 5)]):
        abs_done = ev("s", "sc", lambda e, L=L, gi=gi, i=i: e.activation(
            c.smx2[:, i * DK:(i + 1) * DK], gsl(L, gi), ACT.Abs))
    for i, (L, gi) in enumerate([(0, 2), (0, 5), (1, 2), (1, 5)]):
        em("v", lambda e, L=L, gi=gi, i=i: e.tensor_single_scalar(
            c.smx[:, i * DK:(i + 1) * DK], gsl(L, gi), 0.0, ALU.is_ge))
    em("v", lambda e: e.drain())
    em("v", lambda e: e.tensor_scalar(c.smx[:, 0:64], c.smx[:, 0:64], 2.0, -1.0, ALU.mult, ALU.add))
    r.wait("v", "sc", abs_done)
    em("v", lambda e: e.tensor_scalar_max(c.smx2[:, 0:64], c.smx2[:, 0:64], 0.25))
    em("v", lambda e: e.drain())
    for i, (L, gi) in enumerate([(0, 2), (0, 5), (1, 2), (1, 5)]):
        em("v", lambda e, L=L, gi=gi, i=i: e.tensor_mul(
            gsl(L, gi), c.smx[:, i * DK:(i + 1) * DK], c.smx2[:, i * DK:(i + 1) * DK]))
    em("v", lambda e: e.drain())
    for L in range(2):
        for gi in (1, 4):
            em("v", lambda e, L=L, gi=gi: e.tensor_scalar_add(gsl(L, gi), gsl(L, gi), 1.0))
    gates_done = ev("v", "sv", lambda e: e.drain())
    c.gsl = gsl
    c.gates_done = gates_done

    emit_stage_qkvT(c, c.l0qkv, 768)

    if PHASES >= 1:
        ln_done = emit_ln(c, 0, 0, 1, c.hbuf)
        r.wait("g", "sc", ln_done)
        for k in range(DK):
            dma("g", "sg", c.agi_h[ts(k), :], c.hbuf[:, k * TC:(k + 1) * TC])
        r.wait("g", "sg", r.val("sg"))
        c.cc_h0 = c.cc(lambda e: e.collective_compute(
            "AllGather", ALU.bypass, replica_groups=c.RG,
            ins=[c.agi_h.ap().opt()], outs=[c.ago_h.ap().opt()]))

        if DEBUG and "dbg_h" in c.dbg:
            emit_dump_bf16(c, c.hbuf, c.dbg["dbg_h"], after_sc=ln_done)

    if PHASES >= 2:
        emit_phase_l0attn(c)
        if DEBUG and "dbg_x0a" in c.dbg:
            emit_dump_f32(c, c.xres, c.dbg["dbg_x0a"], after_sv=c.r.val("sv"))

    if PHASES >= 3:
        emit_phase_moe(c)
        if DEBUG and "dbg_x0b" in c.dbg:
            emit_dump_f32(c, c.xres, c.dbg["dbg_x0b"], after_sv=c.r.val("sv"))

    if PHASES >= 4:
        emit_phase_l1attn(c)
        if DEBUG and "dbg_x1a" in c.dbg:
            emit_dump_f32(c, c.xres, c.dbg["dbg_x1a"], after_sv=c.r.val("sv"))

    if PHASES >= 5:
        emit_phase_ffn(c)

    r.wait("y", "sc", r.val("sc"))
    r.wait("y", "sv", r.val("sv"))
    r.wait("y", "st", r.val("st"))
    r.wait("y", "sg", r.val("sg"))
    fsem = r.fresh("fin")
    for k in range(DK):
        dma("y", fsem, c.out[ts(k), :], c.xres[:, k * TC:(k + 1) * TC])
    r.wait("y", fsem, r.val(fsem))


def emit_ln(c, L, gi_s, gi_c, outbuf, after=(), f32_out=None):
    r = c.r
    em, ev, dma = c.em, c.ev, c.dma
    gs = c.gsl(L, gi_s)
    gc = c.gsl(L, gi_c)
    r.wait("v", "sv", c.gates_done)
    r.wait("s", "sv", c.gates_done)
    for tk in after:
        r.wait("v", tk[0], tk[1])
        r.wait("s", tk[0], tk[1])
        r.wait("t", tk[0], tk[1])
    if c.ubuf_guard:
        for _tk in c.ubuf_guard:
            r.wait("v", _tk[0], _tk[1])
    if f32_out is not None:
        if not hasattr(c, "ln4sem"):
            c.ln4sem = [r.fresh("l4"), r.fresh("l4")]
        ln4_toks = {}
        ln4_red = {}
    last_apply_sc = {}
    for n in range(2):
        mmk_done = {}
        for k in range(DK):
            xsl = c.xres[:, k * TC + n * 512: k * TC + n * 512 + 512]
            if k >= 2:
                r.wait("v", "st", mmk_done[k - 2])
                r.wait("s", "st", mmk_done[k - 2])
            vv = ev("v", "sv", lambda e, xsl=xsl, k=k: e.tensor_copy(c.lnt[:, k % 2, :], xsl))
            vs = ev("s", "sc", lambda e, xsl=xsl, k=k: e.activation(c.lnq[:, k % 2, :], xsl, ACT.Square))
            r.wait("t", "sv", vv)
            r.wait("t", "sc", vs)
            em("t", lambda e, k=k: e.matmul(c.psB[0:1, :], c.onec[:, :], c.lnt[:, k % 2, :],
                                            start=(k == 0), stop=(k == DK - 1)))
            mmk_done[k] = ev("t", "st", lambda e, k=k: e.matmul(
                c.psC[0:1, :], c.onec[:, :], c.lnq[:, k % 2, :],
                start=(k == 0), stop=(k == DK - 1)))
        sums_done = mmk_done[DK - 1]
        r.wait("v", "st", sums_done)
        em("v", lambda e: e.tensor_scalar_mul(c.rowscr[:, 0:512], c.psB[0:1, :], 1.0 / D))
        em("v", lambda e: e.drain())
        em("v", lambda e: e.tensor_mul(c.rowscr[:, 512:1024], c.rowscr[:, 0:512], c.rowscr[:, 0:512]))
        em("v", lambda e: e.drain())
        em("v", lambda e: e.scalar_tensor_tensor(c.rowscr[:, 512:1024], c.psC[0:1, :], 1.0 / D,
                                                 c.rowscr[:, 512:1024], ALU.mult, ALU.subtract))
        em("v", lambda e: e.drain())
        em("v", lambda e: e.tensor_scalar_add(c.rowscr[:, 512:1024], c.rowscr[:, 512:1024], EPS))
        em("v", lambda e: e.drain())
        vv = ev("v", "sv", lambda e: e.reciprocal(c.rowscr[:, 512:1024], c.rowscr[:, 512:1024]))
        r.wait("s", "sv", vv)
        vs = ev("s", "sc", lambda e: e.activation(c.rowscr[:, 512:1024], c.rowscr[:, 512:1024], ACT.Sqrt))
        r.wait("v", "sc", vs)
        vv2 = ev("v", "sv", lambda e: e.tensor_mul(c.rowscr[:, 0:512], c.rowscr[:, 0:512], c.rowscr[:, 512:1024]))
        r.wait("t", "sv", vv2)
        em("t", lambda e: e.matmul(c.psD[:, :], c.onerf[:, :], c.rowscr[0:1, 512:1024], start=True, stop=True))
        bc_done = ev("t", "st", lambda e: e.matmul(c.psE[:, :], c.onerf[:, :], c.rowscr[0:1, 0:512], start=True, stop=True))
        r.wait("v", "st", bc_done)
        r.wait("s", "st", bc_done)
        for k in range(DK):
            xsl = c.xres[:, k * TC + n * 512: k * TC + n * 512 + 512]
            osl = outbuf[:, k * TC + n * 512: k * TC + n * 512 + 512]
            usl = c.ubuf[:, k % 2, :]
            nk = n * DK + k
            if k >= 2:
                r.wait("v", "sc", last_apply_sc[(n, k - 2)])
            if f32_out is not None and nk >= 2:
                r.wait("v", c.ln4sem[k % 2], ln4_toks[nk - 2])
                r.wait("v", "sv", ln4_red[nk - 2])
            em("v", lambda e, xsl=xsl, usl=usl: e.tensor_mul(usl, xsl, c.psD[:, :]))
            em("v", lambda e: e.drain())
            vv = ev("v", "sv", lambda e, usl=usl: e.tensor_sub(usl, usl, c.psE[:, :]))
            r.wait("s", "sv", vv)
            if f32_out is None:
                last_apply_sc[(n, k)] = ev("s", "sc", lambda e, usl=usl, osl=osl, k=k: e.activation(
                    osl, usl, ACT.Identity, bias=gs[:, k:k + 1], scale=gc[:, k:k + 1]))
            else:
                last_apply_sc[(n, k)] = ev("s", "sc", lambda e, usl=usl, k=k: e.activation(
                    usl, usl, ACT.Identity, bias=gs[:, k:k + 1], scale=gc[:, k:k + 1]))
                r.wait("y", "sc", last_apply_sc[(n, k)])
                ln4_toks[nk] = c.dma("y", c.ln4sem[k % 2],
                                     f32_out.ap()[ts(k), n * 512:(n + 1) * 512], usl)
                r.wait("v", "sc", last_apply_sc[(n, k)])
                ln4_red[nk] = ev("v", "sv", lambda e, usl=usl, k=k, n=n: e.reduce_max(
                    c.smx[:, 32 + k * 2 + n:33 + k * 2 + n], usl, axis=AXX,
                    apply_absolute_value=True))
        if n == 0:
            r.wait("t", "sv", r.val("sv"))
    c.ubuf_guard = [("sc", r.val("sc"))]
    if f32_out is not None:
        c.ln4_last = ([(s_, r.val(s_)) for s_ in c.ln4sem], ln4_red[2 * DK - 1])
    return last_apply_sc[(1, DK - 1)]


def emit_dump_bf16(c, srcbuf, dst, after_sc=0):
    r = c.r
    em, ev, dma = c.em, c.ev, c.dma
    if not hasattr(c, "du_sem"):
        c.du_sem = r.fresh("du")
    dsem = c.du_sem
    r.wait("v", "sc", after_sc)
    if c.ubuf_guard:
        for _tk in c.ubuf_guard:
            r.wait("v", _tk[0], _tk[1])
    for k in range(DK):
        vv = ev("v", "sv", lambda e, k=k: e.tensor_copy(c.ubuf[:, 0:2, :].rearrange("p a b -> p (a b)"),
                                                        srcbuf[:, k * TC:(k + 1) * TC]))
        r.wait("y", "sv", vv)
        vv2 = dma("y", dsem, dst[ts(k), :], c.ubuf[:, 0:2, :].rearrange("p a b -> p (a b)"))
        r.wait("y", dsem, vv2)
        r.wait("v", dsem, vv2)
    c.ubuf_guard = [(dsem, r.val(dsem))]
    return r.val(dsem)


def shard_inputs(I):
    maps = []
    x = np.asarray(I["L0_attn_x"])[0]
    WK, WQ, WV = np.asarray(I["L0_attn_W_K"]), np.asarray(I["L0_attn_W_Q"]), np.asarray(I["L0_attn_W_V"])
    for cid in range(NCORE):
        m = {}
        m["x"] = np.ascontiguousarray(x[cid * TC:(cid + 1) * TC, :].T)
        m["l0c"] = np.ascontiguousarray(np.asarray(I["L0_c"]).reshape(D, 1))
        m["l1c"] = np.ascontiguousarray(np.asarray(I["L1_c"]).reshape(D, 1))
        m["l0wmod"] = np.ascontiguousarray(np.asarray(I["L0_wmod"])[:, cid * 1536:(cid + 1) * 1536])
        m["l1wmod"] = np.ascontiguousarray(np.asarray(I["L1_wmod"])[:, cid * 1536:(cid + 1) * 1536])
        qkv = np.concatenate([WK[2 * cid:2 * cid + 2].reshape(256, D),
                              WQ[2 * cid:2 * cid + 2].reshape(256, D),
                              WV[2 * cid:2 * cid + 2].reshape(256, D)], axis=0)
        m["l0qkv"] = np.ascontiguousarray(qkv)
        m["l1qkv"] = np.ascontiguousarray(np.concatenate(
            [np.asarray(I["L1_attn_Wq"])[cid * 256:(cid + 1) * 256, :],
             np.asarray(I["L1_attn_Wk"])[cid * 256:(cid + 1) * 256, :],
             np.asarray(I["L1_attn_Wv"])[cid * 256:(cid + 1) * 256, :],
             np.asarray(I["L1_attn_W_ilr"])[4 * cid:4 * cid + 4, :]], axis=0))
        m["l0wo_s"] = np.ascontiguousarray(np.asarray(I["L0_attn_W_O"])[:, cid * 256:(cid + 1) * 256])
        m["l1wo_s"] = np.ascontiguousarray(np.asarray(I["L1_attn_Wo"])[:, cid * 256:(cid + 1) * 256])
        m["wg"] = np.ascontiguousarray(np.asarray(I["L0_ffn_Wg"]))
        m["moe1s"] = np.ascontiguousarray(np.asarray(I["L0_ffn_W1"])[:, :, cid * 256:(cid + 1) * 256])
        m["moe2s"] = np.ascontiguousarray(np.asarray(I["L0_ffn_W2"])[:, :, cid * 256:(cid + 1) * 256])
        m["moe3s"] = np.ascontiguousarray(np.asarray(I["L0_ffn_W3"])[:, :, cid * 176:(cid + 1) * 176])
        m["bilr"] = np.ascontiguousarray(np.asarray(I["L1_attn_b_ilr"])[4 * cid:4 * cid + 4].reshape(1, 4))
        m["w1h"] = np.ascontiguousarray(np.asarray(I["L1_attn_W1"])[4 * cid:4 * cid + 4].reshape(256, 64))
        m["wfc_s"] = np.ascontiguousarray(np.asarray(I["L1_ffn_w_fc"])[:, cid * 256:(cid + 1) * 256])
        m["wproj_s"] = np.ascontiguousarray(np.asarray(I["L1_ffn_w_proj"])[:, cid * 1024:(cid + 1) * 1024])
        maps.append(m)
    return maps


def kernel(**inputs):
    nc = build()
    maps = shard_inputs(inputs)
    res = run_bass_kernel_spmd(nc, maps, core_ids=list(range(NCORE)))
    full = np.concatenate([np.asarray(r_["out"]).T for r_ in res.results], axis=0)
    return full.reshape(1, T, D).astype(np.float32)


def kernel_debug(inputs):
    nc = build()
    maps = shard_inputs(inputs)
    return run_bass_kernel_spmd(nc, maps, core_ids=list(range(NCORE)))


def emit_dump_f32(c, srcbuf, dst, after_sv=0):
    r = c.r
    dma = c.dma
    if not hasattr(c, "du2_sem"):
        c.du2_sem = r.fresh("du2")
    dsem = c.du2_sem
    r.wait("y", "sv", after_sv)
    r.wait("y", "sc", r.val("sc"))
    for k in range(DK):
        dma("y", dsem, dst[ts(k), :], srcbuf[:, k * TC:(k + 1) * TC])
        r.wait("y", dsem, r.val(dsem))
    return r.val(dsem)


def emit_stage_qkvT(c, src, ncols, col_off=0):
    r, em, ev, dma = c.r, c.em, c.ev, c.dma
    psb = [c.psD.ap().bitcast(BF16), c.psE.ap().bitcast(BF16)]
    if not hasattr(c, "qkT_sems"):
        c.qkT_sems = [r.fresh("qk", "gdma"), r.fresh("qk", "gdma")]
        c.qkT_i = 0
    if c.stg_gate is not None:
        c.r.wait("g", c.stg_gate[0], c.stg_gate[1])
    qsem = c.qkT_sems
    nrt = ncols // 128
    r.wait("t", "sv", r.val("sv"))
    r.wait("t", "sc", r.val("sc"))

    ld_tok = {}
    ev_done = {}
    for i, (rt, half) in enumerate([(rt, h) for rt in range(nrt) for h in range(2)]):
        gi = c.qkT_i + i
        buf = gi % 2
        sem = qsem[buf]
        if i >= 2:
            r.wait("g", "st", ev_done[i - 2])
        elif c.qkT_i > 0:
            r.wait("g", "st", c.qkT_lastev)
        dma("g", sem, c.stg[:, buf, :], src[ts(rt), half * 1024:(half + 1) * 1024])
        ld_tok[i] = (sem, r.val(sem))
        r.wait("t", sem, r.val(sem))
        for grp in range(2):
            pb = psb[grp]
            if i >= 1:
                pass
            for sub in range(4):
                k = half * 8 + grp * 4 + sub
                em("t", lambda e, buf=buf, grp=grp, sub=sub, pb=pb: e.transpose(
                    pb[:, sub * 128:(sub + 1) * 128], c.stg[:, buf, (grp * 4 + sub) * 128:(grp * 4 + sub + 1) * 128],
                    c.idb[:, :]))
            tok = r.mark("t", "st")
            r.wait("v", "st", tok)
            k0 = half * 8 + grp * 4
            vv = ev("v", "sv", lambda e, pb=pb, k0=k0, rt=rt: e.tensor_copy(
                c.wqkvT[:, k0:k0 + 4, col_off + rt * 128: col_off + (rt + 1) * 128], pb[:, 0:512]))
            r.wait("t", "sv", vv)
        ev_done[i] = r.val("st")
    c.qkvT_done = r.val("sv")
    c.qkT_i += 2 * nrt
    c.qkT_lastev = ev_done[2 * nrt - 1]
    c.stg_gate = ("st", r.val("st"))


def emit_stage_agi(c, src, R, C, agi_rows_ap, ag_in_ap, ag_out_ap, tag):
    r, em, ev, dma = c.r, c.em, c.ev, c.dma
    psb = [c.psD.ap().bitcast(BF16), c.psE.ap().bitcast(BF16)]
    if not hasattr(c, "agi_sems"):
        c.agi_sems = [c.r.fresh("ag", "gdma"), c.r.fresh("ag", "gdma")]
        c.agi_i = 0
        c.agi_ev = {}
    if c.stg_gate is not None:
        c.r.wait("g", c.stg_gate[0], c.stg_gate[1])
    gsem = c.agi_sems
    r.wait("t", "sv", r.val("sv"))
    r.wait("t", "sc", r.val("sc"))
    asmv = c.stgf.ap().bitcast(BF16)
    asmv = asmv.rearrange("p a b -> p (a b)")
    cts = []
    off = 0
    while off < C:
        sz = min(128, C - off)
        cts.append((off, sz))
        off += sz
    nrt = (R + 127) // 128
    for rt in range(nrt):
        gi = c.agi_i + rt
        rsz = min(128, R - rt * 128)
        buf = gi % 2
        sem = gsem[buf]
        if gi >= 2:
            r.wait("g", "st", c.agi_ev[gi - 2])
        dma("g", sem, c.stg[0:rsz, buf, 0:C], src[rt * 128: rt * 128 + rsz, :])
        r.wait("t", sem, r.val(sem))
        for ci, (coff, csz) in enumerate(cts):
            pb = psb[gi % 2]
            em("t", lambda e, buf=buf, coff=coff, csz=csz, rsz=rsz, pb=pb, ci=ci: e.transpose(
                pb[0:csz, ci * 128: ci * 128 + rsz], c.stg[0:rsz, buf, coff:coff + csz], c.idb[:, :]))
        tok = r.mark("t", "st")
        c.agi_ev[gi] = tok
        r.wait("v", "st", tok)
        for ci, (coff, csz) in enumerate(cts):
            pb = psb[gi % 2]
            vv = ev("v", "sv", lambda e, pb=pb, ci=ci, csz=csz, rsz=rsz, rt=rt: e.tensor_copy(
                asmv[0:csz, ci * 3072 + rt * 128: ci * 3072 + rt * 128 + rsz], pb[0:csz, ci * 128: ci * 128 + rsz]))
        r.wait("t", "sv", r.val("sv"))
    r.wait("g", "sv", r.val("sv"))
    r.wait("g", "sg", r.val("sg"))
    for ci, (coff, csz) in enumerate(cts):
        dma("g", "sg", agi_rows_ap(ci, csz), asmv[0:csz, ci * 3072: ci * 3072 + R])
    r.wait("g", "sg", r.val("sg"))
    tok = c.cc(lambda e: e.collective_compute(
        "AllGather", ALU.bypass, replica_groups=c.RG,
        ins=[ag_in_ap], outs=[ag_out_ap]))
    c.agi_i += nrt
    c.stg_gate = ("st", r.val("st"))
    return tok


def emit_phase_l0attn(c):
    r, em, ev, dma = c.r, c.em, c.ev, c.dma
    nc = c.nc
    AR = c.arena
    Q0, K0, V0, PT, ZST = 0, 8192, 16384, 24576, 26624
    psb_S = c.psA
    psZ = c.psB
    psSum = c.psC

    emit_stage_woT(c, c.l0wo_s, c.wo0T)
    c.cc_moe = {}
    for e_ in range(2):
        emit_stage_moe_expert(c, e_)

    emit_wmx_pass(c)

    c.waitcc("y", c.cc_h0)
    for hd in range(2):
        emit_qkv_prod(c, hd, Q0, K0, V0, PT)
        emit_stabilizer(c, hd, Q0, K0)
        emit_attn_bands(c, hd, Q0, K0, V0, PT, ZST)
        if hd == 0:
            for e_ in range(2, 5):
                emit_stage_moe_expert(c, e_)
        else:
            for e_ in range(5, 8):
                emit_stage_moe_expert(c, e_)
    if DEBUG and "dbg_z" in c.dbg:
        r.wait("g", c.attn_done[0], c.attn_done[1])
        r.wait("g", "sg", r.val("sg"))
        dma("g", "sg", c.dbg["dbg_z"].ap(), c.zloc0.ap())
    emit_wo_partial(c, c.zloc0, c.wo0T, c.attn_done, L=0, gi_gate=2)


def emit_stage_moe_expert(c, e_):
    r = c.r
    tok1 = emit_stage_agi(c, c.moe1s.ap()[e_], HM, 256,
                          lambda ci, csz, e_=e_: c.agi_m1.ap()[e_, ci * 128: ci * 128 + csz, :],
                          c.agi_m1.ap()[e_], c.ago_m1[e_].ap().opt(), f"m1_{e_}")
    tok2 = emit_stage_agi(c, c.moe2s.ap()[e_], HM, 256,
                          lambda ci, csz, e_=e_: c.agi_m2.ap()[e_, ci * 128: ci * 128 + csz, :],
                          c.agi_m2.ap()[e_], c.ago_m2[e_].ap().opt(), f"m2_{e_}")
    tok3 = emit_stage_agi(c, c.moe3s.ap()[e_], D, 176,
                          lambda ci, csz, e_=e_: c.agi_m3.ap()[e_, ci * 128: ci * 128 + csz, :],
                          c.agi_m3.ap()[e_], c.ago_m3[e_].ap().opt(), f"m3_{e_}")
    c.cc_moe[e_] = (tok1, tok2, tok3)
def emit_wmx_pass(c):
    r, em, ev, dma = c.r, c.em, c.ev, c.dma
    for j, (src, RR, CC) in enumerate(((c.wfc_s, FFH, 256), (c.wproj_s, D, 1024))):
        flat = src.ap().rearrange("a b -> (a b)").rearrange("(n p q) -> n p q", p=128, q=512)
        nld = (RR * CC) // (128 * 512)
        toks = {}
        if not hasattr(c, "wx_sems"):
            c.wx_sems = [r.fresh("wx"), r.fresh("wx")]
        xsem = c.wx_sems
        if c.ubuf_guard:
            for _tk in c.ubuf_guard:
                r.wait("y", _tk[0], _tk[1])
        for i in range(nld):
            buf = i % 2
            sem = xsem[buf]
            if i >= 2:
                r.wait("y", "sv", toks[i - 2])
            dma("y", sem, c.ubuf[:, buf, :], flat[i])
            r.wait("v", sem, r.val(sem))
            toks[i] = ev("v", "sv", lambda e, buf=buf, j=j, i=i: e.reduce_max(
                c.smx[:, j * 32 + i:j * 32 + i + 1], c.ubuf[:, buf, :], axis=AXX,
                apply_absolute_value=True))
        em("v", lambda e: e.drain())
        em("v", lambda e, j=j, nld=nld: e.reduce_max(c.smx2[:, j:j + 1], c.smx[:, j * 32:j * 32 + nld], axis=AXX))
        vg = ev("v", "sv", lambda e: e.drain())
        c.ubuf_guard = [("sv", vg)]
    vv = ev("v", "sv", lambda e: e.drain())
    r.wait("t", "sv", vv)
    r.wait("t", "sc", r.val("sc"))
    tok = ev("t", "st", lambda e: e.transpose(c.psE[0:2, 0:128], c.smx2[:, 0:2], c.idf[:, :]))
    r.wait("v", "st", tok)
    vv = ev("v", "sv", lambda e: e.reduce_max(c.smx2[0:2, 2:3], c.psE[0:2, 0:128], axis=AXX))
    r.wait("g", "sv", vv)
    dma("g", "sg", c.agi_wmx[:, :], c.smx2[0:2, 2:3])
    r.wait("g", "sg", r.val("sg"))
    c.cc_wmx = c.cc(lambda e: e.collective_compute(
        "AllReduce", ALU.max, replica_groups=c.RG,
        ins=[c.agi_wmx.ap().opt()], outs=[c.ago_wmx.ap().opt()]))
    c.waitcc("g", c.cc_wmx)
    r.wait("g", "sg", r.val("sg"))
    dma("g", "sg", c.smx2[0:2, 3:4], c.ago_wmx[:, :])
    r.wait("v", "sg", r.val("sg"))
    em("v", lambda e: e.tensor_scalar(c.smx2[0:2, 3:4], c.smx2[0:2, 3:4], 1.0 / 448.0, 1e-6, ALU.mult, ALU.max))
    em("v", lambda e: e.drain())
    em("v", lambda e: e.reciprocal(c.smx2[0:2, 4:5], c.smx2[0:2, 3:4]))
    em("v", lambda e: e.drain())
    vv = ev("v", "sv", lambda e: e.tensor_scalar_mul(c.smx2[0:2, 4:5], c.smx2[0:2, 4:5], 0.5))
    r.wait("t", "sv", vv)
    tok = ev("t", "st", lambda e: e.transpose(c.psE[0:1, 24:26], c.smx2[0:2, 4:5], c.idf[0:2, 0:2]))
    r.wait("v", "st", tok)
    vv = ev("v", "sv", lambda e: e.tensor_copy(c.smx[0:1, 30:32], c.psE[0:1, 24:26]))
    r.wait("t", "sv", vv)
    em("t", lambda e: e.matmul(c.psE[:, 8:10], c.onerf[:, :], c.smx[0:1, 30:32], start=True, stop=True))
    tok = r.mark("t", "st")
    r.wait("v", "st", tok)
    vv = ev("v", "sv", lambda e: e.tensor_copy(c.smx2[:, 5:7], c.psE[:, 8:10]))
    c.rw_done = vv


def emit_qkv_prod(c, hd, Q0, K0, V0, PT):
    r, em, ev, dma = c.r, c.em, c.ev, c.dma
    AR = c.arena
    kcol = 128 * hd
    qcol = 256 + 128 * hd
    vcol = 512 + 128 * hd
    r.wait("t", "sv", c.qkvT_done)
    r.wait("t", "sv", r.val("sv"))
    r.wait("t", "sc", r.val("sc"))
    r.wait("y", "st", r.val("st"))
    r.wait("y", "sc", r.val("sc"))
    r.wait("y", "sv", r.val("sv"))
    c.waitcc("y", c.cc_h0)
    evac_tok = 0
    if not hasattr(c, "h_sems"):
        c.h_sems = [r.fresh("hs"), r.fresh("hs")]
        c.h_i = 0
        c.h_lastmm = None
    hsem = c.h_sems
    mm_tok = {}
    for r8 in range(8):
        for k in range(DK):
            i = r8 * DK + k
            buf = (c.h_i + i) % 2
            sem = hsem[buf]
            if i >= 2:
                r.wait("y", "st", mm_tok[i - 2])
            elif c.h_lastmm is not None:
                r.wait("y", "st", c.h_lastmm)
            dma("y", sem, AR[:, PT + buf * 1024: PT + (buf + 1) * 1024],
                c.ago_h.ap()[r8 * D + k * 128: r8 * D + (k + 1) * 128, :])
            r.wait("t", sem, r.val(sem))
            if k == 0 and evac_tok:
                r.wait("t", "sv", evac_tok)
                r.wait("t", "sc", evac_sc)
            ht = AR[:, PT + buf * 1024: PT + (buf + 1) * 1024]
            for n in range(2):
                em("t", lambda e, ht=ht, n=n, k=k, qcol=qcol: e.matmul(
                    c.psA[:, n * 512:(n + 1) * 512], c.wqkvT[:, k, qcol:qcol + 128],
                    ht[:, n * 512:(n + 1) * 512], start=(k == 0), stop=(k == DK - 1)))
                em("t", lambda e, ht=ht, n=n, k=k, kcol=kcol: e.matmul(
                    c.psA[:, 1024 + n * 512: 1024 + (n + 1) * 512], c.wqkvT[:, k, kcol:kcol + 128],
                    ht[:, n * 512:(n + 1) * 512], start=(k == 0), stop=(k == DK - 1)))
            for n in range(2):
                ps = c.psB if n == 0 else c.psC
                em("t", lambda e, ht=ht, n=n, k=k, vcol=vcol, ps=ps: e.matmul(
                    ps[:, :], c.wqkvT[:, k, vcol:vcol + 128],
                    ht[:, n * 512:(n + 1) * 512], start=(k == 0), stop=(k == DK - 1)))
            mm_tok[i] = r.mark("t", "st")
        r.wait("s", "st", mm_tok[r8 * DK + DK - 1])
        r.wait("v", "st", mm_tok[r8 * DK + DK - 1])
        for n in range(2):
            em("s", lambda e, r8=r8, n=n, Q0=Q0: e.activation(
                AR[:, Q0 + r8 * 1024 + n * 512: Q0 + r8 * 1024 + (n + 1) * 512],
                c.psA[:, n * 512:(n + 1) * 512], ACT.Copy))
            em("s", lambda e, r8=r8, n=n, K0=K0: e.activation(
                AR[:, K0 + r8 * 1024 + n * 512: K0 + r8 * 1024 + (n + 1) * 512],
                c.psA[:, 1024 + n * 512: 1024 + (n + 1) * 512], ACT.Copy))
        evac_sc = r.mark("s", "sc")
        em("v", lambda e: e.tensor_copy(c.lnq[:, 0, :], c.psB[:, :]))
        vfm = ev("v", "sv", lambda e: e.tensor_copy(c.lnq[:, 1, :], c.psC[:, :]))
        r.wait("t", "sv", vfm)
        psDbf2 = c.psD.ap().bitcast(BF16)
        psEbf2 = c.psE.ap().bitcast(BF16)
        for j in range(8):
            pbx = psDbf2 if j < 4 else psEbf2
            em("t", lambda e, j=j, pbx=pbx: e.transpose(
                pbx[:, (j % 4) * 128:(j % 4 + 1) * 128],
                c.lnq[:, j // 4, (j % 4) * 128:(j % 4 + 1) * 128], c.idb[:, :]))
        ttk = r.mark("t", "st")
        r.wait("v", "st", ttk)
        em("v", lambda e, r8=r8, V0=V0, psDbf2=psDbf2: e.tensor_copy(
            AR[:, V0 + r8 * 1024: V0 + r8 * 1024 + 512], psDbf2[:, 0:512]))
        evac_tok = ev("v", "sv", lambda e, r8=r8, V0=V0, psEbf2=psEbf2: e.tensor_copy(
            AR[:, V0 + r8 * 1024 + 512: V0 + (r8 + 1) * 1024], psEbf2[:, 0:512]))
        r.wait("t", "sv", evac_tok)
    c.qkv_prod_done = (evac_tok, evac_sc)
    c.h_i += 8 * DK
    c.h_lastmm = mm_tok[8 * DK - 1]


def emit_stabilizer(c, hd, Q0, K0):
    r, em, ev, dma = c.r, c.em, c.ev, c.dma
    AR = c.arena
    r.wait("s", "sv", c.qkv_prod_done[0])
    r.wait("t", "sv", r.val("sv"))
    r.wait("t", "sc", r.val("sc"))
    r.wait("s", "st", r.val("st"))
    sq_tok = {}
    for j, base in enumerate((Q0, K0)):
        for k in range(16):
            kk = j * 16 + k
            if kk >= 2:
                r.wait("s", "st", sq_tok[kk - 2])
            vs = ev("s", "sc", lambda e, base=base, k=k: e.activation(
                c.lnq[:, k % 2, :], AR[:, base + k * 512: base + (k + 1) * 512], ACT.Square))
            r.wait("t", "sc", vs)
            sq_tok[kk] = ev("t", "st", lambda e, k=k: e.matmul(
                c.psE[0:1, :], c.onec[:, :], c.lnq[:, k % 2, :], start=True, stop=True))
            r.wait("v", "st", sq_tok[kk])
            ev("v", "sv", lambda e, j=j, k=k: e.reduce_max(
                c.smx[0:1, 32 + j * 16 + k: 33 + j * 16 + k], c.psE[0:1, :], axis=AXX))
            r.wait("t", "sv", r.val("sv"))
    em("v", lambda e: e.drain())
    em("v", lambda e: e.reduce_max(c.smx[0:1, 60:61], c.smx[0:1, 32:48], axis=AXX))
    em("v", lambda e: e.drain())
    em("v", lambda e: e.reduce_max(c.smx[0:1, 61:62], c.smx[0:1, 48:60], axis=AXX) if False else
       e.reduce_max(c.smx[0:1, 61:62], c.smx[0:1, 48:64], axis=AXX))
    em("v", lambda e: e.drain())
    em("v", lambda e: e.tensor_mul(c.smx[0:1, 62:63], c.smx[0:1, 60:61], c.smx[0:1, 61:62]))
    vv = ev("v", "sv", lambda e: e.drain())
    r.wait("s", "sv", vv)
    vs = ev("s", "sc", lambda e: e.activation(c.smx[0:1, 62:63], c.smx[0:1, 62:63], ACT.Sqrt))
    r.wait("v", "sc", vs)
    vv = ev("v", "sv", lambda e: e.tensor_scalar_mul(c.smx[0:1, 63:64], c.smx[0:1, 62:63], -RS128))
    r.wait("t", "sv", vv)
    em("t", lambda e: e.matmul(c.psE[:, 12:13], c.onerf[:, :], c.smx[0:1, 63:64], start=True, stop=True))
    tok = r.mark("t", "st")
    r.wait("v", "st", tok)
    c.bias_done = ev("v", "sv", lambda e: e.tensor_copy(c.smx2[:, 10:11], c.psE[:, 12:13]))


def emit_attn_bands(c, hd, Q0, K0, V0, PT, ZST):
    r, em, ev, dma = c.r, c.em, c.ev, c.dma
    AR = c.arena
    r.wait("t", "sv", c.bias_done)
    r.wait("s", "sv", c.bias_done)
    if not hasattr(c, "z_sem"):
        c.z_sem = c.r.fresh("zs")
        c.lnt1_guard = None
        c.bandend_guard = None
        c.psB_guard = None
        c.band_pend = None
        c.zdma_prev = 0
    zsem = c.z_sem
    prev_vadds = c.band_prev_vadds if hasattr(c, "band_prev_vadds") else None
    for J in range(16):
        nkt = 4 * J + 4
        exp_toks = {}
        vmsk_toks = {}
        vadd_toks = {}
        av_last = 0
        for step in range(nkt + 2):
            kt = step
            if kt < nkt:
                gpar = (kt // 2) % 2
                slot = kt % 4
                if kt >= 4:
                    r.wait("t", "sc", exp_toks[kt - 4])
                em("t", lambda e, kt=kt, J=J, gpar=gpar, K0=K0, Q0=Q0: e.matmul(
                    c.psA[:, gpar * 1024 + (kt % 2) * 512: gpar * 1024 + (kt % 2 + 1) * 512],
                    AR[:, K0 + kt * 128: K0 + (kt + 1) * 128],
                    AR[:, Q0 + J * 512: Q0 + (J + 1) * 512], start=True, stop=True))
                stok = r.mark("t", "st")
                r.wait("s", "st", stok)
                if kt >= 4:
                    r.wait("s", "sv", vadd_toks[kt - 4])
                elif prev_vadds is not None:
                    r.wait("s", "sv", prev_vadds[kt])
                exp_toks[kt] = ev("s", "sc", lambda e, kt=kt, gpar=gpar, slot=slot: e.activation(
                    AR[:, PT + slot * 512: PT + (slot + 1) * 512],
                    c.psA[:, gpar * 1024 + (kt % 2) * 512: gpar * 1024 + (kt % 2 + 1) * 512],
                    ACT.Exp, bias=c.smx2[:, 10:11], scale=RS128))
                r.wait("v", "sc", exp_toks[kt])
                if kt >= 4 * J:
                    rrel = kt - 4 * J
                    if rrel:
                        em("v", lambda e, slot=slot, rrel=rrel: e.memset(
                            AR[:, PT + slot * 512: PT + slot * 512 + rrel * 128], 0.0))
                    em("v", lambda e, slot=slot, rrel=rrel: e.tensor_tensor(
                        AR[:, PT + slot * 512 + rrel * 128: PT + slot * 512 + (rrel + 1) * 128],
                        AR[:, PT + slot * 512 + rrel * 128: PT + slot * 512 + (rrel + 1) * 128],
                        c.mtri[:, :], ALU.mult))
                    vmsk_toks[kt] = ev("v", "sv", lambda e: e.drain())
                if kt == 0:
                    em("v", lambda e, slot=slot: e.tensor_copy(
                        c.psC[:, :], AR[:, PT + slot * 512: PT + (slot + 1) * 512]))
                else:
                    em("v", lambda e, slot=slot: e.tensor_tensor(
                        c.psC[:, :], c.psC[:, :],
                        AR[:, PT + slot * 512: PT + (slot + 1) * 512], ALU.add))
                vadd_toks[kt] = ev("v", "sv", lambda e: e.drain())
            j = step - 2
            if 0 <= j < nkt:
                slot = j % 4
                if j == 0 and c.psB_guard is not None:
                    r.wait("t", "sv", c.psB_guard)
                if j >= 4 * J:
                    r.wait("t", "sv", vmsk_toks[j])
                else:
                    r.wait("t", "sc", exp_toks[j])
                em("t", lambda e, j=j, slot=slot, V0=V0, nkt=nkt: e.matmul(
                    c.psB[:, :], AR[:, V0 + j * 128: V0 + (j + 1) * 128],
                    AR[:, PT + slot * 512: PT + (slot + 1) * 512],
                    start=(j == 0), stop=(j == nkt - 1)))
                av_last = r.mark("t", "st")
        if c.band_pend is not None:
            emit_band_end(c, zsem, c.band_pend)
        r.wait("v", "st", av_last)
        if c.lnt1_guard is not None:
            r.wait("v", c.lnt1_guard[0], c.lnt1_guard[1])
        zv = ev("v", "sv", lambda e: e.tensor_copy(c.lnq[:, 0, :], c.psB[:, :]))
        c.psB_guard = zv
        vsum = ev("v", "sv", lambda e: e.tensor_copy(c.lnt[:, 1, :], c.psC[:, :]))
        em("v", lambda e: e.drain())
        c.band_pend = (J, hd, vsum)
        prev_vadds = [vadd_toks[nkt - 4 + i] for i in range(4)]
    emit_band_end(c, zsem, c.band_pend)
    c.band_pend = None
    c.z_prev = c.zdma_prev
    c.band_prev_vadds = prev_vadds
    c.attn_done = (zsem, c.zdma_prev)


def emit_band_end(c, zsem, pend):
    r, em, ev, dma = c.r, c.em, c.ev, c.dma
    J, hd, vsum = pend
    r.wait("t", "sv", vsum)
    if c.bandend_guard is not None:
        r.wait("t", "sv", c.bandend_guard[0])
        r.wait("t", "sc", c.bandend_guard[1])
    psDbf = c.psD.ap().bitcast(BF16)
    for j in range(4):
        em("t", lambda e, j=j, psDbf=psDbf: e.transpose(
            psDbf[:, j * 128:(j + 1) * 128], c.lnq[:, 0, j * 128:(j + 1) * 128], c.idb[:, :]))
    em("t", lambda e: e.matmul(c.psE[0:1, 0:512], c.onec[:, :], c.lnt[:, 1, :],
                               start=True, stop=True))
    ttok = r.mark("t", "st")
    c.lnt1_guard = ("st", ttok)
    r.wait("v", "st", ttok)
    vv1 = ev("v", "sv", lambda e: e.tensor_copy(c.rowscr[:, 0:512], c.psE[0:1, 0:512]))
    r.wait("t", "sv", vv1)
    for j in range(4):
        em("t", lambda e, j=j: e.transpose(c.psE[:, j:j + 1], c.rowscr[0:1, j * 128:(j + 1) * 128],
                                           c.idf[0:1, 0:1]))
    ttok2 = r.mark("t", "st")
    r.wait("v", "st", ttok2)
    vv = ev("v", "sv", lambda e: e.reciprocal(c.smx[:, 24:28], c.psE[:, 0:4]))
    r.wait("s", "sv", vv)
    if c.zdma_prev:
        r.wait("s", zsem, c.zdma_prev)
    for j in range(4):
        em("s", lambda e, j=j, psDbf=psDbf: e.activation(
            c.lnt[:, 0, j * 128:(j + 1) * 128], psDbf[:, j * 128:(j + 1) * 128],
            ACT.Identity, bias=0.0, scale=c.smx[:, 24 + j:25 + j]))
    sevt = r.mark("s", "sc")
    r.wait("s", "sc", sevt)
    dst = c.zloc0.ap()[J * 512:(J + 1) * 512, hd * 128:(hd + 1) * 128].rearrange(
        "(j p) f -> p j f", p=128)
    dma("s", zsem, dst, c.lnt[:, 0, :].rearrange("p (j f) -> p j f", j=4))
    c.zdma_prev = r.val(zsem)
    c.bandend_guard = (vv, sevt)


def emit_stage_wilr(c):
    r, em, ev, dma = c.r, c.em, c.ev, c.dma
    ilsem = r.fresh("il", "gdma")
    if c.stg_gate is not None:
        r.wait("g", c.stg_gate[0], c.stg_gate[1])
    dma("g", ilsem, c.stg[0:4, 0, :], c.l1qkv[768:772, 0:1024])
    dma("g", ilsem, c.stg[0:4, 1, :], c.l1qkv[768:772, 1024:2048])
    r.wait("t", "sv", r.val("sv"))
    r.wait("t", "sc", r.val("sc"))
    r.wait("t", ilsem, r.val(ilsem))
    psDbf = c.psD.ap().bitcast(BF16)
    for k in range(DK):
        half, off = divmod(k, 8)
        em("t", lambda e, half=half, off=off, k=k, psDbf=psDbf: e.transpose(
            psDbf[0:128, k * 4:(k + 1) * 4], c.stg[0:4, half, off * 128:(off + 1) * 128], c.idb[0:4, 0:4]))
    tok = r.mark("t", "st")
    r.wait("v", "st", tok)
    vv = ev("v", "sv", lambda e, psDbf=psDbf: e.tensor_copy(
        c.wqkvT[:, 0:DK, 768:772], psDbf[:, 0:64].rearrange("p (k f) -> p k f", f=4)))
    r.wait("t", "sv", vv)
    blsem = r.fresh("bl", "gdma")
    dma("g", blsem, c.smx[0:1, 28:32], c.bilr_in[:, :])
    r.wait("t", blsem, r.val(blsem))
    em("t", lambda e: e.matmul(c.psE[:, 16:20], c.onerf[:, :], c.smx[0:1, 28:32], start=True, stop=True))
    tok = r.mark("t", "st")
    r.wait("v", "st", tok)
    c.bilr_done = ev("v", "sv", lambda e: e.tensor_copy(c.smx2[:, 12:16], c.psE[:, 16:20]))
    c.stg_gate = ("st", r.val("st"))
def emit_stage_woT(c, wo_s, woT_dram):
    r, em, ev, dma = c.r, c.em, c.ev, c.dma
    psb = [c.psD.ap().bitcast(BF16), c.psE.ap().bitcast(BF16)]
    if not hasattr(c, "woT_sems"):
        c.woT_sems = [r.fresh("wo", "gdma"), r.fresh("wo", "gdma")]
        c.woT_osem = [r.fresh("woo"), r.fresh("woo")]
        c.woT_g = None
    wsem = c.woT_sems
    osem = c.woT_osem
    if c.stg_gate is not None:
        r.wait("g", c.stg_gate[0], c.stg_gate[1])
    if c.woT_g is not None:
        r.wait("g", "sv", c.woT_g)
    r.wait("t", "sv", r.val("sv"))
    r.wait("t", "sc", r.val("sc"))
    ub = c.ubuf.ap().bitcast(BF16)
    if c.ubuf_guard:
        for _tk in c.ubuf_guard:
            r.wait("v", _tk[0], _tk[1])
    evtoks = {}
    for rt in range(DK):
        buf = rt % 2
        sem = wsem[buf]
        if rt >= 2:
            r.wait("g", "sv", evtoks[rt - 2])
        dma("g", sem, c.stg[:, buf, 0:256], wo_s[ts(rt), :])
        r.wait("t", sem, r.val(sem))
        if rt >= 2:
            r.wait("t", "sv", evtoks[rt - 2])
        pb = psb[rt % 2]
        for ci in range(2):
            em("t", lambda e, buf=buf, ci=ci, pb=pb: e.transpose(
                pb[:, ci * 128:(ci + 1) * 128], c.stg[:, buf, ci * 128:(ci + 1) * 128], c.idb[:, :]))
        tok = r.mark("t", "st")
        r.wait("v", "st", tok)
        if rt >= 2:
            r.wait("v", osem[rt % 2], dmatoks[rt - 2])
        evtoks[rt] = ev("v", "sv", lambda e, pb=pb, buf=buf, ub=ub: e.tensor_copy(
            ub[:, buf, 0:256], pb[:, 0:256]))
        r.wait("s", "sv", evtoks[rt])
        if rt == 0:
            dmatoks = {}
        dmatoks[rt] = dma("s", osem[rt % 2], woT_dram.ap()[:, rt * 128:(rt + 1) * 128].rearrange(
            "(a p) f -> p a f", p=128), ub[:, buf, 0:256].rearrange("p (a f) -> p a f", a=2))
    c.woT_stage_done = [(osem[0], r.val(osem[0])), (osem[1], r.val(osem[1]))]
    c.ubuf_guard = list(c.woT_stage_done)
    c.woT_g = r.val("sv")
    c.stg_gate = ("st", r.val("st"))


def emit_wo_partial(c, zloc, woT_dram, zloc_ready, L, gi_gate):
    r, em, ev, dma = c.r, c.em, c.ev, c.dma
    AR = c.arena
    ga = c.gsl(L, gi_gate)
    r.wait("y", "sc", r.val("sc"))
    r.wait("y", "st", r.val("st"))
    r.wait("y", "sv", r.val("sv"))
    if not hasattr(c, "wop_zl"):
        c.wop_zl = r.fresh("zl")
        c.wop_wl = r.fresh("wl")
        c.wop_ps = [r.fresh("pp"), r.fresh("pp")]
        c.wop_rs = r.fresh("rs")
        c.wop_gate = None
    zl = c.wop_zl
    r.wait("y", zloc_ready[0], zloc_ready[1])
    if c.wop_gate is not None:
        r.wait("y", c.wop_gate[0], c.wop_gate[1])
    for fchunk in range(2):
        for tg in range(8):
            dma("y", zl, AR[:, fchunk * 8192 + tg * 1024: fchunk * 8192 + (tg + 1) * 1024],
                zloc.ap()[tg * 1024:(tg + 1) * 1024, fchunk * 128:(fchunk + 1) * 128],
                transpose=True)
    r.wait("t", zl, r.val(zl))
    woT = AR[:, 16384:20480].rearrange("p (a f) -> p a f", a=2)
    wl = c.wop_wl
    for tkn in c.woT_stage_done:
        r.wait("y", tkn[0], tkn[1])
    dma("y", wl, woT, woT_dram.ap().rearrange("(a p) f -> p a f", p=128))
    r.wait("t", wl, r.val(wl))
    r.wait("t", "sv", r.val("sv"))
    r.wait("t", "sc", r.val("sc"))
    psem = c.wop_ps
    if c.ubuf_guard:
        for _tk in c.ubuf_guard:
            r.wait("s", _tk[0], _tk[1])
    ev_toks = {}
    st_toks = {}
    for tt in range(64):
        for n in range(4):
            psx = c.psA[:, ((tt * 4 + n) % 2) * 1024 + (n % 2) * 512:
                        ((tt * 4 + n) % 2) * 1024 + (n % 2 + 1) * 512]
            idx = tt * 4 + n
            if idx >= 2:
                r.wait("t", "sc", ev_toks[idx - 2])
            for k in range(2):
                em("t", lambda e, tt=tt, n=n, k=k, psx=psx, woT=woT: e.matmul(
                    psx, AR[:, k * 8192 + tt * 128:k * 8192 + (tt + 1) * 128],
                    woT[:, k, n * 512:(n + 1) * 512], start=(k == 0), stop=(k == 1)))
            st_toks[idx] = r.mark("t", "st")
            r.wait("s", "st", st_toks[idx])
            if idx >= 2:
                r.wait("s", psem[(idx % 2)], part_dma[idx - 2])
            ev_toks[idx] = ev("s", "sc", lambda e, tt=tt, n=n, psx=psx: e.activation(
                c.ubuf[:, n % 2, :].bitcast(BF16)[:, 0:512], psx, ACT.Copy))
            r.wait("y", "sc", ev_toks[idx])
            if idx == 0:
                part_dma = {}
            part_dma[idx] = dma("y", psem[idx % 2],
                c.part.ap()[tt * 128:(tt + 1) * 128, n * 512:(n + 1) * 512],
                c.ubuf[:, n % 2, :].bitcast(BF16)[:, 0:512])
    r.wait("g", psem[0], r.val(psem[0]))
    r.wait("g", psem[1], r.val(psem[1]))
    cctok = c.cc(lambda e: e.collective_compute(
        "ReduceScatter", ALU.add, replica_groups=c.RG,
        ins=[c.part.ap().opt()], outs=[c.rsout.ap().opt()]))
    c.waitcc("y", cctok)
    rsem = c.wop_rs
    for k in range(DK):
        if k >= 1:
            r.wait("y", "sv", rs_toks[k - 1])
        dma("y", rsem, c.lnt[:, 0:2, :].rearrange("p a b -> p (a b)"),
            c.rsout.ap()[:, k * 128:(k + 1) * 128], transpose=True)
        r.wait("v", rsem, r.val(rsem))
        if k == 0:
            rs_toks = {}
        rs_toks[k] = ev("v", "sv", lambda e, k=k, ga=ga: e.scalar_tensor_tensor(
            c.xres[:, k * TC:(k + 1) * TC], c.lnt[:, 0:2, :].rearrange("p a b -> p (a b)"),
            ga[:, k:k + 1], c.xres[:, k * TC:(k + 1) * TC], ALU.mult, ALU.add))
    c.wo_done = r.val("sv")
    c.ubuf_guard = [("sc", r.val("sc"))]
    c.wop_gate = ("sv", r.val("sv"))


def emit_phase_moe(c):
    r, em, ev, dma = c.r, c.em, c.ev, c.dma
    AR = c.arena
    ln2_done = emit_ln(c, 0, 3, 4, c.hbuf, after=[("sv", c.wo_done)])

    wgsem = r.fresh("wg", "gdma")
    if c.stg_gate is not None:
        r.wait("g", c.stg_gate[0], c.stg_gate[1])
    dma("g", wgsem, c.stg[0:8, 0, :], c.wg_in[:, 0:1024])
    dma("g", wgsem, c.stg[0:8, 1, :], c.wg_in[:, 1024:2048])
    r.wait("t", "sv", r.val("sv"))
    r.wait("t", "sc", r.val("sc"))
    r.wait("t", wgsem, r.val(wgsem))
    psDbf = c.psD.ap().bitcast(BF16)
    for k in range(DK):
        half, off = divmod(k, 8)
        em("t", lambda e, half=half, off=off, k=k, psDbf=psDbf: e.transpose(
            psDbf[:, k * 8:(k + 1) * 8], c.stg[0:8, half, off * 128:(off + 1) * 128], c.idb[0:8, 0:8]))
    tok = r.mark("t", "st")
    r.wait("v", "st", tok)
    vv = ev("v", "sv", lambda e, psDbf=psDbf: e.tensor_copy(c.wgT[:, :], psDbf[:, 0:128]))
    r.wait("t", "sv", vv)
    c.stg_gate = ("st", r.val("st"))

    r.wait("t", "sc", ln2_done)
    if c.ubuf_guard:
        for _tk in c.ubuf_guard:
            r.wait("v", _tk[0], _tk[1])
    for n in range(2):
        ps = c.psB if n == 0 else c.psC
        for k in range(DK):
            em("t", lambda e, n=n, k=k, ps=ps: e.matmul(
                ps[0:8, :], c.wgT[:, k * 8:(k + 1) * 8],
                c.hbuf[:, k * TC + n * 512: k * TC + (n + 1) * 512],
                start=(k == 0), stop=(k == DK - 1)))
    tok = r.mark("t", "st")
    r.wait("v", "st", tok)
    em("v", lambda e: e.tensor_copy(c.ubuf[0:8, 0, :], c.psB[0:8, :]))
    vv = ev("v", "sv", lambda e: e.tensor_copy(c.ubuf[0:8, 1, :], c.psC[0:8, :]))
    r.wait("t", "sv", vv)
    for cch in range(8):
        half, off = divmod(cch, 4)
        em("t", lambda e, half=half, off=off, cch=cch: e.transpose(
            c.psD[:, cch * 8:(cch + 1) * 8], c.ubuf[0:8, half, off * 128:(off + 1) * 128],
            c.idf[0:8, 0:8]))
    tok = r.mark("t", "st")
    r.wait("v", "st", tok)
    vv = ev("v", "sv", lambda e: e.tensor_copy(c.smx[:, 0:64], c.psD[:, 0:64]))
    r.wait("t", "sv", vv)
    em("v", lambda e: e.drain())

    s3 = c.smx[:, 0:64].rearrange("p (c e) -> p c e", e=8)
    m1 = c.ubuf[:, 0, 0:8]
    b2 = c.ubuf[:, 0, 8:16]
    t64 = c.ubuf[:, 1, 0:64]
    t64r = t64.rearrange("p (c e) -> p c e", e=8)
    d64 = c.ubuf[:, 0, 64:128]
    em("v", lambda e: e.reduce_max(m1, s3, axis=AXX))
    em("v", lambda e: e.drain())
    em("v", lambda e: e.tensor_tensor(s3, s3, m1.rearrange("p c -> p c ()").broadcast_to([128, 8, 8]), ALU.subtract))
    vv = ev("v", "sv", lambda e: e.drain())
    r.wait("s", "sv", vv)
    vs = ev("s", "sc", lambda e: e.activation(d64, c.smx[:, 0:64], ACT.Relu, bias=1.0, scale=1e30))
    r.wait("v", "sc", vs)
    em("v", lambda e: e.tensor_scalar_min(d64, d64, 1.0))
    em("v", lambda e: e.drain())
    em("v", lambda e: e.scalar_tensor_tensor(t64, d64, -1e4, c.smx[:, 0:64], ALU.mult, ALU.add))
    em("v", lambda e: e.drain())
    em("v", lambda e: e.reduce_max(b2, t64r, axis=AXX))
    em("v", lambda e: e.drain())
    em("v", lambda e: e.tensor_tensor(t64r, s3, b2.rearrange("p c -> p c ()").broadcast_to([128, 8, 8]), ALU.subtract))
    vv = ev("v", "sv", lambda e: e.drain())
    r.wait("s", "sv", vv)
    vs = ev("s", "sc", lambda e: e.activation(t64, t64, ACT.Relu, bias=1.0, scale=1e30))
    em("s", lambda e: e.activation(c.smx[:, 0:64], c.smx[:, 0:64], ACT.Exp))
    vs = ev("s", "sc", lambda e: e.activation(b2, b2, ACT.Exp))
    r.wait("v", "sc", vs)
    em("v", lambda e: e.tensor_scalar_min(t64, t64, 1.0))
    em("v", lambda e: e.drain())
    em("v", lambda e: e.tensor_scalar_add(b2, b2, 1.0))
    em("v", lambda e: e.drain())
    em("v", lambda e: e.reciprocal(b2, b2))
    em("v", lambda e: e.drain())
    em("v", lambda e: e.tensor_mul(c.wmoe[:, :], t64, c.smx[:, 0:64]))
    em("v", lambda e: e.drain())
    em("v", lambda e: e.tensor_tensor(c.wmoe.ap().rearrange("p (cc ee) -> p cc ee", ee=8),
                                      c.wmoe.ap().rearrange("p (cc ee) -> p cc ee", ee=8),
                                      b2.rearrange("p c -> p c ()").broadcast_to([128, 8, 8]), ALU.mult))
    w_done = ev("v", "sv", lambda e: e.drain())
    c.ubuf_guard = [("sv", w_done)]
    if DEBUG and "dbg_w" in c.dbg:
        wds = r.fresh("wd")
        r.wait("y", "sv", w_done)
        dma("y", wds, c.dbg["dbg_w"].ap(), c.wmoe[:, :])
        r.wait("y", wds, r.val(wds))

    P = c.wqkvT.ap().rearrange("p k f -> p (k f)")
    Pv = P[:, 0:11264].rearrange("p (h t) -> p h t", t=1024)
    YB = AR[:, 0:16384].rearrange("p (tt f) -> p tt f", f=2048)
    msem = [r.fresh("mo"), r.fresh("mo"), r.fresh("mo"), r.fresh("mo")]
    w3sem = [r.fresh("w3"), r.fresh("w3")]
    r.wait("y", "st", r.val("st"))
    r.wait("y", "sc", r.val("sc"))
    r.wait("y", "sv", r.val("sv"))
    slabs = [(i, 1) for i in range(HMC)]
    pe_done = {}
    w3_done = {}
    for e_ in range(NE):
        c.waitcc("y", c.cc_moe[e_][0])
        c.waitcc("y", c.cc_moe[e_][1])
        r.release(c.cc_moe[e_][0][0]); r.release(c.cc_moe[e_][1][0])
        for si, (h0, hn) in enumerate(slabs):
            sidx = e_ * HMC + si
            bpar = sidx % 2
            if si < 2 and e_ > 0:
                r.wait("y", "st", w3_done[e_ * 8 - 1])
            if sidx >= 2:
                r.wait("y", "st", pe_done[sidx - 2])
            dma("y", msem[bpar * 2], AR[:, 16384 + bpar * 2048: 16384 + (bpar + 1) * 2048].rearrange(
                "p (k h) -> p k h", k=DK),
                c.ago_m1[e_].ap().rearrange("(k p) h -> p k h", p=128)[:, :, h0 * 128:(h0 + hn) * 128])
            dma("y", msem[bpar * 2 + 1],
                AR[:, 20480 + bpar * 2048: 20480 + (bpar + 1) * 2048].rearrange(
                "p (k h) -> p k h", k=DK),
                c.ago_m2[e_].ap().rearrange("(k p) h -> p k h", p=128)[:, :, h0 * 128:(h0 + hn) * 128])
            r.wait("t", msem[bpar * 2], r.val(msem[bpar * 2]))
            r.wait("t", msem[bpar * 2 + 1], r.val(msem[bpar * 2 + 1]))
            for hh in range(hn):
                hc = h0 + hh
                for n in range(2):
                    pgrp = (hc * 2 + n) % 2
                    ps1 = c.psB if pgrp == 0 else c.psD
                    ps2 = c.psC if pgrp == 0 else c.psE
                    if True:
                        r.wait("t", "sc", r.val("sc")) if False else None
                    for k in range(DK):
                        em("t", lambda e, bpar=bpar, k=k, n=n, ps1=ps1: e.matmul(
                            ps1[:, :],
                            AR[:, 16384 + bpar * 2048:16384 + (bpar + 1) * 2048].rearrange(
                                "p (kk h) -> p kk h", kk=DK)[:, k, :],
                            c.hbuf[:, k * TC + n * 512: k * TC + (n + 1) * 512],
                            start=(k == 0), stop=(k == DK - 1)))
                    for k in range(DK):
                        em("t", lambda e, bpar=bpar, k=k, n=n, ps2=ps2: e.matmul(
                            ps2[:, :],
                            AR[:, 20480 + bpar * 2048:20480 + (bpar + 1) * 2048].rearrange(
                                "p (kk h) -> p kk h", kk=DK)[:, k, :],
                            c.hbuf[:, k * TC + n * 512: k * TC + (n + 1) * 512],
                            start=(k == 0), stop=(k == DK - 1)))
                    mmk = r.mark("t", "st")
                    r.wait("s", "st", mmk)
                    vs = ev("s", "sc", lambda e, pgrp=pgrp, ps1=ps1: e.activation(
                        c.lnq[:, pgrp, :], ps1[:, :], ACT.Silu))
                    r.wait("v", "sc", vs)
                    pvv = ev("v", "sv", lambda e, pgrp=pgrp, hc=hc, n=n, ps2=ps2: e.tensor_tensor(
                        Pv[:, hc, n * 512:(n + 1) * 512], c.lnq[:, pgrp, :], ps2[:, :], ALU.mult))
                    r.wait("t", "sv", pvv)
            pe_done[sidx] = r.val("st")
        c.waitcc("y", c.cc_moe[e_][2])
        r.release(c.cc_moe[e_][2][0])
        for nh in range(8):
            widx = e_ * 8 + nh
            wpar = widx % 2
            if nh == 0:
                r.wait("y", "st", pe_done[e_ * HMC + HMC - 1])
            if widx >= 2:
                r.wait("y", "st", w3_done[widx - 2])
            dma("y", w3sem[wpar], AR[:, 16384 + wpar * 2816: 16384 + (wpar + 1) * 2816].rearrange(
                "p (k d) -> p k d", k=HMC),
                c.ago_m3[e_].ap().rearrange("(k p) d -> p k d", p=128)[:, :, nh * 256:(nh + 1) * 256])
            r.wait("t", w3sem[wpar], r.val(w3sem[wpar]))
            for mg in range(2):
                for mm_ in range(4):
                    tt = mg * 4 + mm_
                    psx = c.psA[:, mm_ * 512: mm_ * 512 + 256]
                    for k in range(HMC):
                        em("t", lambda e, wpar=wpar, k=k, tt=tt, psx=psx: e.matmul(
                            psx, Pv[:, k, tt * 128:(tt + 1) * 128],
                            AR[:, 16384 + wpar * 2816:16384 + (wpar + 1) * 2816].rearrange(
                                "p (kk d) -> p kk d", kk=HMC)[:, k, :],
                            start=(k == 0), stop=(k == HMC - 1)))
                    mmk = r.mark("t", "st")
                    r.wait("v", "st", mmk)
                    wcol = c.wmoe[:, tt * 8 + e_: tt * 8 + e_ + 1]
                    if e_ == 0:
                        lv = ev("v", "sv", lambda e, tt=tt, nh=nh, psx=psx, wcol=wcol: e.tensor_scalar(
                            YB[:, tt, nh * 256:(nh + 1) * 256], psx, wcol, None, ALU.mult))
                    else:
                        lv = ev("v", "sv", lambda e, tt=tt, nh=nh, psx=psx, wcol=wcol: e.scalar_tensor_tensor(
                            YB[:, tt, nh * 256:(nh + 1) * 256], psx, wcol,
                            YB[:, tt, nh * 256:(nh + 1) * 256], ALU.mult, ALU.add))
                    r.wait("t", "sv", lv)
            w3_done[widx] = r.val("st")
    gb = c.gsl(0, 5)
    ydone = r.val("sv")
    r.wait("t", "sv", ydone)
    psDbf = c.psD.ap().bitcast(BF16)
    psEbf = c.psE.ap().bitcast(BF16)
    for dc in range(DK):
        for grp in range(2):
            pb = psDbf if grp == 0 else psEbf
            for j in range(4):
                tt = grp * 4 + j
                em("t", lambda e, dc=dc, tt=tt, j=j, pb=pb: e.transpose(
                    pb[:, j * 128:(j + 1) * 128], YB[:, tt, dc * 128:(dc + 1) * 128], c.idb[:, :]))
            tok = r.mark("t", "st")
            r.wait("v", "st", tok)
            vv = ev("v", "sv", lambda e, dc=dc, grp=grp, pb=pb, gb=gb: e.scalar_tensor_tensor(
                c.xres[:, dc * TC + grp * 512: dc * TC + (grp + 1) * 512], pb[:, 0:512],
                gb[:, dc:dc + 1], c.xres[:, dc * TC + grp * 512: dc * TC + (grp + 1) * 512],
                ALU.mult, ALU.add))
            r.wait("t", "sv", vv)
    c.moe_done = r.val("sv")


def emit_phase_l1attn(c):
    r, em, ev, dma = c.r, c.em, c.ev, c.dma
    AR = c.arena
    ln3_done = emit_ln(c, 1, 0, 1, c.hbuf, after=[("sv", c.moe_done)])
    r.wait("g", "sc", ln3_done)
    r.wait("g", "sg", r.val("sg"))
    for k in range(DK):
        dma("g", "sg", c.agi_h[ts(k), :], c.hbuf[:, k * TC:(k + 1) * TC])
    r.wait("g", "sg", r.val("sg"))
    c.cc_h1 = c.cc(lambda e: e.collective_compute(
        "AllGather", ALU.bypass, replica_groups=c.RG,
        ins=[c.agi_h.ap().opt()], outs=[c.ago_h.ap().opt()]))

    emit_stage_qkvT(c, c.l1qkv, 768)
    emit_stage_wilr(c)

    emit_stage_woT(c, c.l1wo_s, c.wo1T)

    if SUBPH >= 2:
        emit_fp8_stage(c)

    if SUBPH >= 3:
        emit_l1_scan(c)

    if SUBPH >= 4:
        emit_wo_partial(c, c.zloc1, c.wo1T, c.scan_done, L=1, gi_gate=2)


def emit_fp8_stage(c):
    r, em, ev, dma = c.r, c.em, c.ev, c.dma
    AR8 = c.arena.ap().bitcast(FP8)
    fsem = [r.fresh("f8", "gdma"), r.fresh("f8", "gdma")]
    osem = r.fresh("f8o")
    r.wait("t", "sv", r.val("sv"))
    r.wait("t", "sc", r.val("sc"))
    r.wait("v", "sv", c.rw_done)
    ev_t = {}
    for rt in range(64):
        buf = rt % 2
        if rt >= 2:
            r.wait("g", "st", ev_t[rt - 2])
        dma("g", fsem[buf], c.stgf[:, buf, 0:256], c.wfc_s[ts(rt), :])
        r.wait("t", fsem[buf], r.val(fsem[buf]))
        if rt >= 2:
            r.wait("t", "sv", q_done[rt - 2])
        ps = c.psD if buf == 0 else c.psE
        for ci in range(2):
            em("t", lambda e, buf=buf, ci=ci, ps=ps: e.transpose(
                ps[:, ci * 128:(ci + 1) * 128], c.stgf[:, buf, ci * 128:(ci + 1) * 128], c.idf[:, :]))
        ev_t[rt] = r.mark("t", "st")
        r.wait("v", "st", ev_t[rt])
        for ci in range(2):
            vv = ev("v", "sv", lambda e, rt=rt, ci=ci, ps=ps: e.tensor_scalar(
                AR8[:, ci * 8192 + rt * 128: ci * 8192 + (rt + 1) * 128],
                ps[:, ci * 128:(ci + 1) * 128], c.smx2[:, 5:6], None, ALU.mult))
        if rt == 0:
            q_done = {}
        q_done[rt] = r.val("sv")
    r.wait("g", "sv", r.val("sv"))
    r.wait("g", "sg", r.val("sg"))
    for ci in range(2):
        dma("g", "sg", c.agi_fc8.ap()[ci * 128:(ci + 1) * 128, :], AR8[:, ci * 8192:(ci + 1) * 8192])
    r.wait("g", "sg", r.val("sg"))
    c.cc_fc8 = c.cc(lambda e: e.collective_compute(
        "AllGather", ALU.bypass, replica_groups=c.RG,
        ins=[c.agi_fc8.ap().opt()], outs=[c.ago_fc8.ap().opt()]))
    ev_t2 = {}
    q2 = {}
    for rt in range(16):
        buf = rt % 2
        if rt >= 2:
            r.wait("g", "st", ev_t2[rt - 2])
        dma("g", fsem[buf], c.stgf[:, buf, 0:1024], c.wproj_s[ts(rt), :])
        r.wait("t", fsem[buf], r.val(fsem[buf]))
        if rt >= 2:
            r.wait("t", "sv", q2[rt - 2])
        ps = c.psD if buf == 0 else c.psE
        for ci in range(8):
            if ci == 4:
                tk = r.mark("t", "st")
                r.wait("v", "st", tk)
                for cj in range(4):
                    vv = ev("v", "sv", lambda e, rt=rt, cj=cj, ps=ps: e.tensor_scalar(
                        AR8[:, cj * 2048 + rt * 128: cj * 2048 + (rt + 1) * 128],
                        ps[:, cj * 128:(cj + 1) * 128], c.smx2[:, 6:7], None, ALU.mult))
                r.wait("t", "sv", r.val("sv"))
            em("t", lambda e, buf=buf, ci=ci, ps=ps: e.transpose(
                ps[:, (ci % 4) * 128:(ci % 4 + 1) * 128], c.stgf[:, buf, ci * 128:(ci + 1) * 128], c.idf[:, :]))
        ev_t2[rt] = r.mark("t", "st")
        r.wait("v", "st", ev_t2[rt])
        for cj in range(4):
            vv = ev("v", "sv", lambda e, rt=rt, cj=cj, ps=ps: e.tensor_scalar(
                AR8[:, (4 + cj) * 2048 + rt * 128: (4 + cj) * 2048 + (rt + 1) * 128],
                ps[:, cj * 128:(cj + 1) * 128], c.smx2[:, 6:7], None, ALU.mult))
        q2[rt] = r.val("sv")
    r.wait("g", "sv", r.val("sv"))
    r.wait("g", "sg", r.val("sg"))
    for ci in range(8):
        dma("g", "sg", c.agi_pj8.ap()[ci * 128:(ci + 1) * 128, :], AR8[:, ci * 2048:(ci + 1) * 2048])
    r.wait("g", "sg", r.val("sg"))
    c.cc_pj8 = c.cc(lambda e: e.collective_compute(
        "AllGather", ALU.bypass, replica_groups=c.RG,
        ins=[c.agi_pj8.ap().opt()], outs=[c.ago_pj8.ap().opt()]))
def emit_l1_scan(c):
    r, em, ev, dma = c.r, c.em, c.ev, c.dma
    AR = c.arena
    ln3_sc = None
    SBB = [0, 3264]
    WS = 6528
    c.WS = WS
    HTB = WS + 4928
    arf = AR.ap().bitcast(F32)
    c.wst = arf[:, (WS + 5952) // 2:(WS + 5952) // 2 + 128]
    c.wsb2 = AR[:, WS + 6336: WS + 6592]
    wsem = r.fresh("w1l", "gdma")
    for h in range(4):
        dma("g", wsem, c.wst[(h % 2) * 64:(h % 2) * 64 + 64, (h // 2) * 64:(h // 2) * 64 + 64],
            c.w1h_in[h * 64:(h + 1) * 64, :])
    r.wait("v", wsem, r.val(wsem))
    if c.ubuf_guard:
        for _tk in c.ubuf_guard:
            r.wait("v", _tk[0], _tk[1])
    em("v", lambda e: e.memset(c.wsb2, 0.0))
    em("v", lambda e: e.drain())
    for h in range(4):
        em("v", lambda e, h=h: e.tensor_copy(
            c.wsb2[(h % 2) * 64:(h % 2) * 64 + 64,
                   (h // 2) * 128 + (h % 2) * 64:(h // 2) * 128 + (h % 2) * 64 + 64],
            c.wst[(h % 2) * 64:(h % 2) * 64 + 64, (h // 2) * 64:(h // 2) * 64 + 64]))
    wb_ready = ev("v", "sv", lambda e: e.drain())

    ssem = [r.fresh("sp"), r.fresh("sp")]
    zsem1 = r.fresh("zs1")
    c.zprev_scan = 0
    proj_done = {}

    def proj_strip(s):
        base = SBB[s % 2]
        r8, half = divmod(s, 2)
        r.wait("t", "sv", r.val("sv"))
        r.wait("t", "sc", r.val("sc"))
        ld_tok = {}
        mmk_tok = {}
        for k in range(DK):
            gk = s * DK + k
            hpar = gk % 2
            if k >= 2:
                r.wait("y", "st", mmk_tok[k - 2])
            elif s > 0:
                r.wait("y", "st", proj_done[s - 1])
            dma("y", ssem[hpar], AR[:, HTB + hpar * 512: HTB + (hpar + 1) * 512],
                c.ago_h.ap()[r8 * D + k * 128: r8 * D + (k + 1) * 128,
                             half * 512:(half + 1) * 512])
            r.wait("t", ssem[hpar], r.val(ssem[hpar]))
            ht = AR[:, HTB + hpar * 512: HTB + (hpar + 1) * 512]
            for mat, coloff, aroff in (("b", 256, 0), ("c", 0, 1024)):
                for g in range(2):
                    em("t", lambda e, k=k, g=g, coloff=coloff, aroff=aroff, ht=ht: e.matmul(
                        c.psA[:, aroff + g * 512: aroff + (g + 1) * 512],
                        c.wqkvT[:, k, coloff + g * 128: coloff + (g + 1) * 128],
                        ht, start=(k == 0), stop=(k == DK - 1)))
            psavv = [c.psB, c.psC, c.psD, c.psE]
            for sc_ in range(4):
                em("t", lambda e, k=k, sc_=sc_, ht=ht, psavv=psavv: e.matmul(
                    psavv[sc_][:, 0:260], ht[:, sc_ * 128:(sc_ + 1) * 128],
                    c.wqkvT[:, k, 512:772], start=(k == 0), stop=(k == DK - 1)))
            mmk_tok[k] = r.mark("t", "st")
        t_ = mmk_tok[DK - 1]
        r.wait("s", "st", t_)
        r.wait("v", "st", t_)
        for g in range(2):
            em("s", lambda e, g=g, base=base: e.activation(
                AR[:, base + g * 512: base + (g + 1) * 512],
                c.psA[:, g * 512:(g + 1) * 512], ACT.Copy))
            em("s", lambda e, g=g, base=base: e.activation(
                AR[:, base + 1024 + g * 512: base + 1024 + (g + 1) * 512],
                c.psA[:, 1024 + g * 512: 1024 + (g + 1) * 512], ACT.Copy))
        psavv = [c.psB, c.psC, c.psD, c.psE]
        for sc_ in range(4):
            em("s", lambda e, sc_=sc_, base=base, psavv=psavv: e.activation(
                AR[:, base + 2048 + sc_ * 256: base + 2048 + (sc_ + 1) * 256],
                psavv[sc_][:, 0:256], ACT.Copy))
            em("v", lambda e, sc_=sc_, psavv=psavv: e.tensor_tensor(
                c.ubuf[:, 1, sc_ * 4:(sc_ + 1) * 4], psavv[sc_][:, 256:260],
                c.smx2[:, 12:16], ALU.add))
        a_sc = r.mark("s", "sc")
        vv = ev("v", "sv", lambda e: e.drain())
        r.wait("s", "sv", vv)
        vs = ev("s", "sc", lambda e: e.activation(c.ubuf[:, 1, 0:16], c.ubuf[:, 1, 0:16], ACT.Sigmoid))
        r.wait("v", "sc", vs)
        em("v", lambda e: e.tensor_tensor(
            c.ubuf[:, 1, 0:16].rearrange("p (s f) -> p s f", f=4),
            c.ubuf[:, 1, 0:16].rearrange("p (s f) -> p s f", f=4),
            c.tkx.rearrange("p o -> p () o").broadcast_to([128, 4, 4]), ALU.mult))
        em("v", lambda e: e.drain())
        cof = AR.ap().bitcast(F32)
        cov = ev("v", "sv", lambda e, base=base, cof=cof: e.tensor_copy(
            cof[:, (base + 3072) // 2:(base + 3072) // 2 + 16], c.ubuf[:, 1, 0:16]))
        proj_done[s] = r.val("st")
        return cov

    r.wait("t", "sv", r.val("sv"))
    r.wait("t", "sc", r.val("sc"))
    r.wait("y", "st", r.val("st"))
    r.wait("y", "sc", r.val("sc"))
    r.wait("y", "sv", r.val("sv"))
    c.waitcc("y", c.cc_h1)
    r.wait("t", "sv", c.qkvT_done)
    r.wait("t", "sv", c.bilr_done)
    r.wait("t", "sv", wb_ready)
    cov = proj_strip(0)
    for s in range(16):
        cov_next = proj_strip(s + 1) if s < 15 else None
        base = SBB[s % 2]
        r.wait("v", "sv", cov)
        r.wait("t", "sv", cov)
        r.wait("t", "sc", r.val("sc"))
        for sc_ in range(4):
            emit_scan_sc(c, s, sc_, base, zsem1)
        cov = cov_next
    c.scan_done = (zsem1, c.zprev_scan)


def emit_scan_sc(c, s, sc_, base, zsem1):
    r, em, ev, dma = c.r, c.em, c.ev, c.dma
    AR = c.arena
    WS = c.WS
    LP, K1o, GUP, GBCN, DTT = WS, WS + 512, WS + 1024, WS + 1536, WS + 2048
    RPo, T1o, T2o, VVo, Z1o = WS + 2560, WS + 2816, WS + 3072, WS + 3328, WS + 3584
    BTMo, L2o, ZBo = WS + 3840, WS + 4096, WS + 4608
    sci = s * 4 + sc_
    if SCSTEP < 1:
        return
    cof = AR.ap().bitcast(F32)
    co4 = cof[:, (base + 3072) // 2 + sc_ * 4:(base + 3072) // 2 + (sc_ + 1) * 4]
    co4b = co4.rearrange("p f -> p f ()").broadcast_to([128, 4, 128])
    co4b64 = co4.rearrange("p f -> p f ()").broadcast_to([128, 4, 64])
    mBLb = c.mBL.ap().rearrange("p s2 -> p () s2").broadcast_to([128, 4, 128])
    mBUb = c.mBUn.ap().rearrange("p s2 -> p () s2").broadcast_to([128, 4, 128])
    mBDb = c.mBD.ap().rearrange("p s2 -> p () s2").broadcast_to([128, 4, 128])

    def bsl(h):
        g = h // 2
        return AR[(h % 2) * 64:(h % 2) * 64 + 64,
                  base + g * 512 + sc_ * 128: base + g * 512 + (sc_ + 1) * 128]

    def csl(h):
        g = h // 2
        return AR[(h % 2) * 64:(h % 2) * 64 + 64,
                  base + 1024 + g * 512 + sc_ * 128: base + 1024 + g * 512 + (sc_ + 1) * 128]

    def wsb2col(h):
        return c.wsb2[:, (h // 2) * 128 + (h % 2) * 64:(h // 2) * 128 + (h % 2) * 64 + 64]

    def pairsl(g):
        return AR[:, base + g * 512 + sc_ * 128: base + g * 512 + (sc_ + 1) * 128]

    def cpairsl(g):
        return AR[:, base + 1024 + g * 512 + sc_ * 128: base + 1024 + g * 512 + (sc_ + 1) * 128]

    for h in range(4):
        em("t", lambda e, h=h: e.matmul(c.psD[:, h * 64:(h + 1) * 64], pairsl(h // 2), wsb2col(h),
                                        start=True, stop=True))
    for h in (0, 2):
        em("t", lambda e, h=h: e.matmul(c.psB[:, h * 128:(h + 1) * 128], bsl(h), bsl(h),
                                        start=True, stop=True))
        em("t", lambda e, h=h: e.matmul(c.psC[:, h * 128:(h + 1) * 128], bsl(h), csl(h),
                                        start=True, stop=True))
    swk = r.mark("t", "st")
    r.wait("t", "st", swk)
    for h in (1, 3):
        em("t", lambda e, h=h: e.matmul(c.psB[:, h * 128:(h + 1) * 128], bsl(h), bsl(h),
                                        start=True, stop=True))
        em("t", lambda e, h=h: e.matmul(c.psC[:, h * 128:(h + 1) * 128], bsl(h), csl(h),
                                        start=True, stop=True))
    g1 = r.mark("t", "st")
    if SCSTEP < 2:
        return
    r.wait("v", "st", g1)
    psBr = c.psB.ap().rearrange("p (h t2) -> p h t2", t2=128)
    psCr = c.psC.ap().rearrange("p (h t2) -> p h t2", t2=128)
    LPr = AR[:, LP:LP + 512].rearrange("p (h t2) -> p h t2", t2=128)
    em("v", lambda e, LPr=LPr, psBr=psBr, co4b=co4b: e.tensor_tensor(LPr, psBr, co4b, ALU.mult))
    em("v", lambda e: e.drain())
    lp_done = ev("v", "sv", lambda e, LPr=LPr, mBLb=mBLb: e.tensor_tensor(LPr, LPr, mBLb, ALU.mult))
    em("v", lambda e, psBr=psBr, mBUb=mBUb: e.tensor_tensor(
        AR[:, GUP:GUP + 512].rearrange("p (h t2) -> p h t2", t2=128), psBr, mBUb, ALU.mult))
    em("v", lambda e, psCr=psCr, mBUb=mBUb: e.tensor_tensor(
        AR[:, GBCN:GBCN + 512].rearrange("p (h t2) -> p h t2", t2=128), psCr, mBUb, ALU.mult))
    gup_done = ev("v", "sv", lambda e, psCr=psCr, mBDb=mBDb: e.tensor_tensor(
        AR[:, DTT:DTT + 512].rearrange("p (h t2) -> p h t2", t2=128), psCr, mBDb, ALU.mult))
    em("v", lambda e, base=base: e.tensor_sub(
        c.ubuf[:, 0, 0:256], c.psD[:, 0:256],
        AR[:, base + 2048 + sc_ * 256: base + 2048 + (sc_ + 1) * 256]))
    rr_done = ev("v", "sv", lambda e: e.drain())
    em("v", lambda e, co4b64=co4b64: e.tensor_tensor(
        AR[:, RPo:RPo + 256].rearrange("p (h t2) -> p h t2", t2=64),
        c.ubuf[:, 0, 0:256].rearrange("p (h t2) -> p h t2", t2=64), co4b64, ALU.mult))
    em("v", lambda e, co4=co4: e.tensor_scalar_mul(c.ubuf[:, 0, 260:264], co4, -1.0))
    rp_done = ev("v", "sv", lambda e: e.drain())
    if SCSTEP < 3:
        return
    r.wait("t", "sv", lp_done)
    psEbf = c.psE.ap().bitcast(BF16)
    for h in range(4):
        em("t", lambda e, h=h, psEbf=psEbf: e.transpose(
            psEbf[:, h * 128:(h + 1) * 128], AR[:, LP + h * 128: LP + (h + 1) * 128], c.idb[:, :]))
    k1t = r.mark("t", "st")
    r.wait("v", "st", k1t)
    k1_done = ev("v", "sv", lambda e, psEbf=psEbf: e.tensor_copy(AR[:, K1o:K1o + 512], psEbf[:, 0:512]))
    if SCSTEP < 4:
        return
    r.wait("t", "sv", k1_done)
    for h in range(4):
        em("t", lambda e, h=h: e.matmul(c.psE[:, h * 128:(h + 1) * 128],
                                        AR[:, K1o + h * 128: K1o + (h + 1) * 128],
                                        AR[:, LP + h * 128: LP + (h + 1) * 128],
                                        start=True, stop=True))
    l2t = r.mark("t", "st")
    r.wait("v", "st", l2t)
    l2_done = ev("v", "sv", lambda e: e.tensor_copy(AR[:, L2o:L2o + 512], c.psE[:, 0:512]))
    r.wait("t", "sv", l2_done)
    for h in range(4):
        em("t", lambda e, h=h: e.matmul(c.psE[:, h * 128:(h + 1) * 128],
                                        AR[:, LP + h * 128: LP + (h + 1) * 128],
                                        AR[:, K1o + h * 128: K1o + (h + 1) * 128],
                                        start=True, stop=True))
    k2t = r.mark("t", "st")
    r.wait("v", "st", k2t)
    k2_done = ev("v", "sv", lambda e: e.tensor_copy(AR[:, LP:LP + 512], c.psE[:, 0:512]))
    r.wait("t", "sv", k2_done)
    for h in range(4):
        em("t", lambda e, h=h: e.matmul(c.psE[:, h * 128:(h + 1) * 128],
                                        AR[:, L2o + h * 128: L2o + (h + 1) * 128],
                                        AR[:, LP + h * 128: LP + (h + 1) * 128],
                                        start=True, stop=True))
    k4t = r.mark("t", "st")
    r.wait("v", "st", k4t)
    k4_done = ev("v", "sv", lambda e: e.tensor_copy(AR[:, L2o:L2o + 512], c.psE[:, 0:512]))
    if SCSTEP < 5:
        return
    r.wait("t", "sv", k4_done)
    r.wait("t", "sv", rp_done)
    for h in range(4):
        em("t", lambda e, h=h: e.matmul(c.psE[:, h * 64:(h + 1) * 64],
                                        AR[:, L2o + h * 128: L2o + (h + 1) * 128],
                                        AR[:, RPo + h * 64: RPo + (h + 1) * 64],
                                        start=True, stop=True))
    st1 = r.mark("t", "st")
    r.wait("v", "st", st1)
    t1_done = ev("v", "sv", lambda e: e.tensor_add(AR[:, T1o:T1o + 256], AR[:, RPo:RPo + 256],
                                                   c.psE[:, 0:256]))
    r.wait("t", "sv", t1_done)
    for h in range(4):
        em("t", lambda e, h=h: e.matmul(c.psE[:, h * 64:(h + 1) * 64],
                                        AR[:, LP + h * 128: LP + (h + 1) * 128],
                                        AR[:, T1o + h * 64: T1o + (h + 1) * 64],
                                        start=True, stop=True))
    st2 = r.mark("t", "st")
    r.wait("v", "st", st2)
    t2_done = ev("v", "sv", lambda e: e.tensor_add(AR[:, T2o:T2o + 256], AR[:, T1o:T1o + 256],
                                                   c.psE[:, 0:256]))
    r.wait("t", "sv", t2_done)
    for h in range(4):
        em("t", lambda e, h=h: e.matmul(c.psE[:, h * 64:(h + 1) * 64],
                                        AR[:, K1o + h * 128: K1o + (h + 1) * 128],
                                        AR[:, T2o + h * 64: T2o + (h + 1) * 64],
                                        start=True, stop=True))
    st3 = r.mark("t", "st")
    r.wait("v", "st", st3)
    v_done = ev("v", "sv", lambda e: e.tensor_sub(AR[:, VVo:VVo + 256], AR[:, T2o:T2o + 256],
                                                  c.psE[:, 0:256]))
    if SCSTEP < 6:
        return
    r.wait("t", "sv", v_done)
    r.wait("t", "sv", gup_done)
    for h in range(4):
        em("t", lambda e, h=h: e.matmul(c.psE[:, h * 64:(h + 1) * 64],
                                        AR[:, GUP + h * 128: GUP + (h + 1) * 128],
                                        AR[:, VVo + h * 64: VVo + (h + 1) * 64],
                                        start=True, stop=True))
    for h in range(4):
        em("t", lambda e, h=h: e.matmul(c.psB[:, h * 64:(h + 1) * 64], cpairsl(h // 2), wsb2col(h),
                                        start=True, stop=False))
        em("t", lambda e, h=h: e.matmul(c.psB[:, h * 64:(h + 1) * 64],
                                        AR[:, GBCN + h * 128: GBCN + (h + 1) * 128],
                                        AR[:, VVo + h * 64: VVo + (h + 1) * 64],
                                        start=False, stop=True))
    z1t = r.mark("t", "st")
    r.wait("v", "st", z1t)
    r.wait("v", "sv", rr_done)
    z1_done = ev("v", "sv", lambda e: e.tensor_add(AR[:, Z1o:Z1o + 256],
                                                   c.ubuf[:, 0, 0:256], c.psE[:, 0:256]))
    if SCSTEP < 7:
        return
    r.wait("t", "sv", z1_done)
    for h in range(4):
        em("t", lambda e, h=h: e.matmul(c.psC[:, h * 64:(h + 1) * 64],
                                        AR[:, DTT + h * 128: DTT + (h + 1) * 128],
                                        AR[:, Z1o + h * 64: Z1o + (h + 1) * 64],
                                        start=True, stop=True))
    psDbf = c.psD.ap().bitcast(BF16)
    for g in range(2):
        em("t", lambda e, g=g, psDbf=psDbf: e.transpose(
            psDbf[:, 512 + g * 128: 512 + (g + 1) * 128],
            AR[:, base + g * 512 + sc_ * 128: base + g * 512 + (sc_ + 1) * 128], c.idb[:, :]))
    dgt = r.mark("t", "st")
    r.wait("v", "st", dgt)
    if c.zprev_scan:
        r.wait("v", zsem1, c.zprev_scan)
    em("v", lambda e: e.tensor_tensor(
        AR[:, ZBo:ZBo + 256].rearrange("p (h t2) -> p h t2", t2=64),
        c.psC.ap().rearrange("p (h t2) -> p h t2", t2=128)[:, 0:4, 0:64],
        c.ubuf[:, 0, 260:264].rearrange("p f -> p f ()").broadcast_to([128, 4, 64]), ALU.mult))
    em("v", lambda e: e.drain())
    zb_done = ev("v", "sv", lambda e: e.tensor_add(AR[:, ZBo:ZBo + 256],
                                                   AR[:, ZBo:ZBo + 256], c.psB[:, 0:256]))
    btm_done = ev("v", "sv", lambda e, psDbf=psDbf: e.tensor_copy(AR[:, BTMo:BTMo + 256],
                                                                  psDbf[:, 512:768]))
    if SCSTEP < 8:
        return
    r.wait("t", "sv", btm_done)
    for h in range(4):
        em("t", lambda e, h=h: e.matmul(
            c.psE[(h % 2) * 64:(h % 2) * 64 + 64, 256 + (h // 2) * 64: 256 + (h // 2) * 64 + 64],
            AR[:, BTMo + (h // 2) * 128 + (h % 2) * 64: BTMo + (h // 2) * 128 + (h % 2) * 64 + 64],
            AR[:, VVo + h * 64: VVo + (h + 1) * 64],
            start=True, stop=True))
    wut = r.mark("t", "st")
    r.wait("v", "st", wut)
    em("v", lambda e: e.tensor_sub(c.wst, c.wst, c.psE[:, 256:384]))
    em("v", lambda e: e.drain())
    for h in range(3):
        em("v", lambda e, h=h: e.tensor_copy(
            c.wsb2[(h % 2) * 64:(h % 2) * 64 + 64,
                   (h // 2) * 128 + (h % 2) * 64:(h // 2) * 128 + (h % 2) * 64 + 64],
            c.wst[(h % 2) * 64:(h % 2) * 64 + 64, (h // 2) * 64:(h // 2) * 64 + 64]))
    wb_done = ev("v", "sv", lambda e: e.tensor_copy(
        c.wsb2[64:128, 192:256], c.wst[64:128, 64:128]))
    r.wait("t", "sv", wb_done)
    if SCSTEP < 9:
        return
    r.wait("s", "sv", zb_done)
    dma("s", zsem1, c.zloc1.ap()[sci * 128:(sci + 1) * 128, :], AR[:, ZBo:ZBo + 256])
    c.zprev_scan = r.val(zsem1)


def emit_phase_ffn(c):
    r, em, ev, dma = c.r, c.em, c.ev, c.dma
    AR = c.arena
    AR8 = c.arena.ap().bitcast(FP8)
    ln4_done = emit_ln(c, 1, 3, 4, c.hbuf, after=[("sv", c.wo_done)], f32_out=c.x4f)

    r.wait("v", "sv", c.ln4_last[1])
    em("v", lambda e: e.drain())
    em("v", lambda e: e.reduce_max(c.smx[:, 0:1], c.smx[:, 32:64], axis=AXX))
    xm = ev("v", "sv", lambda e: e.drain())
    r.wait("t", "sv", xm)
    r.wait("t", "sc", r.val("sc"))
    tok = ev("t", "st", lambda e: e.transpose(c.psE[0:1, 0:128], c.smx[:, 0:1], c.idf[:, :]))
    r.wait("v", "st", tok)
    vv = ev("v", "sv", lambda e: e.reduce_max(c.smx[0:1, 1:2], c.psE[0:1, 0:128], axis=AXX))
    r.wait("g", "sv", vv)
    r.wait("g", "sg", r.val("sg"))
    dma("g", "sg", c.agi_xmx[:, :], c.smx[0:1, 1:2])
    r.wait("g", "sg", r.val("sg"))
    cc_x = c.cc(lambda e: e.collective_compute(
        "AllReduce", ALU.max, replica_groups=c.RG,
        ins=[c.agi_xmx.ap().opt()], outs=[c.ago_xmx.ap().opt()]))
    c.waitcc("g", cc_x)
    r.wait("g", "sg", r.val("sg"))
    dma("g", "sg", c.smx[0:1, 2:3], c.ago_xmx[:, :])
    r.wait("v", "sg", r.val("sg"))
    em("v", lambda e: e.tensor_scalar(c.smx[0:1, 2:3], c.smx[0:1, 2:3], 1.0 / 448.0, 1e-6, ALU.mult, ALU.max))
    em("v", lambda e: e.drain())
    em("v", lambda e: e.reciprocal(c.smx[0:1, 3:4], c.smx[0:1, 2:3]))
    em("v", lambda e: e.drain())
    em("v", lambda e: e.tensor_scalar_mul(c.smx[0:1, 3:4], c.smx[0:1, 3:4], 0.5))
    vv = ev("v", "sv", lambda e: e.drain())
    r.wait("t", "sv", vv)
    tok = ev("t", "st", lambda e: e.transpose(c.psE[0:1, 20:22], c.smx2[0:2, 3:4], c.idf[0:2, 0:2]))
    r.wait("v", "st", tok)
    em("v", lambda e: e.tensor_copy(c.smx[0:1, 4:6], c.psE[0:1, 20:22]))
    em("v", lambda e: e.drain())
    em("v", lambda e: e.tensor_mul(c.smx[0:1, 6:7], c.smx[0:1, 2:3], c.smx[0:1, 4:5]))
    em("v", lambda e: e.drain())
    vv = ev("v", "sv", lambda e: e.tensor_scalar_mul(c.smx[0:1, 6:7], c.smx[0:1, 6:7], 4.0))
    r.wait("t", "sv", vv)
    em("t", lambda e: e.matmul(c.psE[:, 24:25], c.onerf[:, :], c.smx[0:1, 3:4], start=True, stop=True))
    tok = ev("t", "st", lambda e: e.matmul(c.psE[:, 25:26], c.onerf[:, :], c.smx[0:1, 6:7], start=True, stop=True))
    r.wait("v", "st", tok)
    vv = ev("v", "sv", lambda e: e.tensor_copy(c.smx2[:, 8:10], c.psE[:, 24:26]))
    em("v", lambda e: e.drain())
    x4sem = [r.fresh("x4"), r.fresh("x4")]
    if c.ubuf_guard:
        for _tk in c.ubuf_guard:
            r.wait("y", _tk[0], _tk[1])
    for _tk in c.ln4_last[0]:
        r.wait("y", _tk[0], _tk[1])
    r.wait("y", "st", r.val("st"))
    x4toks = {}
    for kk2 in range(2 * DK):
        k, half = kk2 // 2, kk2 % 2
        bpar = kk2 % 2
        if kk2 >= 2:
            r.wait("y", "sv", x4toks[kk2 - 2])
        dma("y", x4sem[bpar], c.ubuf[:, bpar, :], c.x4f.ap()[ts(k), half * 512:(half + 1) * 512])
        r.wait("v", x4sem[bpar], r.val(x4sem[bpar]))
        x4toks[kk2] = ev("v", "sv", lambda e, k=k, half=half, bpar=bpar: e.tensor_scalar(
            AR8[:, k * 1024 + half * 512: k * 1024 + (half + 1) * 512], c.ubuf[:, bpar, :],
            c.smx2[:, 8:9], None, ALU.mult))
    x8_done = ev("v", "sv", lambda e: e.drain())

    c.waitcc("y", c.cc_fc8)
    r.wait("y", "st", r.val("st"))
    if c.ubuf_guard:
        for _tk in c.ubuf_guard:
            r.wait("s", _tk[0], _tk[1])
    fsem = [r.fresh("fc", ), r.fresh("fc")]
    hsem = r.fresh("hb")
    r.wait("t", "sv", x8_done)
    X8 = AR8[:, 0:16384].rearrange("p (k t2) -> p k t2", t2=1024)
    slab_mm = {}
    strip_i = 0
    hb_toks = {}
    for hs in range(16):
        bpar = hs % 2
        if hs >= 2:
            r.wait("y", "st", slab_mm[hs - 2])
        dma("y", fsem[bpar], AR8[:, 16384 + bpar * 8192: 16384 + (bpar + 1) * 8192].rearrange(
            "p (k m) -> p k m", k=DK),
            c.ago_fc8.ap().rearrange("(k p) m -> p k m", p=128)[:, :, hs * 512:(hs + 1) * 512])
        r.wait("t", fsem[bpar], r.val(fsem[bpar]))
        WSL = AR8[:, 16384 + bpar * 8192: 16384 + (bpar + 1) * 8192].rearrange(
            "p (k m) -> p k m", k=DK)
        for mh in range(4):
            hchunk = hs * 4 + mh
            for n in range(2):
                ppar = (hchunk * 2 + n) % 2
                psx = c.psA[:, ppar * 1024:ppar * 1024 + 512]
                if hchunk * 2 + n >= 2:
                    r.wait("t", "sc", ev_sc[hchunk * 2 + n - 2])
                for k in range(DK):
                    em("t", lambda e, k=k, mh=mh, n=n, psx=psx, WSL=WSL: e.matmul(
                        psx, WSL[:, k, mh * 128:(mh + 1) * 128],
                        X8[:, k, n * 512:(n + 1) * 512], start=(k == 0), stop=(k == DK - 1)))
                mk = r.mark("t", "st")
                r.wait("s", "st", mk)
                if hchunk * 2 + n == 0:
                    ev_sc = {}
                    sq_tok = {}
                if hchunk * 2 + n >= 2:
                    r.wait("s", "sv", sq_tok[hchunk * 2 + n - 2])
                if n == 0 and hchunk >= 1:
                    r.wait("v", hsem, hb_toks[hchunk - 1])
                ev_sc[hchunk * 2 + n] = ev("s", "sc", lambda e, ppar=ppar, psx=psx: e.activation(
                    c.ubuf[:, ppar, :], psx, ACT.Relu, scale=c.smx2[:, 9:10]))
                r.wait("v", "sc", ev_sc[hchunk * 2 + n])
                sq_tok[hchunk * 2 + n] = ev("v", "sv", lambda e, ppar=ppar, n=n: e.tensor_mul(
                    c.lnt[:, n, :], c.ubuf[:, ppar, :], c.ubuf[:, ppar, :]))
            em("v", lambda e: e.drain())
            em("v", lambda e, strip_i=strip_i: e.reduce_max(
                c.smx[:, 32 + strip_i % 32:33 + strip_i % 32],
                c.lnt[:, 0:2, :].rearrange("p a b -> p (a b)"), axis=AXX,
                apply_absolute_value=True))
            if strip_i % 32 == 31:
                em("v", lambda e: e.drain())
                em("v", lambda e, strip_i=strip_i: e.tensor_scalar_max(
                    c.smx[:, 7 + strip_i // 32:8 + strip_i // 32] if False else c.smx[:, 7:8],
                    c.smx[:, 7:8] if strip_i >= 32 else c.smx[:, 32:33], 0.0) if False else e.reduce_max(
                    c.smx[:, 7:8] if strip_i // 32 == 0 else c.smx[:, 8:9],
                    c.smx[:, 32:64], axis=AXX))
            strip_i += 1
            hvv = ev("v", "sv", lambda e: e.drain())
            r.wait("y", "sv", hvv)
            if hchunk >= 2:
                r.wait("y", hsem, hb_toks[hchunk - 2])
            hb_toks[hchunk] = dma("y", hsem, c.hbounce.ap()[hchunk * 128:(hchunk + 1) * 128, :],
                                  c.lnt[:, 0:2, :].rearrange("p a b -> p (a b)"))
        slab_mm[hs] = r.val("st")
    em("v", lambda e: e.drain())
    em("v", lambda e: e.tensor_max(c.smx[:, 7:8], c.smx[:, 7:8], c.smx[:, 8:9]))
    hm = ev("v", "sv", lambda e: e.drain())
    r.wait("t", "sv", hm)
    r.wait("t", "sc", r.val("sc"))
    tok = ev("t", "st", lambda e: e.transpose(c.psE[0:1, 0:128], c.smx[:, 7:8], c.idf[:, :]))
    r.wait("v", "st", tok)
    vv = ev("v", "sv", lambda e: e.reduce_max(c.smx[0:1, 10:11], c.psE[0:1, 0:128], axis=AXX))
    r.wait("g", "sv", vv)
    r.wait("g", "sg", r.val("sg"))
    dma("g", "sg", c.agi_hmx[:, :], c.smx[0:1, 10:11])
    r.wait("g", "sg", r.val("sg"))
    cc_h = c.cc(lambda e: e.collective_compute(
        "AllReduce", ALU.max, replica_groups=c.RG,
        ins=[c.agi_hmx.ap().opt()], outs=[c.ago_hmx.ap().opt()]))
    c.waitcc("g", cc_h)
    r.wait("g", "sg", r.val("sg"))
    dma("g", "sg", c.smx[0:1, 11:12], c.ago_hmx[:, :])
    r.wait("v", "sg", r.val("sg"))
    em("v", lambda e: e.tensor_scalar(c.smx[0:1, 11:12], c.smx[0:1, 11:12], 1.0 / 448.0, 1e-6, ALU.mult, ALU.max))
    em("v", lambda e: e.drain())
    em("v", lambda e: e.reciprocal(c.smx[0:1, 12:13], c.smx[0:1, 11:12]))
    em("v", lambda e: e.drain())
    em("v", lambda e: e.tensor_scalar_mul(c.smx[0:1, 12:13], c.smx[0:1, 12:13], 0.5))
    em("v", lambda e: e.drain())
    em("v", lambda e: e.tensor_mul(c.smx[0:1, 13:14], c.smx[0:1, 11:12], c.smx[0:1, 5:6]))
    em("v", lambda e: e.drain())
    vv = ev("v", "sv", lambda e: e.tensor_scalar_mul(c.smx[0:1, 13:14], c.smx[0:1, 13:14], 4.0))
    r.wait("t", "sv", vv)
    em("t", lambda e: e.matmul(c.psE[:, 24:25], c.onerf[:, :], c.smx[0:1, 12:13], start=True, stop=True))
    tok = ev("t", "st", lambda e: e.matmul(c.psE[:, 25:26], c.onerf[:, :], c.smx[0:1, 13:14], start=True, stop=True))
    r.wait("v", "st", tok)
    vv = ev("v", "sv", lambda e: e.tensor_copy(c.smx2[:, 10:12], c.psE[:, 24:26]))
    em("v", lambda e: e.drain())
    gb1 = c.gsl(1, 5)
    em("v", lambda e, gb1=gb1: e.tensor_tensor(
        c.smx[:, 16:32], gb1, c.smx2[:, 11:12].broadcast_to([128, 16]), ALU.mult))
    gbdq = ev("v", "sv", lambda e: e.drain())

    q8sem = [r.fresh("q8"), r.fresh("q8")]
    r.wait("y", hsem, r.val(hsem))
    qs_toks = {}
    for k in range(64):
        bpar = k % 2
        if k >= 2:
            r.wait("y", "sv", qs_toks[k - 2][0])
        r.wait("y", hsem, hb_toks[k] if k in hb_toks else r.val(hsem))
        dma("y", q8sem[bpar], c.lnt[:, bpar, :].bitcast(F32)[:, 0:256] if False else c.lnq[:, bpar, :],
            c.hbounce.ap()[k * 128:(k + 1) * 128, :].rearrange("p (a b) -> p a b", a=2)[:, 0, :] if False
            else c.hbounce.ap()[k * 128:(k + 1) * 128, 0:512])
        dma("y", q8sem[bpar], c.lnt[:, bpar, :], c.hbounce.ap()[k * 128:(k + 1) * 128, 512:1024])
        r.wait("v", q8sem[bpar], r.val(q8sem[bpar]))
        vq = ev("v", "sv", lambda e, bpar=bpar, k=k: e.tensor_scalar(
            AR8[:, 49152 + bpar * 1024: 49152 + bpar * 1024 + 512], c.lnq[:, bpar, :],
            c.smx2[:, 10:11], None, ALU.mult))
        vq2 = ev("v", "sv", lambda e, bpar=bpar, k=k: e.tensor_scalar(
            AR8[:, 49152 + bpar * 1024 + 512: 49152 + (bpar + 1) * 1024], c.lnt[:, bpar, :],
            c.smx2[:, 10:11], None, ALU.mult))
        r.wait("y", "sv", vq2)
        t1 = dma("y", q8sem[bpar], c.h8b.ap()[k * 128:(k + 1) * 128, :],
                 AR8[:, 49152 + bpar * 1024: 49152 + (bpar + 1) * 1024])
        qs_toks[k] = (r.val("sv"), t1)
    c.waitcc("y", c.cc_pj8)
    r.wait("y", "sv", r.val("sv"))
    r.wait("y", q8sem[0], r.val(q8sem[0]))
    r.wait("y", q8sem[1], r.val(q8sem[1]))
    psem = [r.fresh("pj"), r.fresh("pj")]
    h8sem = [r.fresh("h8"), r.fresh("h8")]
    r.wait("v", "sv", gbdq)
    PJ = [AR8[:, 16384 + i * 16384: 16384 + (i + 1) * 16384].rearrange("p (k m) -> p k m", m=256)
          for i in range(2)]
    H8 = [AR8[:, 0 + i * 1024: 0 + (i + 1) * 1024] for i in range(2)]
    pj_mm = {}
    h8_mm = {}
    for dg in range(4):
        for half in range(2):
            sidx = dg * 2 + half
            spar = sidx % 2
            if sidx >= 2:
                r.wait("y", "st", pj_mm[sidx - 2])
            dma("y", psem[spar], PJ[spar],
                c.ago_pj8.ap().rearrange("(k p) m -> p k m", p=128)[:, :, dg * 512 + half * 256: dg * 512 + (half + 1) * 256])
            r.wait("t", psem[spar], r.val(psem[spar]))
            if sidx >= 1:
                r.wait("t", "sv", ev_last)
            mmtok = {}
            for k in range(64):
                kb = k % 2
                if k >= 2:
                    r.wait("y", "st", mmtok[k - 2])
                elif sidx >= 1:
                    r.wait("y", "st", pj_mm[sidx - 1])
                dma("y", h8sem[kb], H8[kb], c.h8b.ap()[k * 128:(k + 1) * 128, :])
                r.wait("t", h8sem[kb], r.val(h8sem[kb]))
                for dm in range(2):
                    for n in range(2):
                        em("t", lambda e, k=k, dm=dm, n=n, spar=spar, kb=kb: e.matmul(
                            c.psA[:, (dm * 2 + n) * 512:(dm * 2 + n + 1) * 512],
                            PJ[spar][:, k, dm * 128:(dm + 1) * 128],
                            H8[kb][:, n * 512:(n + 1) * 512], start=(k == 0), stop=(k == 63)))
                mmtok[k] = r.mark("t", "st")
            r.wait("v", "st", mmtok[63])
            for dm in range(2):
                for n in range(2):
                    dchunk = dg * 4 + half * 2 + dm
                    ev_last = ev("v", "sv", lambda e, dchunk=dchunk, n=n, dm=dm: e.scalar_tensor_tensor(
                        c.xres[:, dchunk * TC + n * 512: dchunk * TC + (n + 1) * 512],
                        c.psA[:, (dm * 2 + n) * 512:(dm * 2 + n + 1) * 512],
                        c.smx[:, 16 + dchunk:17 + dchunk],
                        c.xres[:, dchunk * TC + n * 512: dchunk * TC + (n + 1) * 512],
                        ALU.mult, ALU.add))
            pj_mm[sidx] = r.val("st")
    c.ffn_done = r.val("sv")
